# version 19
# baseline (speedup 1.0000x reference)
import numpy as np
import concourse.bass as bass
import concourse.mybir as mybir
from concourse.bass_utils import run_bass_kernel_spmd

F32 = mybir.dt.float32
BF16 = mybir.dt.bfloat16
AX = mybir.AxisListType.X
ALU = mybir.AluOpType
ACT = mybir.ActivationFunctionType

DM = 1024
DS = 128
NTOK = 2048
W = 128
LCH = 32
NCH = NTOK // LCH
NSTEP = W + LCH
TW = W + NTOK
NDC = DM // 128
NTC = TW // 128
WC = W // 128
NMC = NTOK // 128
NEXP = 8
NFC = 8
NST = 4
STW = 512
EPS = 1e-5
ALPHA = 1.702
LIMIT = 7.0
SSTR = LCH * (NCH - 1) + 1


def build_nc(debug=False, upto="full", ntc_run=None, skip_big=False):
    import os as _os
    LNL = int(_os.environ.get("MK_LN_LEVEL", "9"))
    PH = {"ln0": -1, "ln": 0, "usp": 1, "chain": 2, "attn": 3, "ffn": 4, "full": 5}[upto]
    nc = bass.Bass("TRN2", target_bir_lowering=False)

    d_xm = nc.dram_tensor("xm", [NTOK, DM], F32, kind="ExternalInput").ap()
    d_xw = nc.dram_tensor("xw", [W, DM], F32, kind="ExternalInput").ap()
    d_dmask = nc.dram_tensor("dmask", [128, W], F32, kind="ExternalInput").ap()
    d_eye = nc.dram_tensor("eye", [128, 128], F32, kind="ExternalInput").ap()
    d_AT = nc.dram_tensor("AT", [128, 128], F32, kind="ExternalInput").ap()
    d_BT = nc.dram_tensor("BT", [128, 128], F32, kind="ExternalInput").ap()
    d_CT = nc.dram_tensor("CT", [128, 128], F32, kind="ExternalInput").ap()
    d_WiT = nc.dram_tensor("WiT", [128, NDC, 128], F32, kind="ExternalInput").ap()
    d_WsT = nc.dram_tensor("WsT", [128, NDC, 128], F32, kind="ExternalInput").ap()
    d_WoT = nc.dram_tensor("WoT", [128, NDC, 128], F32, kind="ExternalInput").ap()
    d_Wr = nc.dram_tensor("Wr", [128, NDC, 8], F32, kind="ExternalInput").ap()
    d_bi = nc.dram_tensor("bi", [128, 1], F32, kind="ExternalInput").ap()
    d_bs = nc.dram_tensor("bs", [128, 1], F32, kind="ExternalInput").ap()
    d_bo = nc.dram_tensor("bo", [128, 8], F32, kind="ExternalInput").ap()
    d_bg = nc.dram_tensor("bg", [128, NEXP * NFC], F32, kind="ExternalInput").ap()
    d_bu = nc.dram_tensor("bu", [128, NEXP * NFC], F32, kind="ExternalInput").ap()
    d_dbr = nc.dram_tensor("dbr", [NEXP, DM], F32, kind="ExternalInput").ap()
    d_on1 = nc.dram_tensor("on1", [1, 128], F32, kind="ExternalInput").ap()
    d_sel = nc.dram_tensor("sel", [NEXP, NEXP * 128], F32, kind="ExternalInput").ap()
    if not skip_big:
        d_Wg = nc.dram_tensor("Wg", [NEXP, NFC, 128, NDC, 128], F32, kind="ExternalInput").ap()
        d_Wu = nc.dram_tensor("Wu", [NEXP, NFC, 128, NDC, 128], F32, kind="ExternalInput").ap()
        d_Wd = nc.dram_tensor("Wd", [NEXP, NFC, 128, NDC, 128], F32, kind="ExternalInput").ap()
    d_y = nc.dram_tensor("y", [NTOK, DM], F32, kind="ExternalOutput").ap()
    dbg = {}
    if debug:
        dbg["xnfm"] = nc.dram_tensor("dbg_xnfm", [128, NTC * DM], F32, kind="ExternalOutput").ap()
        dbg["D"] = nc.dram_tensor("dbg_D", [128, TW], F32, kind="ExternalOutput").ap()
        dbg["G"] = nc.dram_tensor("dbg_G", [128, NTOK], F32, kind="ExternalOutput").ap()
        dbg["wfm"] = nc.dram_tensor("dbg_wfm", [8, NTOK], F32, kind="ExternalOutput").ap()
        dbg["ys"] = nc.dram_tensor("dbg_ys", [128, NTOK], F32, kind="ExternalOutput").ap()
        dbg["acc"] = nc.dram_tensor("dbg_acc", [128, NDC * NTOK], F32, kind="ExternalOutput").ap()
        dbg["u"] = nc.dram_tensor("dbg_u", [128, TW], F32, kind="ExternalOutput").ap()
        dbg["sp"] = nc.dram_tensor("dbg_sp", [128, TW], F32, kind="ExternalOutput").ap()

    ops = {k: [] for k in ("te", "ve", "sc", "gp", "sy")}
    cnt = {}

    def _inc(sem, n=1):
        cnt[sem] = cnt.get(sem, 0) + n
        return cnt[sem]

    from contextlib import ExitStack
    with ExitStack() as _es:
        xn_fm = _es.enter_context(nc.sbuf_tensor("xn_fm", [128, NTC, NDC, 128], BF16))
        SPb = _es.enter_context(nc.sbuf_tensor("SPb", [128, 1 + TW], BF16))
        u_bf = _es.enter_context(nc.sbuf_tensor("u_bf", [128, TW], BF16))
        Dt = _es.enter_context(nc.sbuf_tensor("Dt", [128, TW], BF16))
        G = _es.enter_context(nc.sbuf_tensor("G", [128, NTOK], BF16))
        gw = _es.enter_context(nc.sbuf_tensor("gw", [128, 2 * NCH], BF16))
        gz = _es.enter_context(nc.sbuf_tensor("gz", [128, NCH], BF16))
        ys_bf = _es.enter_context(nc.sbuf_tensor("ys_bf", [128, NTOK], BF16))
        w_fmb = _es.enter_context(nc.sbuf_tensor("w_fmb", [8, NTOK], BF16))
        acc = _es.enter_context(nc.sbuf_tensor("acc", [128, NDC, NTOK], BF16))
        act_w = _es.enter_context(nc.sbuf_tensor("act_w", [128, NFC, NTOK], BF16))
        wbc = _es.enter_context(nc.sbuf_tensor("wbc", [128, 2, NTOK], BF16))
        sAT = _es.enter_context(nc.sbuf_tensor("sAT", [128, 128], BF16))
        sBT = _es.enter_context(nc.sbuf_tensor("sBT", [128, 128], BF16))
        sCT = _es.enter_context(nc.sbuf_tensor("sCT", [128, 128], BF16))
        sI16 = _es.enter_context(nc.sbuf_tensor("sI16", [128, 128], BF16))
        sI32 = _es.enter_context(nc.sbuf_tensor("sI32", [128, 128], F32))
        sWiT = _es.enter_context(nc.sbuf_tensor("sWiT", [128, NDC, 128], BF16))
        sWsT = _es.enter_context(nc.sbuf_tensor("sWsT", [128, NDC, 128], BF16))
        sWoT = _es.enter_context(nc.sbuf_tensor("sWoT", [128, NDC, 128], BF16))
        sWr = _es.enter_context(nc.sbuf_tensor("sWr", [128, NDC, 8], BF16))
        sbi = _es.enter_context(nc.sbuf_tensor("sbi", [128, 1], F32))
        sbs = _es.enter_context(nc.sbuf_tensor("sbs", [128, 1], F32))
        sbo = _es.enter_context(nc.sbuf_tensor("sbo", [128, 8], F32))
        sbg = _es.enter_context(nc.sbuf_tensor("sbg", [128, NEXP * NFC], F32))
        sbu = _es.enter_context(nc.sbuf_tensor("sbu", [128, NEXP * NFC], F32))
        sdb = _es.enter_context(nc.sbuf_tensor("sdb", [NEXP, DM], BF16))
        son1 = _es.enter_context(nc.sbuf_tensor("son1", [1, 128], BF16))
        ssel = _es.enter_context(nc.sbuf_tensor("ssel", [NEXP, NEXP * 128], BF16))
        sdm = _es.enter_context(nc.sbuf_tensor("sdm", [128, W], F32))
        xln = _es.enter_context(nc.sbuf_tensor("xln", [128, 3, DM], F32))
        xnb = _es.enter_context(nc.sbuf_tensor("xnb", [128, 2, DM], BF16))
        stat = _es.enter_context(nc.sbuf_tensor("stat", [128, 16], F32))
        lgb = _es.enter_context(nc.sbuf_tensor("lgb", [128, 2, 8], F32))
        rt = _es.enter_context(nc.sbuf_tensor("rt", [128, 2, 16], F32))
        wtm = _es.enter_context(nc.sbuf_tensor("wtm", [128, 2, 8], F32))
        wgu = _es.enter_context(nc.sbuf_tensor("wgu", [128, 4, NDC, 128], BF16))
        wdb_ = _es.enter_context(nc.sbuf_tensor("wdb_", [128, 2, NFC, 128], BF16))
        et2 = _es.enter_context(nc.sbuf_tensor("et2", [128, 2, STW], BF16))
        et2w = _es.enter_context(nc.sbuf_tensor("et2w", [128, 2, STW], BF16))
        etg = _es.enter_context(nc.sbuf_tensor("etg", [128, 2, STW], BF16))
        etu = _es.enter_context(nc.sbuf_tensor("etu", [128, 2, STW], BF16))
        xres = _es.enter_context(nc.sbuf_tensor("xres", [128, 4, DM], BF16))
        otm = _es.enter_context(nc.sbuf_tensor("otm", [128, 2, DM], F32))
        pb0 = _es.enter_context(nc.psum_tensor([128, 512], F32))
        pb1 = _es.enter_context(nc.psum_tensor([128, 512], F32))
        pb2 = _es.enter_context(nc.psum_tensor([128, 512], F32))
        pb3 = _es.enter_context(nc.psum_tensor([128, 512], F32))
        pb4 = _es.enter_context(nc.psum_tensor([128, 512], F32))
        pb5 = _es.enter_context(nc.psum_tensor([128, 512], F32))
        pbtA = _es.enter_context(nc.psum_tensor([128, 1024], BF16))
        pbtB = _es.enter_context(nc.psum_tensor([128, 1024], BF16))
        s_lc = _es.enter_context(nc.semaphore("s_lc"))
        s_lc2 = _es.enter_context(nc.semaphore("s_lc2"))
        s_lc3 = _es.enter_context(nc.semaphore("s_lc3"))
        s_lw0 = _es.enter_context(nc.semaphore("s_lw0"))
        s_lw1 = _es.enter_context(nc.semaphore("s_lw1"))
        s_ld0 = _es.enter_context(nc.semaphore("s_ld0"))
        s_ld1 = _es.enter_context(nc.semaphore("s_ld1"))
        s_lx0 = _es.enter_context(nc.semaphore("s_lx0"))
        s_lx1 = _es.enter_context(nc.semaphore("s_lx1"))
        s_lf0 = _es.enter_context(nc.semaphore("s_lf0"))
        s_lf1 = _es.enter_context(nc.semaphore("s_lf1"))
        s_te = _es.enter_context(nc.semaphore("s_te"))
        s_ve = _es.enter_context(nc.semaphore("s_ve"))
        s_sc = _es.enter_context(nc.semaphore("s_sc"))
        s_ou0 = _es.enter_context(nc.semaphore("s_ou0"))
        s_ou1 = _es.enter_context(nc.semaphore("s_ou1"))
        s_gp = _es.enter_context(nc.semaphore("s_gp"))
        block = _es.enter_context(nc.Block())
        if True:
            pb = [pb0, pb1, pb2, pb3, pb4, pb5]
            pbt = [pbtA, pbtB]
            sems = {"s_lc": s_lc, "s_lc2": s_lc2, "s_lc3": s_lc3, "s_lw0": s_lw0, "s_lw1": s_lw1,
                    "s_ld0": s_ld0, "s_ld1": s_ld1,
                    "s_lx0": s_lx0, "s_lx1": s_lx1, "s_lf0": s_lf0, "s_lf1": s_lf1, "s_te": s_te,
                    "s_ve": s_ve, "s_sc": s_sc, "s_ou0": s_ou0, "s_ou1": s_ou1,
                    "s_gp": s_gp}

            def mk(eng_key, sem_name, self_wait=False):
                def emitfn(fn, wait=(), inc=True):
                    for sn, val in wait:
                        assert val <= cnt.get(sn, 0), (sn, val, cnt.get(sn, 0), eng_key)
                    w = tuple(wait)
                    if self_wait and cnt.get(sem_name, 0) > 0:
                        w = ((sem_name, cnt[sem_name]),) + w
                    v = _inc(sem_name) if inc else None
                    def op(e, fn=fn, w=w, v=v, sem_name=sem_name):
                        for sn, val in w:
                            e.wait_ge(sems[sn], val)
                        ins = fn(e)
                        if v is not None:
                            ins.then_inc(sems[sem_name], 1)
                    ops[eng_key].append(op)
                    return cnt.get(sem_name, 0)
                return emitfn

            te = mk("te", "s_te")
            ve = mk("ve", "s_ve", self_wait=True)
            sc = mk("sc", "s_sc", self_wait=True)
            gpc = mk("gp", "s_gp", self_wait=True)

            class _Done(Exception):
                pass

            def dma(eng_key, sem_name, dst, src, wait=()):
                for sn, val in wait:
                    assert val <= cnt.get(sn, 0), (sn, val, cnt.get(sn, 0), "dma")
                w = tuple(wait)
                if cnt.get(sem_name, 0) > 0:
                    w = ((sem_name, cnt[sem_name]),) + w
                _inc(sem_name, 16)
                def op(e, dst=dst, src=src, w=w, sem_name=sem_name):
                    for sn, val in w:
                        e.wait_ge(sems[sn], val)
                    e.dma_start(dst, src).then_inc(sems[sem_name], 16)
                ops[eng_key].append(op)
                return cnt[sem_name]

            try:
                dma("gp", "s_lc2", sI16[:], d_eye)
                lc_eye = dma("gp", "s_lc2", sI32[:], d_eye)
                for j, (dst, csrc) in enumerate(
                        [(sAT[:], d_AT), (sBT[:], d_BT), (sCT[:], d_CT),
                         (sWiT[:], d_WiT), (sWsT[:], d_WsT), (sWoT[:], d_WoT),
                         (sWr[:], d_Wr), (sbi[:], d_bi), (sbs[:], d_bs),
                         (sbo[:], d_bo), (sbg[:], d_bg), (sbu[:], d_bu),
                         (sdb[:], d_dbr), (son1[:], d_on1), (ssel[:], d_sel),
                         (sdm[:], d_dmask)]):
                    dma("gp", "s_lc" if j % 2 == 0 else "s_lc3", dst, csrc)
                lc_wait = [("s_lc", cnt["s_lc"]), ("s_lc3", cnt["s_lc3"])]

                v_init = ve(lambda e: e.memset(gz[:], 0.0))
                v_init = ve(lambda e: e.memset(stat[:, 14:15], EPS))
                v_init = ve(lambda e: e.memset(SPb[:, 0:1], 0.0))

                x_srcs = [d_xw[t * 128:(t + 1) * 128, :] for t in range(WC)] + \
                         [d_xm[t * 128:(t + 1) * 128, :] for t in range(NMC)]
                ln_free = {}
                xnb_free = {}
                tr_evac = {}
                router_evac = {}
                logit_evac = {}
                wload = {}
                gu_cons = {}
                wd_cons = {}

                def sched_wdma(kind, ei, fc):
                    par = (ei * NFC + fc) % 2
                    if kind == "g":
                        dst, src, ru, sem = wgu[:, 2 * par + 0], d_Wg[ei, fc], gu_cons.get(par), "s_lw%d" % par
                    elif kind == "u":
                        dst, src, ru, sem = wgu[:, 2 * par + 1], d_Wu[ei, fc], gu_cons.get(par), "s_lw%d" % par
                    else:
                        dst, src, ru, sem = wdb_[:, par], d_Wd[ei, fc], wd_cons.get(par), "s_ld%d" % par
                    wl = [("s_te", ru)] if ru is not None else []
                    wload[(kind, ei, fc)] = (sem, dma("gp", sem, dst, src, wait=wl))

                for fc in range(2):
                    sched_wdma("g", 0, fc); sched_wdma("u", 0, fc)
                sched_wdma("d", 0, 0); sched_wdma("d", 0, 1)

                ew_done = {}
                ts_read = {}
                gtile = [0]

                def emit_gu_tile(ei, fc, stt, weighted, bc_ev_l=None):
                    for _ in gu_tile_gen(ei, fc, stt, weighted, bc_ev_l):
                        pass

                def gu_tile_gen(ei, fc, stt, weighted, bc_ev_l=None):
                    par = (ei * NFC + fc) % 2
                    lwsem, lwg = wload[("u", ei, fc)]
                    epar = ei % 2
                    tile = gtile[0]
                    gtile[0] += 1
                    tp = tile % 2
                    bg_ = pb[tp]
                    bu2 = pb[2 + tp]
                    gwl = [(lwsem, lwg), ("s_sc", st_ready_ve[stt])]
                    if tile < 2:
                        gwl.extend(lc_wait)
                    if tile >= 2:
                        gwl.append(("s_sc", ts_read[tp][0]))
                    mg = None
                    for dc in range(NDC):
                        rhs = xn_fm[:, WC + stt * 4:WC + stt * 4 + 4, dc, :]
                        mg = te(lambda e, dc=dc, bg_=bg_, par=par, rhs=rhs:
                                e.matmul(bg_[:, 0:STW], wgu[:, 2 * par, dc, :], rhs,
                                         start=(dc == 0), stop=(dc == NDC - 1)),
                                wait=(gwl if dc == 0 else ()), inc=(dc == NDC - 1))
                        if dc % 2 == 1:
                            yield
                    uwl2 = [("s_sc", ts_read[tp][1])] if tile >= 2 else []
                    mu2 = None
                    for dc in range(NDC):
                        rhs = xn_fm[:, WC + stt * 4:WC + stt * 4 + 4, dc, :]
                        mu2 = te(lambda e, dc=dc, bu2=bu2, par=par, rhs=rhs:
                                 e.matmul(bu2[:, 0:STW], wgu[:, 2 * par + 1, dc, :], rhs,
                                          start=(dc == 0), stop=(dc == NDC - 1)),
                                 wait=(uwl2 if dc == 0 else ()), inc=(dc == NDC - 1))
                        if dc % 2 == 1:
                            yield
                    if stt == NST - 1:
                        gu_cons[par] = cnt["s_te"]
                        nxt = ei * NFC + fc + 2
                        if nxt < NEXP * NFC:
                            ne, nf = divmod(nxt, NFC)
                            sched_wdma("g", ne, nf)
                            sched_wdma("u", ne, nf)
                    t2 = et2[:, tp]; t2w = et2w[:, tp]
                    tg = etg[:, tp]; tu = etu[:, tp]
                    t1 = tg; t3 = t2w; t4 = tu; t5_ = tu
                    gbias = sbg[:, ei * NFC + fc: ei * NFC + fc + 1]
                    ubias = sbu[:, ei * NFC + fc: ei * NFC + fc + 1]
                    cg = sc(lambda e, tg=tg, bg_=bg_: e.copy(tg, bg_[:, 0:STW]),
                            wait=[("s_te", mg)])
                    cu = sc(lambda e, tu=tu, bu2=bu2: e.copy(tu, bu2[:, 0:STW]),
                            wait=[("s_te", mu2)])
                    ts_read[tp] = (cg, cu)
                    vg = ve(lambda e, t1=t1, tg=tg, gbias=gbias:
                            e.tensor_scalar(t1, tg, gbias, LIMIT, ALU.add, ALU.min),
                            wait=[("s_sc", cg)])
                    ve(lambda e, t4=t4, tu=tu, ubias=ubias:
                       e.tensor_scalar(t4, tu, ubias, LIMIT, ALU.add, ALU.min),
                       wait=[("s_sc", cu)])
                    ve(lambda e, t5_=t5_, t4=t4:
                       e.tensor_scalar(t5_, t4, -LIMIT, 1.0, ALU.max, ALU.add))
                    sg = sc(lambda e, t2=t2, t1=t1:
                            e.activation(t2, t1, ACT.Sigmoid, scale=ALPHA),
                            wait=[("s_ve", vg)])
                    if weighted:
                        wbs = wbc[:, epar, stt * STW:(stt + 1) * STW]
                        ve(lambda e, t2w=t2w, t2=t2, wbs=wbs:
                           e.tensor_mul(t2w, t2, wbs),
                           wait=[("s_sc", sg), ("s_sc", bc_ev_l[stt])])
                        ve(lambda e, t3=t3, t1=t1, t2w=t2w: e.tensor_mul(t3, t1, t2w))
                    else:
                        ve(lambda e, t3=t3, t1=t1, t2=t2: e.tensor_mul(t3, t1, t2),
                           wait=[("s_sc", sg)])
                    g2 = ve(lambda e, fc=fc, stt=stt, t5_=t5_, t3=t3:
                            e.tensor_mul(act_w[:, fc, stt * STW:(stt + 1) * STW],
                                         t5_, t3))
                    ew_done[(ei, fc, stt)] = g2


                pump_q = [(0, fc, stt) for fc in range(NFC) for stt in range(NST)]
                pump_cur = [None]
                ready_st = [-1]
                st_ready_ve = {}

                def pump(n):
                    k = 0
                    while k < n:
                        if pump_cur[0] is None:
                            if not pump_q:
                                return
                            ei_, fc_, stt_ = pump_q[0]
                            if stt_ > ready_st[0]:
                                return
                            pump_q.pop(0)
                            pump_cur[0] = gu_tile_gen(ei_, fc_, stt_, weighted=False)
                        try:
                            next(pump_cur[0])
                            k += 1
                        except StopIteration:
                            pump_cur[0] = None

                def pump_all():
                    for q in range(NST):
                        if q not in st_ready_ve:
                            st_ready_ve[q] = cnt["s_sc"]
                    ready_st[0] = NST - 1
                    while pump_q or pump_cur[0] is not None:
                        pump(8)

                NTCR = ntc_run if ntc_run else NTC
                lnA = {}
                lnB = {}

                def emitLoad(tc):
                    par3 = tc % 3
                    wl = []
                    if tc >= 3:
                        pv, ps = ln_free[par3]
                        wl = [("s_ve", pv), ("s_sc", ps)]
                    lxsem = "s_lx%d" % (tc % 2)
                    lxv = dma("sy", lxsem, xln[:, par3], x_srcs[tc], wait=wl)
                    lnA[tc] = (lxsem, lxv)

                def emitA(tc):
                    par = tc % 2
                    par3 = tc % 3
                    lxsem, lxv = lnA[tc]
                    xl = xln[:, par3]
                    ssum = stat[:, par * 8 + 0: par * 8 + 1]
                    negm = stat[:, par * 8 + 1: par * 8 + 2]
                    bb = stat[:, par * 8 + 5: par * 8 + 6]
                    var = stat[:, par * 8 + 2: par * 8 + 3]
                    ve(lambda e, xl=xl, ssum=ssum: e.reduce_sum(ssum, xl, AX),
                       wait=[(lxsem, lxv)])
                    ve(lambda e, ssum=ssum, negm=negm:
                       e.tensor_scalar_mul(negm, ssum, -1.0 / DM))
                    bwl2 = [("s_sc", bb_free[par])] if tc >= 2 else []
                    vb = ve(lambda e, negm=negm, bb=bb:
                            e.scalar_tensor_tensor(bb, negm, -1.0, negm,
                                                   ALU.mult, ALU.mult),
                            wait=bwl2)
                    vb = ve(lambda e, bb=bb: e.tensor_scalar_add(bb, bb, EPS))
                    s2 = sc(lambda e, xl=xl, var=var:
                            e.activation(otm[:, 0], xl, ACT.Square, accum_out=var),
                            wait=[(lxsem, lxv)])
                    lnB[tc] = (vb, s2)

                def emitB(tc):
                    par = tc % 2
                    par3 = tc % 3
                    xl = xln[:, par3]
                    xn = xnb[:, par]
                    negm = stat[:, par * 8 + 1: par * 8 + 2]
                    var = stat[:, par * 8 + 2: par * 8 + 3]
                    sq_ = stat[:, par * 8 + 3: par * 8 + 4]
                    inv = stat[:, par * 8 + 4: par * 8 + 5]
                    bb = stat[:, par * 8 + 5: par * 8 + 6]
                    vb, s2 = lnB[tc]
                    s3 = sc(lambda e, var=var, sq_=sq_, bb=bb:
                            e.activation(sq_, var, ACT.Sqrt, bias=bb,
                                         scale=1.0 / DM),
                            wait=[("s_ve", vb)])
                    bb_free[par] = s3
                    vwl = [("s_sc", s3)]
                    if tc >= 2:
                        vwl.append(("s_sc", inv_free[par]))
                    v3 = ve(lambda e, inv=inv, sq_=sq_: e.reciprocal(inv, sq_),
                            wait=vwl)
                    xwl = [("s_ve", v3)]
                    if tc >= 2:
                        xwl.append(("s_te", xnb_free[par]))
                    v4 = ve(lambda e, xn=xn, xl=xl, negm=negm, inv=inv:
                            e.tensor_scalar(xn, xl, negm, inv, ALU.add, ALU.mult),
                            wait=xwl)
                    inv_free[par] = cnt["s_sc"]
                    ln_free[par3] = (v4, s2)
                    lnB[tc] = v4

                def emitC(tc):
                    par = tc % 2
                    xn = xnb[:, par]
                    v4 = lnB[tc]
                    bank = pbt[par]
                    tw = [("s_ve", v4)]
                    if tc < 2:
                        tw.append(("s_lc2", lc_eye))
                    else:
                        tw.append(("s_sc", tr_evac[par]))
                    tvals = []
                    for dc in range(NDC):
                        t_ = te(lambda e, bank=bank, dc=dc, xn=xn:
                                e.transpose(bank[:, dc * 128:(dc + 1) * 128],
                                            xn[:, dc * 128:(dc + 1) * 128], sI16[:]),
                                wait=(tw if dc == 0 else ()), inc=(dc in (3, 7)))
                        tvals.append(t_)
                    xnb_free[par] = tvals[7]
                    sc(lambda e, bank=bank, tc=tc:
                       e.copy(xn_fm[:, tc, :, :], bank[:]),
                       wait=[("s_te", tvals[7])])
                    tr_evac[par] = cnt["s_sc"]

                inv_free = {}
                bb_free = {}
                for col in range(NTCR + 2):
                    if col < NTCR:
                        emitLoad(col)
                        emitA(col)
                    if 1 <= col <= NTCR:
                        emitB(col - 1)
                    if col >= 2:
                        emitC(col - 2)
                        pass
                ve_xn_all = cnt["s_ve"]

                if PH < 0: raise _Done()
                ve_xn_all0 = cnt["s_ve"]
                sc_xn_all0 = cnt["s_sc"]
                r_mm = {}
                r_s6 = {}
                r_t6 = {}
                r_v7 = {}

                def emitR1(mc):
                    rpar = mc % 2
                    tc = mc + WC
                    lbank = pb[4]
                    lg = lgb[:, rpar]
                    mmv = None
                    for dc in range(NDC):
                        lwl = []
                        if dc == 0:
                            if mc == 0:
                                lwl.append(("s_ve", ve_xn_all0))
                                lwl.append(("s_sc", sc_xn_all0))
                                lwl.extend(lc_wait)
                            else:
                                lwl.append(("s_sc", logit_evac[(mc - 1) % 2]))
                        mmv = te(lambda e, dc=dc, lbank=lbank, tc=tc:
                                 e.matmul(lbank[:, 0:8], xn_fm[:, tc, dc, :],
                                          sWr[:, dc, :],
                                          start=(dc == 0), stop=(dc == NDC - 1)),
                                 wait=lwl, inc=(dc == NDC - 1))
                    ewl = [("s_te", mmv)]
                    if mc >= 2:
                        ewl.append(("s_ve", r_v7[rpar]))
                    logit_evac[rpar] = sc(lambda e, lg=lg, lbank=lbank:
                                          e.copy(lg, lbank[:, 0:8]), wait=ewl)
                    r_mm[mc] = logit_evac[rpar]

                def emitR2(mc):
                    rpar = mc % 2
                    lg = lgb[:, rpar]
                    m1 = rt[:, rpar, 0:1]; nm1 = rt[:, rpar, 1:2]
                    m2 = rt[:, rpar, 2:3]; ssum2 = rt[:, rpar, 3:4]
                    sinv = rt[:, rpar, 4:5]; eq = rt[:, rpar, 5:13]
                    vwl = [("s_sc", r_mm[mc])]
                    if mc >= 2:
                        vwl.append(("s_sc", r_s6[rpar]))
                    v4 = ve(lambda e, m1=m1, lg=lg: e.reduce_max(m1, lg, AX),
                            wait=vwl)
                    ve(lambda e, nm1=nm1, m1=m1: e.tensor_scalar_mul(nm1, m1, -1.0),
                       wait=[("s_ve", v4)])
                    ve(lambda e, eq=eq, lg=lg, m1=m1:
                       e.tensor_scalar(eq, lg, m1, None, ALU.is_equal))
                    ve(lambda e, eq=eq, lg=lg:
                       e.scalar_tensor_tensor(eq, eq, -1e30, lg, ALU.mult, ALU.add))
                    vm2 = ve(lambda e, m2=m2, eq=eq: e.reduce_max(m2, eq, AX))
                    v6 = ve(lambda e, eq=eq, lg=lg, m2=m2:
                            e.tensor_scalar(eq, lg, m2, None, ALU.is_ge),
                            wait=[("s_ve", vm2)])
                    s5 = sc(lambda e, lg=lg, nm1=nm1:
                            e.activation(lg, lg, ACT.Exp, bias=nm1),
                            wait=[("s_ve", v6)])
                    v7 = ve(lambda e, lg=lg, eq=eq: e.tensor_mul(lg, eq, lg),
                            wait=[("s_sc", s5)])
                    v8 = ve(lambda e, ssum2=ssum2, lg=lg: e.reduce_sum(ssum2, lg, AX))
                    v9 = ve(lambda e, sinv=sinv, ssum2=ssum2: e.reciprocal(sinv, ssum2),
                            wait=[("s_ve", v8)])
                    wt = wtm[:, rpar]
                    swl = [("s_ve", v9)]
                    if mc >= 2:
                        swl.append(("s_te", r_t6[rpar]))
                    r_s6[rpar] = sc(lambda e, wt=wt, lg=lg, sinv=sinv:
                                    e.activation(wt, lg, ACT.Identity, scale=sinv),
                                    wait=swl)
                    r_v7[rpar] = v9

                def emitR3(mc):
                    rpar = mc % 2
                    wt = wtm[:, rpar]
                    tbank = pb[5]
                    t6w = [("s_sc", r_s6[rpar])]
                    if mc >= 1:
                        t6w.append(("s_sc", router_evac[(mc - 1) % 2]))
                    t6 = te(lambda e, tbank=tbank, wt=wt:
                            e.transpose(tbank[0:8, 0:128], wt, sI32[:]), wait=t6w)
                    r_t6[rpar] = t6
                    router_evac[rpar] = sc(
                        lambda e, mc=mc, tbank=tbank:
                        e.copy(w_fmb[0:8, mc * 128:(mc + 1) * 128], tbank[0:8, 0:128]),
                        wait=[("s_te", t6)])

                for q in range(NST):
                    if q not in st_ready_ve:
                        st_ready_ve[q] = cnt["s_sc"]
                ready_st[0] = NST - 1
                for col in range(NMC + 2):
                    if col >= 2:
                        emitR3(col - 2)
                    if 1 <= col <= NMC:
                        emitR2(col - 1)
                    if col < NMC:
                        emitR1(col)

                ve_xn_all = cnt["s_ve"]
                sc_r_all = cnt["s_sc"]

                bc0_ev = {}
                for stt in range(NST):
                    bk = pb[4 + stt % 2]
                    bwl = (([("s_sc", sc_r_all)] + (lc_wait if stt == 0 else []))
                           if stt < 2 else [("s_sc", bc0_ev[stt - 2])])
                    m_ = te(lambda e, bk=bk, stt=stt:
                            e.matmul(bk[:, 0:STW], ssel[:, 0:128],
                                     w_fmb[:, stt * STW:(stt + 1) * STW],
                                     start=True, stop=True),
                            wait=bwl)
                    bc0_ev[stt] = sc(
                        lambda e, bk=bk, stt=stt:
                        e.copy(wbc[:, 0, stt * STW:(stt + 1) * STW], bk[:, 0:STW]),
                        wait=[("s_te", m_)])
                    pump(2)
                bc0_done = cnt["s_sc"]


                if PH < 1: raise _Done()
                twid = [512, 512, 512, 512, TW - 2048]
                d_evac = {}
                su_h, ss_h = {}, {}
                for t5 in range(5):
                    wt_ = twid[t5]
                    c0 = t5 * 512
                    tc0 = c0 // 128
                    ntc_ = wt_ // 128
                    bu_, bs_, bd_ = pb[4], pb[5], pb[4]
                    uwl = (([("s_ve", ve_xn_all), ("s_sc", bc0_done)] + lc_wait) if t5 == 0
                           else [("s_ve", d_evac[t5 - 1])])
                    mu = None
                    for dc in range(NDC):
                        rhs = xn_fm[:, tc0:tc0 + ntc_, dc, :]
                        mu = te(lambda e, dc=dc, bu_=bu_, rhs=rhs, wt_=wt_:
                                e.matmul(bu_[:, 0:wt_], sWiT[:, dc, :], rhs,
                                         start=(dc == 0), stop=(dc == NDC - 1)),
                                wait=(uwl if dc == 0 else ()), inc=(dc == NDC - 1))
                    su_h[t5] = sc(lambda e, bu_=bu_, c0=c0, wt_=wt_:
                                  e.activation(u_bf[:, c0:c0 + wt_], bu_[:, 0:wt_],
                                               ACT.Identity, bias=sbi[:]),
                                  wait=[("s_te", mu)])
                    swl2 = [] if t5 < 1 else [("s_sc", ss_h[t5 - 1])]
                    ms = None
                    for dc in range(NDC):
                        rhs = xn_fm[:, tc0:tc0 + ntc_, dc, :]
                        ms = te(lambda e, dc=dc, bs_=bs_, rhs=rhs, wt_=wt_:
                                e.matmul(bs_[:, 0:wt_], sWsT[:, dc, :], rhs,
                                         start=(dc == 0), stop=(dc == NDC - 1)),
                                wait=(swl2 if dc == 0 else ()), inc=(dc == NDC - 1))
                    ss_h[t5] = sc(lambda e, bs_=bs_, c0=c0, wt_=wt_:
                                  e.activation(SPb[:, 1 + c0:1 + c0 + wt_], bs_[:, 0:wt_],
                                               ACT.Identity, bias=sbs[:]),
                                  wait=[("s_te", ms)])
                    vs = ss_h[t5]
                    dwl = [("s_sc", vs), ("s_ve", v_init), ("s_sc", su_h[t5])]
                    te(lambda e, bd_=bd_, c0=c0, wt_=wt_:
                       e.matmul(bd_[:, 0:wt_], sAT[:], SPb[:, c0:c0 + wt_],
                                start=True, stop=False, skip_group_check=True),
                       wait=dwl, inc=False)
                    md2 = te(lambda e, bd_=bd_, c0=c0, wt_=wt_:
                             e.matmul(bd_[:, 0:wt_], sBT[:], u_bf[:, c0:c0 + wt_],
                                      start=False, stop=True, skip_group_check=True),
                             wait=[("s_sc", su_h[t5])])
                    pump(2)
                    if t5 == 0:
                        ve(lambda e, bd_=bd_: e.tensor_mul(Dt[:, 0:W], bd_[:, 0:W], sdm[:]),
                           wait=[("s_te", md2)])
                        d_evac[t5] = ve(lambda e, bd_=bd_:
                                        e.tensor_copy(Dt[:, W:512], bd_[:, W:512]))
                    else:
                        d_evac[t5] = ve(lambda e, bd_=bd_, c0=c0, wt_=wt_:
                                        e.tensor_copy(Dt[:, c0:c0 + wt_], bd_[:, 0:wt_]),
                                        wait=[("s_te", md2)])
                ve_d_all = cnt["s_ve"]

                if PH < 2: raise _Done()
                ch_act = {}
                for s in range(NSTEP):
                    pump(2 if s % 5 < 3 else 1)
                    b = s % 2
                    if b == 0:
                        for k in range(2):
                            sk = s + k
                            if sk >= NSTEP:
                                break
                            wl = []
                            if k == 0:
                                wl = [("s_ve", ve_d_all),
                                      ("s_sc", ch_act[s - 1] if s > 0 else cnt["s_sc"])]
                            te(lambda e, sk=sk, bk=pb[4 + sk % 2]:
                               e.matmul(bk[:, 0:NCH], sI16[:], Dt[:, sk:sk + SSTR:LCH],
                                        start=True, stop=False, skip_group_check=True),
                               wait=wl, inc=False)
                    if s == 0:
                        rhs = gz[:]
                    elif s <= W:
                        rhs = gw[:, ((s - 1) % 2) * NCH:((s - 1) % 2) * NCH + NCH]
                    else:
                        t0 = s - 1 - W
                        rhs = G[:, t0:t0 + SSTR:LCH]
                    wl = [("s_sc", ch_act[s - 1])] if s > 0 else []
                    mv = te(lambda e, b=b, rhs=rhs:
                            e.matmul(pb[4 + b][:, 0:NCH], sAT[:], rhs,
                                     start=False, stop=True, skip_group_check=True),
                            wait=wl)
                    if s < W:
                        dst = gw[:, (s % 2) * NCH:(s % 2) * NCH + NCH]
                    else:
                        t0 = s - W
                        dst = G[:, t0:t0 + SSTR:LCH]
                    ch_act[s] = sc(lambda e, b=b, dst=dst:
                                   e.activation(dst, pb[4 + b][:, 0:NCH], ACT.Tanh),
                                   wait=[("s_te", mv)])
                pump_all()
                sc_chain = cnt["s_sc"]
                for stt in range(NST):
                    for fc in range(NFC):
                        fwl = [("s_sc", bc0_done)] if (stt == 0 and fc == 0) else []
                        ew_done[(0, fc, stt)] = ve(
                            lambda e, fc=fc, stt=stt:
                            e.tensor_mul(act_w[:, fc, stt * STW:(stt + 1) * STW],
                                         act_w[:, fc, stt * STW:(stt + 1) * STW],
                                         wbc[:, 0, stt * STW:(stt + 1) * STW]),
                            wait=fwl)
                wbc_cons0 = cnt["s_ve"]

                if PH < 3: raise _Done()
                vstate = {}
                for stt in range(NST):
                    c0 = stt * STW
                    vstate[stt] = ve(lambda e, c0=c0:
                                     e.tensor_add(u_bf[:, c0:c0 + STW], G[:, c0:c0 + STW],
                                                  SPb[:, 1 + W + c0:1 + W + c0 + STW]),
                                     wait=([("s_sc", sc_chain)] if stt == 0 else ()))
                ys_evac = {}
                for stt in range(NST):
                    bk = pb[4 + stt % 2]
                    ywl = [("s_ve", vstate[stt])]
                    ywl.append(("s_sc", sc_chain if stt < 2 else ys_evac[stt - 2]))
                    m_ = te(lambda e, bk=bk, stt=stt:
                            e.matmul(bk[:, 0:STW], sCT[:], u_bf[:, stt * STW:(stt + 1) * STW],
                                     start=True, stop=True),
                            wait=ywl)
                    ys_evac[stt] = sc(lambda e, bk=bk, stt=stt:
                                      e.copy(ys_bf[:, stt * STW:(stt + 1) * STW], bk[:, 0:STW]),
                                      wait=[("s_te", m_)])
                sc_ys = cnt["s_sc"]

                attn_ev = {}
                for hc in range(NDC):
                    for stt in range(NST):
                        idx = hc * NST + stt
                        bk = pb[4 + idx % 2]
                        wl = [("s_sc", sc_ys if idx < 2 else attn_ev[idx - 2])]
                        te(lambda e, bk=bk, hc=hc, stt=stt:
                           e.matmul(bk[:, 0:STW], sWoT[:, hc, :],
                                    ys_bf[:, stt * STW:(stt + 1) * STW],
                                    start=True, stop=False, skip_group_check=True),
                           wait=wl, inc=False)
                        m_ = te(lambda e, bk=bk, hc=hc, stt=stt:
                                e.matmul(bk[:, 0:STW], sdb[:, hc * 128:(hc + 1) * 128],
                                         w_fmb[:, stt * STW:(stt + 1) * STW],
                                         start=False, stop=True, skip_group_check=True))
                        attn_ev[idx] = sc(
                            lambda e, bk=bk, hc=hc, stt=stt:
                            e.activation(acc[:, hc, stt * STW:(stt + 1) * STW],
                                         bk[:, 0:STW], ACT.Identity, bias=sbo[:, hc:hc + 1]),
                            wait=[("s_te", m_)])
                sc_attn = cnt["s_sc"]

                fin_add = {}
                out_done = {}

                fxload = {}

                def emit_final_load(mc):
                    par4 = mc % 4
                    fxsem = "s_lf%d" % (mc % 2)
                    xwl = [("s_ve", fin_add[par4])] if mc >= 4 else []
                    fxload[mc] = (fxsem,
                                  dma("gp", fxsem, xres[:, par4],
                                      d_xm[mc * 128:(mc + 1) * 128, :], wait=xwl))

                def emit_final_group(stt):
                    if stt == 0:
                        for mc_ in range(4):
                            emit_final_load(mc_)
                    for mc in range(stt * 4, stt * 4 + 4):
                        par = mc % 2
                        par4 = mc % 4
                        fosem = "s_ou%d" % par
                        fxsem, lxv = fxload[mc]
                        bank = pbt[par]
                        tvals = []
                        for hc in range(NDC):
                            wl2 = []
                            if hc == 0:
                                wl2 = [("s_ve", acc_add[(NDC - 1, stt)])]
                                wl2.append(("s_ve", fin_add.get(par, cnt["s_ve"])
                                            if mc >= 2 else cnt["s_ve"]))
                            t_ = te(lambda e, bank=bank, hc=hc, mc=mc:
                                    e.transpose(bank[:, hc * 128:(hc + 1) * 128],
                                                acc[:, hc, mc * 128:(mc + 1) * 128], sI16[:]),
                                    wait=wl2, inc=(hc in (3, 7)))
                            tvals.append(t_)
                        vwl = [("s_te", tvals[7]), (fxsem, lxv)]
                        if mc >= 2:
                            vwl.append((fosem, out_done[par]))
                        ve(lambda e, bank=bank, par=par, par4=par4:
                           e.tensor_add(otm[:, par, :], bank[:], xres[:, par4, :]),
                           wait=vwl)
                        fin_add[par] = cnt["s_ve"]
                        fin_add[par4] = cnt["s_ve"]
                        if mc + 4 < NMC:
                            emit_final_load(mc + 4)
                        out_done[par] = dma("sy", fosem, d_y[mc * 128:(mc + 1) * 128, :],
                                            otm[:, par], wait=[("s_ve", fin_add[par])])


                if PH < 4: raise _Done()
                acc_add = {}
                wbc_cons = {0: wbc_cons0}
                dn_read = {}
                for ei in range(NEXP):
                    epar = ei % 2
                    if ei == 0:
                        bc_ev_l = bc0_ev
                    else:
                     bc_ev_l = {}
                     for stt in range(NST):
                        bk = pb[4 + stt % 2]
                        bwl = []
                        if stt < 2:
                            if ei == 0:
                                bwl.append(("s_sc", sc_attn))
                            else:
                                bwl.append(("s_ve", dn_read[stt % 2]))
                            if ei >= 2:
                                bwl.append(("s_ve", wbc_cons[epar]))
                        else:
                            bwl.append(("s_sc", bc_ev_l[stt - 2]))
                        m_ = te(lambda e, bk=bk, ei=ei, stt=stt:
                                e.matmul(bk[:, 0:STW], ssel[:, ei * 128:(ei + 1) * 128],
                                         w_fmb[:, stt * STW:(stt + 1) * STW],
                                         start=True, stop=True),
                                wait=bwl)
                        bc_ev_l[stt] = sc(
                            lambda e, bk=bk, epar=epar, stt=stt:
                            e.copy(wbc[:, epar, stt * STW:(stt + 1) * STW], bk[:, 0:STW]),
                            wait=[("s_te", m_)])

                    if ei > 0:
                        for fc in range(NFC):
                            for stt in range(NST):
                                emit_gu_tile(ei, fc, stt, weighted=True,
                                             bc_ev_l=bc_ev_l)
                        wbc_cons[epar] = cnt["s_ve"]

                    for fc in range(NFC):
                        par = (ei * NFC + fc) % 2
                        ldsem, lwd = wload[("d", ei, fc)]
                        for stt in range(NST):
                            idx = ei * NFC * NST + fc * NST + stt
                            dp = (fc * NST + stt) % 2
                            bk = pb[4 + dp]
                            wl = [(ldsem, lwd), ("s_ve", ew_done[(ei, NFC - 1, stt)])]
                            if ei == 0 and fc == 0 and stt < 2:
                                wl.append(("s_sc", sc_attn))
                            elif fc == 0 and stt < 2:
                                wl.append(("s_sc", bc_ev_l[stt + 2]))
                            elif idx >= 2:
                                wl.append(("s_ve", dn_read[dp]))
                            md = None
                            for ic in range(NDC):
                                md = te(lambda e, ic=ic, bk=bk, par=par, stt=stt:
                                        e.matmul(bk[:, 0:STW], wdb_[:, par, ic, :],
                                                 act_w[:, ic, stt * STW:(stt + 1) * STW],
                                                 start=(ic == 0), stop=(ic == NDC - 1),
                                                 skip_group_check=True),
                                        wait=(wl if ic == 0 else ()),
                                        inc=(ic == NDC - 1))
                            if stt == NST - 1:
                                wd_cons[par] = cnt["s_te"]
                                nxt = ei * NFC + fc + 2
                                if nxt < NEXP * NFC:
                                    ne, nf = divmod(nxt, NFC)
                                    sched_wdma("d", ne, nf)
                            awl = [("s_te", md)]
                            if ei == 0:
                                awl.append(("s_sc", attn_ev[fc * NST + stt]))
                            va = ve(lambda e, fc=fc, stt=stt, bk=bk:
                                    e.tensor_add(acc[:, fc, stt * STW:(stt + 1) * STW],
                                                 acc[:, fc, stt * STW:(stt + 1) * STW],
                                                 bk[:, 0:STW]),
                                    wait=awl)
                            dn_read[dp] = va
                            acc_add[(fc, stt)] = va
                            if PH >= 5 and ei == NEXP - 1 and fc == NFC - 1:
                                emit_final_group(stt)

            except _Done:
                pass

            if debug:
                vw_, sw_, tw_ = cnt["s_ve"], cnt["s_sc"], cnt["s_te"]
                def gdbg(name, src_ap):
                    dma("gp", "s_ou0", dbg[name], src_ap,
                        wait=[("s_ve", vw_), ("s_sc", sw_), ("s_te", tw_)])
                gdbg("xnfm", xn_fm[:].rearrange("p a b c -> p (a b c)"))
                gdbg("wfm", w_fmb[:])
                if PH >= 1:
                    gdbg("D", Dt[:]); gdbg("u", u_bf[:]); gdbg("sp", SPb[:, 1:])
                if PH >= 2:
                    gdbg("G", G[:])
                if PH >= 3:
                    gdbg("ys", ys_bf[:])
                    gdbg("acc", acc[:].rearrange("p a b -> p (a b)"))

            fin_o0 = cnt.get("s_ou0", 0)
            fin_o1 = cnt.get("s_ou1", 0)
            def fin(e):
                if fin_o0:
                    e.wait_ge(s_ou0, fin_o0)
                if fin_o1:
                    e.wait_ge(s_ou1, fin_o1)
            ops["sy"].append(fin)

            @block.gpsimd
            def _(g):
                for f in ops["gp"]:
                    f(g)

            @block.sync
            def _(s_):
                for f in ops["sy"]:
                    f(s_)

            @block.tensor
            def _(t_):
                for f in ops["te"]:
                    f(t_)

            @block.vector
            def _(v_):
                for f in ops["ve"]:
                    f(v_)

            @block.scalar
            def _(sc_):
                for f in ops["sc"]:
                    f(sc_)
    return nc


def prep_inputs(L0_attn_x, L0_attn_Wi, L0_attn_bi, L0_attn_Ws, L0_attn_bs,
                L0_attn_Wo, L0_attn_bo, L0_attn_A, L0_attn_B, L0_attn_C,
                L0_ffn_router_weight, L0_ffn_gate_up_proj, L0_ffn_gate_up_proj_bias,
                L0_ffn_down_proj, L0_ffn_down_proj_bias):
    f = lambda a: np.ascontiguousarray(np.asarray(a), dtype=np.float32)
    x = f(L0_attn_x).reshape(16384, DM)
    Wi, Ws, Wo = f(L0_attn_Wi), f(L0_attn_Ws), f(L0_attn_Wo)
    A, B, C = f(L0_attn_A), f(L0_attn_B), f(L0_attn_C)
    bi, bs, bo = f(L0_attn_bi), f(L0_attn_bs), f(L0_attn_bo)
    Wr = f(L0_ffn_router_weight)
    gup, gub = f(L0_ffn_gate_up_proj), f(L0_ffn_gate_up_proj_bias)
    dwn, dwb = f(L0_ffn_down_proj), f(L0_ffn_down_proj_bias)

    com = {}
    com["eye"] = np.eye(128, dtype=np.float32)
    com["AT"] = np.ascontiguousarray(A.T)
    com["BT"] = np.ascontiguousarray(B.T)
    com["CT"] = np.ascontiguousarray(C.T)
    com["WiT"] = np.ascontiguousarray(Wi.T.reshape(NDC, 128, 128).transpose(1, 0, 2))
    com["WsT"] = np.ascontiguousarray(Ws.T.reshape(NDC, 128, 128).transpose(1, 0, 2))
    com["WoT"] = np.ascontiguousarray(Wo.T.reshape(128, NDC, 128))
    com["Wr"] = np.ascontiguousarray(Wr.reshape(NDC, 128, 8).transpose(1, 0, 2))
    com["bi"] = bi.reshape(128, 1)
    com["bs"] = bs.reshape(128, 1)
    com["bo"] = np.ascontiguousarray(bo.reshape(NDC, 128).T)
    gate = np.ascontiguousarray(gup[:, :, 0::2])
    up = np.ascontiguousarray(gup[:, :, 1::2])
    com["Wg"] = np.ascontiguousarray(
        gate.reshape(NEXP, NDC, 128, NFC, 128).transpose(0, 3, 2, 1, 4))
    com["Wu"] = np.ascontiguousarray(
        up.reshape(NEXP, NDC, 128, NFC, 128).transpose(0, 3, 2, 1, 4))
    com["Wd"] = np.ascontiguousarray(
        dwn.reshape(NEXP, NDC, 128, NFC, 128).transpose(0, 3, 2, 1, 4))
    gb = gub[:, 0::2].reshape(NEXP, NFC, 128)
    ub = gub[:, 1::2].reshape(NEXP, NFC, 128)
    com["bg"] = np.ascontiguousarray(gb.reshape(NEXP * NFC, 128).T)
    com["bu"] = np.ascontiguousarray(ub.reshape(NEXP * NFC, 128).T)
    com["dbr"] = np.ascontiguousarray(dwb)
    com["on1"] = np.ones((1, 128), np.float32)
    sel = np.zeros((NEXP, NEXP * 128), np.float32)
    for e_ in range(NEXP):
        sel[e_, e_ * 128:(e_ + 1) * 128] = 1.0
    com["sel"] = sel

    in_maps = []
    for c in range(8):
        m = dict(com)
        r0 = c * NTOK
        m["xm"] = np.ascontiguousarray(x[r0:r0 + NTOK])
        if c % 4 == 0:
            m["xw"] = np.zeros((W, DM), np.float32)
            m["dmask"] = np.zeros((128, W), np.float32)
        else:
            m["xw"] = np.ascontiguousarray(x[r0 - W:r0])
            m["dmask"] = np.ones((128, W), np.float32)
        in_maps.append(m)
    return in_maps


_cache = {}


def kernel(**inputs):
    in_maps = prep_inputs(**inputs)
    if "nc" not in _cache:
        _cache["nc"] = build_nc(debug=False)
    res = run_bass_kernel_spmd(_cache["nc"], in_maps, core_ids=list(range(8)))
    out = np.concatenate([np.asarray(res.results[c]["y"]) for c in range(8)], axis=0)
    return out.reshape(2, 8192, DM)


# version 20
# speedup vs baseline: 1.0146x; 1.0146x over previous
import numpy as np
import concourse.bass as bass
import concourse.mybir as mybir
from concourse.bass_utils import run_bass_kernel_spmd

F32 = mybir.dt.float32
BF16 = mybir.dt.bfloat16
AX = mybir.AxisListType.X
ALU = mybir.AluOpType
ACT = mybir.ActivationFunctionType

DM = 1024
DS = 128
NTOK = 2048
W = 128
LCH = 32
NCH = NTOK // LCH
NSTEP = W + LCH
TW = W + NTOK
NDC = DM // 128
NTC = TW // 128
WC = W // 128
NMC = NTOK // 128
NEXP = 8
NFC = 8
NST = 4
STW = 512
EPS = 1e-5
ALPHA = 1.702
LIMIT = 7.0
SSTR = LCH * (NCH - 1) + 1


def build_nc(debug=False, upto="full", ntc_run=None, skip_big=False):
    import os as _os
    LNL = int(_os.environ.get("MK_LN_LEVEL", "9"))
    PH = {"ln0": -1, "ln": 0, "usp": 1, "chain": 2, "attn": 3, "ffn": 4, "full": 5}[upto]
    nc = bass.Bass("TRN2", target_bir_lowering=False)

    d_xm = nc.dram_tensor("xm", [NTOK, DM], F32, kind="ExternalInput").ap()
    d_xw = nc.dram_tensor("xw", [W, DM], F32, kind="ExternalInput").ap()
    d_dmask = nc.dram_tensor("dmask", [128, W], F32, kind="ExternalInput").ap()
    d_eye = nc.dram_tensor("eye", [128, 128], F32, kind="ExternalInput").ap()
    d_AT = nc.dram_tensor("AT", [128, 128], F32, kind="ExternalInput").ap()
    d_BT = nc.dram_tensor("BT", [128, 128], F32, kind="ExternalInput").ap()
    d_CT = nc.dram_tensor("CT", [128, 128], F32, kind="ExternalInput").ap()
    d_WiT = nc.dram_tensor("WiT", [128, NDC, 128], F32, kind="ExternalInput").ap()
    d_WsT = nc.dram_tensor("WsT", [128, NDC, 128], F32, kind="ExternalInput").ap()
    d_WoT = nc.dram_tensor("WoT", [128, NDC, 128], F32, kind="ExternalInput").ap()
    d_Wr = nc.dram_tensor("Wr", [128, NDC, 8], F32, kind="ExternalInput").ap()
    d_bi = nc.dram_tensor("bi", [128, 1], F32, kind="ExternalInput").ap()
    d_bs = nc.dram_tensor("bs", [128, 1], F32, kind="ExternalInput").ap()
    d_bo = nc.dram_tensor("bo", [128, 8], F32, kind="ExternalInput").ap()
    d_bg = nc.dram_tensor("bg", [128, NEXP * NFC], F32, kind="ExternalInput").ap()
    d_bu = nc.dram_tensor("bu", [128, NEXP * NFC], F32, kind="ExternalInput").ap()
    d_dbr = nc.dram_tensor("dbr", [NEXP, DM], F32, kind="ExternalInput").ap()
    d_on1 = nc.dram_tensor("on1", [1, 128], F32, kind="ExternalInput").ap()
    d_sel = nc.dram_tensor("sel", [NEXP, NEXP * 128], F32, kind="ExternalInput").ap()
    if not skip_big:
        d_Wg = nc.dram_tensor("Wg", [NEXP, NFC, 128, NDC, 128], F32, kind="ExternalInput").ap()
        d_Wu = nc.dram_tensor("Wu", [NEXP, NFC, 128, NDC, 128], F32, kind="ExternalInput").ap()
        d_Wd = nc.dram_tensor("Wd", [NEXP, NFC, 128, NDC, 128], F32, kind="ExternalInput").ap()
    d_y = nc.dram_tensor("y", [NTOK, DM], F32, kind="ExternalOutput").ap()
    dbg = {}
    if debug:
        dbg["xnfm"] = nc.dram_tensor("dbg_xnfm", [128, NTC * DM], F32, kind="ExternalOutput").ap()
        dbg["D"] = nc.dram_tensor("dbg_D", [128, TW], F32, kind="ExternalOutput").ap()
        dbg["G"] = nc.dram_tensor("dbg_G", [128, NTOK], F32, kind="ExternalOutput").ap()
        dbg["wfm"] = nc.dram_tensor("dbg_wfm", [8, NTOK], F32, kind="ExternalOutput").ap()
        dbg["ys"] = nc.dram_tensor("dbg_ys", [128, NTOK], F32, kind="ExternalOutput").ap()
        dbg["acc"] = nc.dram_tensor("dbg_acc", [128, NDC * NTOK], F32, kind="ExternalOutput").ap()
        dbg["u"] = nc.dram_tensor("dbg_u", [128, TW], F32, kind="ExternalOutput").ap()
        dbg["sp"] = nc.dram_tensor("dbg_sp", [128, TW], F32, kind="ExternalOutput").ap()

    ops = {k: [] for k in ("te", "ve", "sc", "gp", "sy")}
    cnt = {}

    def _inc(sem, n=1):
        cnt[sem] = cnt.get(sem, 0) + n
        return cnt[sem]

    from contextlib import ExitStack
    with ExitStack() as _es:
        xn_fm = _es.enter_context(nc.sbuf_tensor("xn_fm", [128, NTC, NDC, 128], BF16))
        SPb = _es.enter_context(nc.sbuf_tensor("SPb", [128, 1 + TW], BF16))
        u_bf = _es.enter_context(nc.sbuf_tensor("u_bf", [128, TW], BF16))
        Dt = _es.enter_context(nc.sbuf_tensor("Dt", [128, TW], BF16))
        G = _es.enter_context(nc.sbuf_tensor("G", [128, NTOK], BF16))
        gw = _es.enter_context(nc.sbuf_tensor("gw", [128, 2 * NCH], BF16))
        gz = _es.enter_context(nc.sbuf_tensor("gz", [128, NCH], BF16))
        ys_bf = _es.enter_context(nc.sbuf_tensor("ys_bf", [128, NTOK], BF16))
        w_fmb = _es.enter_context(nc.sbuf_tensor("w_fmb", [8, NTOK], BF16))
        acc = _es.enter_context(nc.sbuf_tensor("acc", [128, NDC, NTOK], BF16))
        act_w = _es.enter_context(nc.sbuf_tensor("act_w", [128, NFC, NTOK], BF16))
        wbc = _es.enter_context(nc.sbuf_tensor("wbc", [128, 2, NTOK], BF16))
        sAT = _es.enter_context(nc.sbuf_tensor("sAT", [128, 128], BF16))
        sBT = _es.enter_context(nc.sbuf_tensor("sBT", [128, 128], BF16))
        sCT = _es.enter_context(nc.sbuf_tensor("sCT", [128, 128], BF16))
        sI16 = _es.enter_context(nc.sbuf_tensor("sI16", [128, 128], BF16))
        sI32 = _es.enter_context(nc.sbuf_tensor("sI32", [128, 128], F32))
        sWiT = _es.enter_context(nc.sbuf_tensor("sWiT", [128, NDC, 128], BF16))
        sWsT = _es.enter_context(nc.sbuf_tensor("sWsT", [128, NDC, 128], BF16))
        sWoT = _es.enter_context(nc.sbuf_tensor("sWoT", [128, NDC, 128], BF16))
        sWr = _es.enter_context(nc.sbuf_tensor("sWr", [128, NDC, 8], BF16))
        sbi = _es.enter_context(nc.sbuf_tensor("sbi", [128, 1], F32))
        sbs = _es.enter_context(nc.sbuf_tensor("sbs", [128, 1], F32))
        sbo = _es.enter_context(nc.sbuf_tensor("sbo", [128, 8], F32))
        sbg = _es.enter_context(nc.sbuf_tensor("sbg", [128, NEXP * NFC], F32))
        sbu = _es.enter_context(nc.sbuf_tensor("sbu", [128, NEXP * NFC], F32))
        sdb = _es.enter_context(nc.sbuf_tensor("sdb", [NEXP, DM], BF16))
        son1 = _es.enter_context(nc.sbuf_tensor("son1", [1, 128], BF16))
        ssel = _es.enter_context(nc.sbuf_tensor("ssel", [NEXP, NEXP * 128], BF16))
        sdm = _es.enter_context(nc.sbuf_tensor("sdm", [128, W], F32))
        xln = _es.enter_context(nc.sbuf_tensor("xln", [128, 3, DM], F32))
        xnb = _es.enter_context(nc.sbuf_tensor("xnb", [128, 2, DM], BF16))
        stat = _es.enter_context(nc.sbuf_tensor("stat", [128, 16], F32))
        lgb = _es.enter_context(nc.sbuf_tensor("lgb", [128, 2, 8], F32))
        rt = _es.enter_context(nc.sbuf_tensor("rt", [128, 2, 16], F32))
        wtm = _es.enter_context(nc.sbuf_tensor("wtm", [128, 2, 8], F32))
        wgu = _es.enter_context(nc.sbuf_tensor("wgu", [128, 4, NDC, 128], BF16))
        wdb_ = _es.enter_context(nc.sbuf_tensor("wdb_", [128, 2, NFC, 128], BF16))
        et2 = _es.enter_context(nc.sbuf_tensor("et2", [128, 2, STW], BF16))
        et2w = _es.enter_context(nc.sbuf_tensor("et2w", [128, 2, STW], BF16))
        etg = _es.enter_context(nc.sbuf_tensor("etg", [128, 2, STW], BF16))
        etu = _es.enter_context(nc.sbuf_tensor("etu", [128, 2, STW], BF16))
        xres = _es.enter_context(nc.sbuf_tensor("xres", [128, 4, DM], BF16))
        otm = _es.enter_context(nc.sbuf_tensor("otm", [128, 2, DM], F32))
        pb0 = _es.enter_context(nc.psum_tensor([128, 512], F32))
        pb1 = _es.enter_context(nc.psum_tensor([128, 512], F32))
        pb2 = _es.enter_context(nc.psum_tensor([128, 512], F32))
        pb3 = _es.enter_context(nc.psum_tensor([128, 512], F32))
        pb4 = _es.enter_context(nc.psum_tensor([128, 512], F32))
        pb5 = _es.enter_context(nc.psum_tensor([128, 512], F32))
        pbtA = _es.enter_context(nc.psum_tensor([128, 1024], BF16))
        pbtB = _es.enter_context(nc.psum_tensor([128, 1024], BF16))
        s_lc = _es.enter_context(nc.semaphore("s_lc"))
        s_lc2 = _es.enter_context(nc.semaphore("s_lc2"))
        s_lc3 = _es.enter_context(nc.semaphore("s_lc3"))
        s_lw0 = _es.enter_context(nc.semaphore("s_lw0"))
        s_lw1 = _es.enter_context(nc.semaphore("s_lw1"))
        s_ld0 = _es.enter_context(nc.semaphore("s_ld0"))
        s_ld1 = _es.enter_context(nc.semaphore("s_ld1"))
        s_lx0 = _es.enter_context(nc.semaphore("s_lx0"))
        s_lx1 = _es.enter_context(nc.semaphore("s_lx1"))
        s_lf0 = _es.enter_context(nc.semaphore("s_lf0"))
        s_lf1 = _es.enter_context(nc.semaphore("s_lf1"))
        s_te = _es.enter_context(nc.semaphore("s_te"))
        s_ve = _es.enter_context(nc.semaphore("s_ve"))
        s_sc = _es.enter_context(nc.semaphore("s_sc"))
        s_ou0 = _es.enter_context(nc.semaphore("s_ou0"))
        s_ou1 = _es.enter_context(nc.semaphore("s_ou1"))
        s_gp = _es.enter_context(nc.semaphore("s_gp"))
        block = _es.enter_context(nc.Block())
        if True:
            pb = [pb0, pb1, pb2, pb3, pb4, pb5]
            pbt = [pbtA, pbtB]
            sems = {"s_lc": s_lc, "s_lc2": s_lc2, "s_lc3": s_lc3, "s_lw0": s_lw0, "s_lw1": s_lw1,
                    "s_ld0": s_ld0, "s_ld1": s_ld1,
                    "s_lx0": s_lx0, "s_lx1": s_lx1, "s_lf0": s_lf0, "s_lf1": s_lf1, "s_te": s_te,
                    "s_ve": s_ve, "s_sc": s_sc, "s_ou0": s_ou0, "s_ou1": s_ou1,
                    "s_gp": s_gp}

            def mk(eng_key, sem_name, self_wait=False):
                def emitfn(fn, wait=(), inc=True):
                    for sn, val in wait:
                        assert val <= cnt.get(sn, 0), (sn, val, cnt.get(sn, 0), eng_key)
                    w = tuple(wait)
                    if self_wait and cnt.get(sem_name, 0) > 0:
                        w = ((sem_name, cnt[sem_name]),) + w
                    v = _inc(sem_name) if inc else None
                    def op(e, fn=fn, w=w, v=v, sem_name=sem_name):
                        for sn, val in w:
                            e.wait_ge(sems[sn], val)
                        ins = fn(e)
                        if v is not None:
                            ins.then_inc(sems[sem_name], 1)
                    ops[eng_key].append(op)
                    return cnt.get(sem_name, 0)
                return emitfn

            te = mk("te", "s_te")
            ve = mk("ve", "s_ve", self_wait=True)
            sc = mk("sc", "s_sc", self_wait=True)
            gpc = mk("gp", "s_gp", self_wait=True)

            class _Done(Exception):
                pass

            def dma(eng_key, sem_name, dst, src, wait=()):
                for sn, val in wait:
                    assert val <= cnt.get(sn, 0), (sn, val, cnt.get(sn, 0), "dma")
                w = tuple(wait)
                if cnt.get(sem_name, 0) > 0:
                    w = ((sem_name, cnt[sem_name]),) + w
                _inc(sem_name, 16)
                def op(e, dst=dst, src=src, w=w, sem_name=sem_name):
                    for sn, val in w:
                        e.wait_ge(sems[sn], val)
                    e.dma_start(dst, src).then_inc(sems[sem_name], 16)
                ops[eng_key].append(op)
                return cnt[sem_name]

            try:
                dma("gp", "s_lc2", sI16[:], d_eye)
                lc_eye = dma("gp", "s_lc2", sI32[:], d_eye)
                for j, (dst, csrc) in enumerate(
                        [(sAT[:], d_AT), (sBT[:], d_BT), (sCT[:], d_CT),
                         (sWiT[:], d_WiT), (sWsT[:], d_WsT), (sWoT[:], d_WoT),
                         (sWr[:], d_Wr), (sbi[:], d_bi), (sbs[:], d_bs),
                         (sbo[:], d_bo), (sbg[:], d_bg), (sbu[:], d_bu),
                         (sdb[:], d_dbr), (son1[:], d_on1), (ssel[:], d_sel),
                         (sdm[:], d_dmask)]):
                    dma("gp", "s_lc" if j % 2 == 0 else "s_lc3", dst, csrc)
                lc_wait = [("s_lc", cnt["s_lc"]), ("s_lc3", cnt["s_lc3"])]

                v_init = ve(lambda e: e.memset(gz[:], 0.0))
                v_init = ve(lambda e: e.memset(stat[:, 14:15], EPS))
                v_init = ve(lambda e: e.memset(SPb[:, 0:1], 0.0))

                x_srcs = [d_xw[t * 128:(t + 1) * 128, :] for t in range(WC)] + \
                         [d_xm[t * 128:(t + 1) * 128, :] for t in range(NMC)]
                ln_free = {}
                xnb_free = {}
                tr_evac = {}
                router_evac = {}
                logit_evac = {}
                wload = {}
                gu_cons = {}
                wd_cons = {}

                def sched_wdma(kind, ei, fc):
                    par = (ei * NFC + fc) % 2
                    if kind == "g":
                        dst, src, ru, sem = wgu[:, 2 * par + 0], d_Wg[ei, fc], gu_cons.get(par), "s_lw%d" % par
                    elif kind == "u":
                        dst, src, ru, sem = wgu[:, 2 * par + 1], d_Wu[ei, fc], gu_cons.get(par), "s_lw%d" % par
                    else:
                        dst, src, ru, sem = wdb_[:, par], d_Wd[ei, fc], wd_cons.get(par), "s_ld%d" % par
                    wl = [("s_te", ru)] if ru is not None else []
                    wload[(kind, ei, fc)] = (sem, dma("gp", sem, dst, src, wait=wl))

                for fc in range(2):
                    sched_wdma("g", 0, fc); sched_wdma("u", 0, fc)
                sched_wdma("d", 0, 0); sched_wdma("d", 0, 1)

                ew_done = {}
                ts_read = {}
                gtile = [0]

                def emit_gu_tile(ei, fc, stt, weighted, bc_ev_l=None):
                    for _ in gu_tile_gen(ei, fc, stt, weighted, bc_ev_l):
                        pass

                def gu_tile_gen(ei, fc, stt, weighted, bc_ev_l=None):
                    par = (ei * NFC + fc) % 2
                    lwsem, lwg = wload[("u", ei, fc)]
                    epar = ei % 2
                    tile = gtile[0]
                    gtile[0] += 1
                    tp = tile % 2
                    bg_ = pb[tp]
                    bu2 = pb[2 + tp]
                    gwl = [(lwsem, lwg), ("s_sc", st_ready_ve[stt])]
                    if tile < 2:
                        gwl.extend(lc_wait)
                    if tile >= 2:
                        gwl.append(("s_sc", ts_read[tp][0]))
                    mg = None
                    for dc in range(NDC):
                        rhs = xn_fm[:, WC + stt * 4:WC + stt * 4 + 4, dc, :]
                        mg = te(lambda e, dc=dc, bg_=bg_, par=par, rhs=rhs:
                                e.matmul(bg_[:, 0:STW], wgu[:, 2 * par, dc, :], rhs,
                                         start=(dc == 0), stop=(dc == NDC - 1)),
                                wait=(gwl if dc == 0 else ()), inc=(dc == NDC - 1))
                        if dc % 2 == 1:
                            yield
                    uwl2 = [("s_sc", ts_read[tp][1])] if tile >= 2 else []
                    mu2 = None
                    for dc in range(NDC):
                        rhs = xn_fm[:, WC + stt * 4:WC + stt * 4 + 4, dc, :]
                        mu2 = te(lambda e, dc=dc, bu2=bu2, par=par, rhs=rhs:
                                 e.matmul(bu2[:, 0:STW], wgu[:, 2 * par + 1, dc, :], rhs,
                                          start=(dc == 0), stop=(dc == NDC - 1)),
                                 wait=(uwl2 if dc == 0 else ()), inc=(dc == NDC - 1))
                        if dc % 2 == 1:
                            yield
                    if stt == NST - 1:
                        gu_cons[par] = cnt["s_te"]
                        nxt = ei * NFC + fc + 2
                        if nxt < NEXP * NFC:
                            ne, nf = divmod(nxt, NFC)
                            sched_wdma("g", ne, nf)
                            sched_wdma("u", ne, nf)
                    t2 = et2[:, tp]; t2w = et2w[:, tp]
                    tg = etg[:, tp]; tu = etu[:, tp]
                    t1 = tg; t3 = t2w; t4 = tu; t5_ = tu
                    gbias = sbg[:, ei * NFC + fc: ei * NFC + fc + 1]
                    ubias = sbu[:, ei * NFC + fc: ei * NFC + fc + 1]
                    cg = sc(lambda e, tg=tg, bg_=bg_: e.copy(tg, bg_[:, 0:STW]),
                            wait=[("s_te", mg)])
                    cu = sc(lambda e, tu=tu, bu2=bu2: e.copy(tu, bu2[:, 0:STW]),
                            wait=[("s_te", mu2)])
                    ts_read[tp] = (cg, cu)
                    vg = ve(lambda e, t1=t1, tg=tg, gbias=gbias:
                            e.tensor_scalar(t1, tg, gbias, LIMIT, ALU.add, ALU.min),
                            wait=[("s_sc", cg)])
                    ve(lambda e, t4=t4, tu=tu, ubias=ubias:
                       e.tensor_scalar(t4, tu, ubias, LIMIT, ALU.add, ALU.min),
                       wait=[("s_sc", cu)])
                    ve(lambda e, t5_=t5_, t4=t4:
                       e.tensor_scalar(t5_, t4, -LIMIT, 1.0, ALU.max, ALU.add))
                    sg = sc(lambda e, t2=t2, t1=t1:
                            e.activation(t2, t1, ACT.Sigmoid, scale=ALPHA),
                            wait=[("s_ve", vg)])
                    if weighted:
                        wbs = wbc[:, epar, stt * STW:(stt + 1) * STW]
                        ve(lambda e, t2w=t2w, t2=t2, wbs=wbs:
                           e.tensor_mul(t2w, t2, wbs),
                           wait=[("s_sc", sg), ("s_sc", bc_ev_l[stt])])
                        ve(lambda e, t3=t3, t1=t1, t2w=t2w: e.tensor_mul(t3, t1, t2w))
                    else:
                        ve(lambda e, t3=t3, t1=t1, t2=t2: e.tensor_mul(t3, t1, t2),
                           wait=[("s_sc", sg)])
                    g2 = ve(lambda e, fc=fc, stt=stt, t5_=t5_, t3=t3:
                            e.tensor_mul(act_w[:, fc, stt * STW:(stt + 1) * STW],
                                         t5_, t3))
                    ew_done[(ei, fc, stt)] = g2


                pump_q = [(0, fc, stt) for fc in range(NFC) for stt in range(NST)]
                pump_cur = [None]
                ready_st = [-1]
                st_ready_ve = {}

                def pump(n):
                    k = 0
                    while k < n:
                        if pump_cur[0] is None:
                            if not pump_q:
                                return
                            ei_, fc_, stt_ = pump_q[0]
                            if stt_ > ready_st[0]:
                                return
                            pump_q.pop(0)
                            pump_cur[0] = gu_tile_gen(ei_, fc_, stt_, weighted=False)
                        try:
                            next(pump_cur[0])
                            k += 1
                        except StopIteration:
                            pump_cur[0] = None

                def pump_all():
                    for q in range(NST):
                        if q not in st_ready_ve:
                            st_ready_ve[q] = cnt["s_sc"]
                    ready_st[0] = NST - 1
                    while pump_q or pump_cur[0] is not None:
                        pump(8)

                NTCR = ntc_run if ntc_run else NTC
                lnA = {}
                lnB = {}

                def emitLoad(tc):
                    par3 = tc % 3
                    wl = []
                    if tc >= 3:
                        pv, ps = ln_free[par3]
                        wl = [("s_ve", pv), ("s_sc", ps)]
                    lxsem = "s_lx%d" % (tc % 2)
                    lxv = dma("sy", lxsem, xln[:, par3], x_srcs[tc], wait=wl)
                    lnA[tc] = (lxsem, lxv)

                def emitA(tc):
                    par = tc % 2
                    par3 = tc % 3
                    lxsem, lxv = lnA[tc]
                    xl = xln[:, par3]
                    ssum = stat[:, par * 8 + 0: par * 8 + 1]
                    negm = stat[:, par * 8 + 1: par * 8 + 2]
                    bb = stat[:, par * 8 + 5: par * 8 + 6]
                    var = stat[:, par * 8 + 2: par * 8 + 3]
                    ve(lambda e, xl=xl, ssum=ssum: e.reduce_sum(ssum, xl, AX),
                       wait=[(lxsem, lxv)])
                    ve(lambda e, ssum=ssum, negm=negm:
                       e.tensor_scalar_mul(negm, ssum, -1.0 / DM))
                    bwl2 = [("s_sc", bb_free[par])] if tc >= 2 else []
                    vb = ve(lambda e, negm=negm, bb=bb:
                            e.scalar_tensor_tensor(bb, negm, -1.0, negm,
                                                   ALU.mult, ALU.mult),
                            wait=bwl2)
                    vb = ve(lambda e, bb=bb: e.tensor_scalar_add(bb, bb, EPS))
                    s2 = sc(lambda e, xl=xl, var=var:
                            e.activation(otm[:, 0], xl, ACT.Square, accum_out=var),
                            wait=[(lxsem, lxv)])
                    lnB[tc] = (vb, s2)

                def emitB(tc):
                    par = tc % 2
                    par3 = tc % 3
                    xl = xln[:, par3]
                    xn = xnb[:, par]
                    negm = stat[:, par * 8 + 1: par * 8 + 2]
                    var = stat[:, par * 8 + 2: par * 8 + 3]
                    sq_ = stat[:, par * 8 + 3: par * 8 + 4]
                    inv = stat[:, par * 8 + 4: par * 8 + 5]
                    bb = stat[:, par * 8 + 5: par * 8 + 6]
                    vb, s2 = lnB[tc]
                    s3 = sc(lambda e, var=var, sq_=sq_, bb=bb:
                            e.activation(sq_, var, ACT.Sqrt, bias=bb,
                                         scale=1.0 / DM),
                            wait=[("s_ve", vb)])
                    bb_free[par] = s3
                    vwl = [("s_sc", s3)]
                    if tc >= 2:
                        vwl.append(("s_sc", inv_free[par]))
                    v3 = ve(lambda e, inv=inv, sq_=sq_: e.reciprocal(inv, sq_),
                            wait=vwl)
                    xwl = [("s_ve", v3)]
                    if tc >= 2:
                        xwl.append(("s_te", xnb_free[par]))
                    v4 = ve(lambda e, xn=xn, xl=xl, negm=negm, inv=inv:
                            e.tensor_scalar(xn, xl, negm, inv, ALU.add, ALU.mult),
                            wait=xwl)
                    inv_free[par] = cnt["s_sc"]
                    ln_free[par3] = (v4, s2)
                    lnB[tc] = v4

                def emitC(tc):
                    par = tc % 2
                    xn = xnb[:, par]
                    v4 = lnB[tc]
                    bank = pbt[par]
                    tw = [("s_ve", v4)]
                    if tc < 2:
                        tw.append(("s_lc2", lc_eye))
                    else:
                        tw.append(("s_sc", tr_evac[par]))
                    tvals = []
                    for dc in range(NDC):
                        t_ = te(lambda e, bank=bank, dc=dc, xn=xn:
                                e.transpose(bank[:, dc * 128:(dc + 1) * 128],
                                            xn[:, dc * 128:(dc + 1) * 128], sI16[:]),
                                wait=(tw if dc == 0 else ()), inc=(dc in (3, 7)))
                        tvals.append(t_)
                    xnb_free[par] = tvals[7]
                    sc(lambda e, bank=bank, tc=tc:
                       e.copy(xn_fm[:, tc, :, :], bank[:]),
                       wait=[("s_te", tvals[7])])
                    tr_evac[par] = cnt["s_sc"]

                twid = [512, 512, 512, 512, TW - 2048]
                d_evac = {}
                su_h, ss_h = {}, {}

                def emit_usp(t5):
                    wt_ = twid[t5]
                    c0 = t5 * 512
                    tc0 = c0 // 128
                    ntc_ = wt_ // 128
                    bu_, bs_, bd_ = pb[0], pb[1], pb[2]
                    uwl = (([("s_sc", usp_xn[t5])] + lc_wait) if t5 == 0
                           else [("s_sc", usp_xn[t5]), ("s_ve", d_evac[t5 - 1])])
                    mu = None
                    for dc in range(NDC):
                        rhs = xn_fm[:, tc0:tc0 + ntc_, dc, :]
                        mu = te(lambda e, dc=dc, bu_=bu_, rhs=rhs, wt_=wt_:
                                e.matmul(bu_[:, 0:wt_], sWiT[:, dc, :], rhs,
                                         start=(dc == 0), stop=(dc == NDC - 1)),
                                wait=(uwl if dc == 0 else ()), inc=(dc == NDC - 1))
                    su_h[t5] = sc(lambda e, bu_=bu_, c0=c0, wt_=wt_:
                                  e.activation(u_bf[:, c0:c0 + wt_], bu_[:, 0:wt_],
                                               ACT.Identity, bias=sbi[:]),
                                  wait=[("s_te", mu)])
                    swl2 = [] if t5 < 1 else [("s_sc", ss_h[t5 - 1])]
                    ms = None
                    for dc in range(NDC):
                        rhs = xn_fm[:, tc0:tc0 + ntc_, dc, :]
                        ms = te(lambda e, dc=dc, bs_=bs_, rhs=rhs, wt_=wt_:
                                e.matmul(bs_[:, 0:wt_], sWsT[:, dc, :], rhs,
                                         start=(dc == 0), stop=(dc == NDC - 1)),
                                wait=(swl2 if dc == 0 else ()), inc=(dc == NDC - 1))
                    ss_h[t5] = sc(lambda e, bs_=bs_, c0=c0, wt_=wt_:
                                  e.activation(SPb[:, 1 + c0:1 + c0 + wt_], bs_[:, 0:wt_],
                                               ACT.Identity, bias=sbs[:]),
                                  wait=[("s_te", ms)])
                    vs = ss_h[t5]
                    dwl = [("s_sc", vs), ("s_ve", v_init)]
                    if t5 >= 1:
                        dwl.append(("s_ve", d_evac[t5 - 1]))
                    te(lambda e, bd_=bd_, c0=c0, wt_=wt_:
                       e.matmul(bd_[:, 0:wt_], sAT[:], SPb[:, c0:c0 + wt_],
                                start=True, stop=False, skip_group_check=True),
                       wait=dwl, inc=False)
                    md2 = te(lambda e, bd_=bd_, c0=c0, wt_=wt_:
                             e.matmul(bd_[:, 0:wt_], sBT[:], u_bf[:, c0:c0 + wt_],
                                      start=False, stop=True, skip_group_check=True),
                             wait=[("s_sc", su_h[t5])])
                    pump(2)
                    if t5 == 0:
                        ve(lambda e, bd_=bd_: e.tensor_mul(Dt[:, 0:W], bd_[:, 0:W], sdm[:]),
                           wait=[("s_te", md2)])
                        d_evac[t5] = ve(lambda e, bd_=bd_:
                                        e.tensor_copy(Dt[:, W:512], bd_[:, W:512]))
                    else:
                        d_evac[t5] = ve(lambda e, bd_=bd_, c0=c0, wt_=wt_:
                                        e.tensor_copy(Dt[:, c0:c0 + wt_], bd_[:, 0:wt_]),
                                        wait=[("s_te", md2)])

                inv_free = {}
                bb_free = {}
                usp_xn = {}
                for col in range(NTCR + 3):
                    if col < NTCR:
                        emitLoad(col)
                        emitA(col)
                    if 1 <= col <= NTCR:
                        emitB(col - 1)
                    if 2 <= col < NTCR + 2:
                        emitC(col - 2)
                    if not ntc_run:
                        t5r = (col - 6) // 4
                        if 0 <= t5r <= 3 and (col - 6) % 4 == 0:
                            usp_xn[t5r] = cnt["s_sc"]
                            emit_usp(t5r)
                        if col == NTCR + 2:
                            usp_xn[4] = cnt["s_sc"]
                            emit_usp(4)
                ve_xn_all = cnt["s_ve"]

                if PH < 0: raise _Done()
                ve_xn_all0 = cnt["s_ve"]
                sc_xn_all0 = cnt["s_sc"]
                r_mm = {}
                r_s6 = {}
                r_t6 = {}
                r_v7 = {}

                def emitR1(mc):
                    rpar = mc % 2
                    tc = mc + WC
                    lbank = pb[4]
                    lg = lgb[:, rpar]
                    mmv = None
                    for dc in range(NDC):
                        lwl = []
                        if dc == 0:
                            if mc == 0:
                                lwl.append(("s_ve", ve_xn_all0))
                                lwl.append(("s_sc", sc_xn_all0))
                                lwl.extend(lc_wait)
                            else:
                                lwl.append(("s_sc", logit_evac[(mc - 1) % 2]))
                        mmv = te(lambda e, dc=dc, lbank=lbank, tc=tc:
                                 e.matmul(lbank[:, 0:8], xn_fm[:, tc, dc, :],
                                          sWr[:, dc, :],
                                          start=(dc == 0), stop=(dc == NDC - 1)),
                                 wait=lwl, inc=(dc == NDC - 1))
                    ewl = [("s_te", mmv)]
                    if mc >= 2:
                        ewl.append(("s_ve", r_v7[rpar]))
                    logit_evac[rpar] = sc(lambda e, lg=lg, lbank=lbank:
                                          e.copy(lg, lbank[:, 0:8]), wait=ewl)
                    r_mm[mc] = logit_evac[rpar]

                def emitR2(mc):
                    rpar = mc % 2
                    lg = lgb[:, rpar]
                    m1 = rt[:, rpar, 0:1]; nm1 = rt[:, rpar, 1:2]
                    m2 = rt[:, rpar, 2:3]; ssum2 = rt[:, rpar, 3:4]
                    sinv = rt[:, rpar, 4:5]; eq = rt[:, rpar, 5:13]
                    vwl = [("s_sc", r_mm[mc])]
                    if mc >= 2:
                        vwl.append(("s_sc", r_s6[rpar]))
                    v4 = ve(lambda e, m1=m1, lg=lg: e.reduce_max(m1, lg, AX),
                            wait=vwl)
                    ve(lambda e, nm1=nm1, m1=m1: e.tensor_scalar_mul(nm1, m1, -1.0),
                       wait=[("s_ve", v4)])
                    ve(lambda e, eq=eq, lg=lg, m1=m1:
                       e.tensor_scalar(eq, lg, m1, None, ALU.is_equal))
                    ve(lambda e, eq=eq, lg=lg:
                       e.scalar_tensor_tensor(eq, eq, -1e30, lg, ALU.mult, ALU.add))
                    vm2 = ve(lambda e, m2=m2, eq=eq: e.reduce_max(m2, eq, AX))
                    v6 = ve(lambda e, eq=eq, lg=lg, m2=m2:
                            e.tensor_scalar(eq, lg, m2, None, ALU.is_ge),
                            wait=[("s_ve", vm2)])
                    s5 = sc(lambda e, lg=lg, nm1=nm1:
                            e.activation(lg, lg, ACT.Exp, bias=nm1),
                            wait=[("s_ve", v6)])
                    v7 = ve(lambda e, lg=lg, eq=eq: e.tensor_mul(lg, eq, lg),
                            wait=[("s_sc", s5)])
                    v8 = ve(lambda e, ssum2=ssum2, lg=lg: e.reduce_sum(ssum2, lg, AX))
                    v9 = ve(lambda e, sinv=sinv, ssum2=ssum2: e.reciprocal(sinv, ssum2),
                            wait=[("s_ve", v8)])
                    wt = wtm[:, rpar]
                    swl = [("s_ve", v9)]
                    if mc >= 2:
                        swl.append(("s_te", r_t6[rpar]))
                    r_s6[rpar] = sc(lambda e, wt=wt, lg=lg, sinv=sinv:
                                    e.activation(wt, lg, ACT.Identity, scale=sinv),
                                    wait=swl)
                    r_v7[rpar] = v9

                def emitR3(mc):
                    rpar = mc % 2
                    wt = wtm[:, rpar]
                    tbank = pb[5]
                    t6w = [("s_sc", r_s6[rpar])]
                    if mc >= 1:
                        t6w.append(("s_sc", router_evac[(mc - 1) % 2]))
                    t6 = te(lambda e, tbank=tbank, wt=wt:
                            e.transpose(tbank[0:8, 0:128], wt, sI32[:]), wait=t6w)
                    r_t6[rpar] = t6
                    router_evac[rpar] = sc(
                        lambda e, mc=mc, tbank=tbank:
                        e.copy(w_fmb[0:8, mc * 128:(mc + 1) * 128], tbank[0:8, 0:128]),
                        wait=[("s_te", t6)])

                for q in range(NST):
                    if q not in st_ready_ve:
                        st_ready_ve[q] = cnt["s_sc"]
                ready_st[0] = NST - 1
                for col in range(NMC + 2):
                    if col >= 2:
                        emitR3(col - 2)
                    if 1 <= col <= NMC:
                        emitR2(col - 1)
                    if col < NMC:
                        emitR1(col)

                ve_xn_all = cnt["s_ve"]
                sc_r_all = cnt["s_sc"]

                bc0_ev = {}
                for stt in range(NST):
                    bk = pb[4 + stt % 2]
                    bwl = (([("s_sc", sc_r_all)] + (lc_wait if stt == 0 else []))
                           if stt < 2 else [("s_sc", bc0_ev[stt - 2])])
                    m_ = te(lambda e, bk=bk, stt=stt:
                            e.matmul(bk[:, 0:STW], ssel[:, 0:128],
                                     w_fmb[:, stt * STW:(stt + 1) * STW],
                                     start=True, stop=True),
                            wait=bwl)
                    bc0_ev[stt] = sc(
                        lambda e, bk=bk, stt=stt:
                        e.copy(wbc[:, 0, stt * STW:(stt + 1) * STW], bk[:, 0:STW]),
                        wait=[("s_te", m_)])
                    pump(2)
                bc0_done = cnt["s_sc"]


                ve_d_all = cnt["s_ve"]

                if PH < 2: raise _Done()
                ch_act = {}
                for s in range(NSTEP):
                    pump(2)
                    b = s % 2
                    if b == 0:
                        for k in range(2):
                            sk = s + k
                            if sk >= NSTEP:
                                break
                            wl = []
                            if k == 0:
                                wl = [("s_ve", ve_d_all),
                                      ("s_sc", ch_act[s - 1] if s > 0 else cnt["s_sc"])]
                            te(lambda e, sk=sk, bk=pb[4 + sk % 2]:
                               e.matmul(bk[:, 0:NCH], sI16[:], Dt[:, sk:sk + SSTR:LCH],
                                        start=True, stop=False, skip_group_check=True),
                               wait=wl, inc=False)
                    if s == 0:
                        rhs = gz[:]
                    elif s <= W:
                        rhs = gw[:, ((s - 1) % 2) * NCH:((s - 1) % 2) * NCH + NCH]
                    else:
                        t0 = s - 1 - W
                        rhs = G[:, t0:t0 + SSTR:LCH]
                    wl = [("s_sc", ch_act[s - 1])] if s > 0 else []
                    mv = te(lambda e, b=b, rhs=rhs:
                            e.matmul(pb[4 + b][:, 0:NCH], sAT[:], rhs,
                                     start=False, stop=True, skip_group_check=True),
                            wait=wl)
                    if s < W:
                        dst = gw[:, (s % 2) * NCH:(s % 2) * NCH + NCH]
                    else:
                        t0 = s - W
                        dst = G[:, t0:t0 + SSTR:LCH]
                    ch_act[s] = sc(lambda e, b=b, dst=dst:
                                   e.activation(dst, pb[4 + b][:, 0:NCH], ACT.Tanh),
                                   wait=[("s_te", mv)])
                pump_all()
                sc_chain = cnt["s_sc"]
                for stt in range(NST):
                    for fc in range(NFC):
                        fwl = [("s_sc", bc0_done)] if (stt == 0 and fc == 0) else []
                        ew_done[(0, fc, stt)] = ve(
                            lambda e, fc=fc, stt=stt:
                            e.tensor_mul(act_w[:, fc, stt * STW:(stt + 1) * STW],
                                         act_w[:, fc, stt * STW:(stt + 1) * STW],
                                         wbc[:, 0, stt * STW:(stt + 1) * STW]),
                            wait=fwl)
                wbc_cons0 = cnt["s_ve"]

                if PH < 3: raise _Done()
                vstate = {}
                for stt in range(NST):
                    c0 = stt * STW
                    vstate[stt] = ve(lambda e, c0=c0:
                                     e.tensor_add(u_bf[:, c0:c0 + STW], G[:, c0:c0 + STW],
                                                  SPb[:, 1 + W + c0:1 + W + c0 + STW]),
                                     wait=([("s_sc", sc_chain)] if stt == 0 else ()))
                ys_evac = {}
                for stt in range(NST):
                    bk = pb[4 + stt % 2]
                    ywl = [("s_ve", vstate[stt])]
                    ywl.append(("s_sc", sc_chain if stt < 2 else ys_evac[stt - 2]))
                    m_ = te(lambda e, bk=bk, stt=stt:
                            e.matmul(bk[:, 0:STW], sCT[:], u_bf[:, stt * STW:(stt + 1) * STW],
                                     start=True, stop=True),
                            wait=ywl)
                    ys_evac[stt] = sc(lambda e, bk=bk, stt=stt:
                                      e.copy(ys_bf[:, stt * STW:(stt + 1) * STW], bk[:, 0:STW]),
                                      wait=[("s_te", m_)])
                sc_ys = cnt["s_sc"]

                attn_ev = {}
                for hc in range(NDC):
                    for stt in range(NST):
                        idx = hc * NST + stt
                        bk = pb[4 + idx % 2]
                        wl = [("s_sc", sc_ys if idx < 2 else attn_ev[idx - 2])]
                        te(lambda e, bk=bk, hc=hc, stt=stt:
                           e.matmul(bk[:, 0:STW], sWoT[:, hc, :],
                                    ys_bf[:, stt * STW:(stt + 1) * STW],
                                    start=True, stop=False, skip_group_check=True),
                           wait=wl, inc=False)
                        m_ = te(lambda e, bk=bk, hc=hc, stt=stt:
                                e.matmul(bk[:, 0:STW], sdb[:, hc * 128:(hc + 1) * 128],
                                         w_fmb[:, stt * STW:(stt + 1) * STW],
                                         start=False, stop=True, skip_group_check=True))
                        attn_ev[idx] = sc(
                            lambda e, bk=bk, hc=hc, stt=stt:
                            e.activation(acc[:, hc, stt * STW:(stt + 1) * STW],
                                         bk[:, 0:STW], ACT.Identity, bias=sbo[:, hc:hc + 1]),
                            wait=[("s_te", m_)])
                sc_attn = cnt["s_sc"]

                fin_add = {}
                out_done = {}

                fxload = {}

                def emit_final_load(mc):
                    par4 = mc % 4
                    fxsem = "s_lf%d" % (mc % 2)
                    xwl = [("s_ve", fin_add[par4])] if mc >= 4 else []
                    fxload[mc] = (fxsem,
                                  dma("gp", fxsem, xres[:, par4],
                                      d_xm[mc * 128:(mc + 1) * 128, :], wait=xwl))

                def emit_final_group(stt):
                    if stt == 0:
                        for mc_ in range(4):
                            emit_final_load(mc_)
                    for mc in range(stt * 4, stt * 4 + 4):
                        par = mc % 2
                        par4 = mc % 4
                        fosem = "s_ou%d" % par
                        fxsem, lxv = fxload[mc]
                        bank = pbt[par]
                        tvals = []
                        for hc in range(NDC):
                            wl2 = []
                            if hc == 0:
                                wl2 = [("s_ve", acc_add[(NDC - 1, stt)])]
                                wl2.append(("s_ve", fin_add.get(par, cnt["s_ve"])
                                            if mc >= 2 else cnt["s_ve"]))
                            t_ = te(lambda e, bank=bank, hc=hc, mc=mc:
                                    e.transpose(bank[:, hc * 128:(hc + 1) * 128],
                                                acc[:, hc, mc * 128:(mc + 1) * 128], sI16[:]),
                                    wait=wl2, inc=(hc in (3, 7)))
                            tvals.append(t_)
                        vwl = [("s_te", tvals[7]), (fxsem, lxv)]
                        if mc >= 2:
                            vwl.append((fosem, out_done[par]))
                        ve(lambda e, bank=bank, par=par, par4=par4:
                           e.tensor_add(otm[:, par, :], bank[:], xres[:, par4, :]),
                           wait=vwl)
                        fin_add[par] = cnt["s_ve"]
                        fin_add[par4] = cnt["s_ve"]
                        if mc + 4 < NMC:
                            emit_final_load(mc + 4)
                        out_done[par] = dma("sy", fosem, d_y[mc * 128:(mc + 1) * 128, :],
                                            otm[:, par], wait=[("s_ve", fin_add[par])])


                if PH < 4: raise _Done()
                acc_add = {}
                wbc_cons = {0: wbc_cons0}
                dn_read = {}
                for ei in range(NEXP):
                    epar = ei % 2
                    if ei == 0:
                        bc_ev_l = bc0_ev
                    else:
                     bc_ev_l = {}
                     for stt in range(NST):
                        bk = pb[4 + stt % 2]
                        bwl = []
                        if stt < 2:
                            if ei == 0:
                                bwl.append(("s_sc", sc_attn))
                            else:
                                bwl.append(("s_ve", dn_read[stt % 2]))
                            if ei >= 2:
                                bwl.append(("s_ve", wbc_cons[epar]))
                        else:
                            bwl.append(("s_sc", bc_ev_l[stt - 2]))
                        m_ = te(lambda e, bk=bk, ei=ei, stt=stt:
                                e.matmul(bk[:, 0:STW], ssel[:, ei * 128:(ei + 1) * 128],
                                         w_fmb[:, stt * STW:(stt + 1) * STW],
                                         start=True, stop=True),
                                wait=bwl)
                        bc_ev_l[stt] = sc(
                            lambda e, bk=bk, epar=epar, stt=stt:
                            e.copy(wbc[:, epar, stt * STW:(stt + 1) * STW], bk[:, 0:STW]),
                            wait=[("s_te", m_)])

                    if ei > 0:
                        for fc in range(NFC):
                            for stt in range(NST):
                                emit_gu_tile(ei, fc, stt, weighted=True,
                                             bc_ev_l=bc_ev_l)
                        wbc_cons[epar] = cnt["s_ve"]

                    for fc in range(NFC):
                        par = (ei * NFC + fc) % 2
                        ldsem, lwd = wload[("d", ei, fc)]
                        for stt in range(NST):
                            idx = ei * NFC * NST + fc * NST + stt
                            dp = (fc * NST + stt) % 2
                            bk = pb[4 + dp]
                            wl = [(ldsem, lwd), ("s_ve", ew_done[(ei, NFC - 1, stt)])]
                            if ei == 0 and fc == 0 and stt < 2:
                                wl.append(("s_sc", sc_attn))
                            elif fc == 0 and stt < 2:
                                wl.append(("s_sc", bc_ev_l[stt + 2]))
                            elif idx >= 2:
                                wl.append(("s_ve", dn_read[dp]))
                            md = None
                            for ic in range(NDC):
                                md = te(lambda e, ic=ic, bk=bk, par=par, stt=stt:
                                        e.matmul(bk[:, 0:STW], wdb_[:, par, ic, :],
                                                 act_w[:, ic, stt * STW:(stt + 1) * STW],
                                                 start=(ic == 0), stop=(ic == NDC - 1),
                                                 skip_group_check=True),
                                        wait=(wl if ic == 0 else ()),
                                        inc=(ic == NDC - 1))
                            if stt == NST - 1:
                                wd_cons[par] = cnt["s_te"]
                                nxt = ei * NFC + fc + 2
                                if nxt < NEXP * NFC:
                                    ne, nf = divmod(nxt, NFC)
                                    sched_wdma("d", ne, nf)
                            awl = [("s_te", md)]
                            if ei == 0:
                                awl.append(("s_sc", attn_ev[fc * NST + stt]))
                            va = ve(lambda e, fc=fc, stt=stt, bk=bk:
                                    e.tensor_add(acc[:, fc, stt * STW:(stt + 1) * STW],
                                                 acc[:, fc, stt * STW:(stt + 1) * STW],
                                                 bk[:, 0:STW]),
                                    wait=awl)
                            dn_read[dp] = va
                            acc_add[(fc, stt)] = va
                            if PH >= 5 and ei == NEXP - 1 and fc == NFC - 1:
                                emit_final_group(stt)

            except _Done:
                pass

            if debug:
                vw_, sw_, tw_ = cnt["s_ve"], cnt["s_sc"], cnt["s_te"]
                def gdbg(name, src_ap):
                    dma("gp", "s_ou0", dbg[name], src_ap,
                        wait=[("s_ve", vw_), ("s_sc", sw_), ("s_te", tw_)])
                gdbg("xnfm", xn_fm[:].rearrange("p a b c -> p (a b c)"))
                gdbg("wfm", w_fmb[:])
                if PH >= 1:
                    gdbg("D", Dt[:]); gdbg("u", u_bf[:]); gdbg("sp", SPb[:, 1:])
                if PH >= 2:
                    gdbg("G", G[:])
                if PH >= 3:
                    gdbg("ys", ys_bf[:])
                    gdbg("acc", acc[:].rearrange("p a b -> p (a b)"))

            fin_o0 = cnt.get("s_ou0", 0)
            fin_o1 = cnt.get("s_ou1", 0)
            def fin(e):
                if fin_o0:
                    e.wait_ge(s_ou0, fin_o0)
                if fin_o1:
                    e.wait_ge(s_ou1, fin_o1)
            ops["sy"].append(fin)

            @block.gpsimd
            def _(g):
                for f in ops["gp"]:
                    f(g)

            @block.sync
            def _(s_):
                for f in ops["sy"]:
                    f(s_)

            @block.tensor
            def _(t_):
                for f in ops["te"]:
                    f(t_)

            @block.vector
            def _(v_):
                for f in ops["ve"]:
                    f(v_)

            @block.scalar
            def _(sc_):
                for f in ops["sc"]:
                    f(sc_)
    return nc


def prep_inputs(L0_attn_x, L0_attn_Wi, L0_attn_bi, L0_attn_Ws, L0_attn_bs,
                L0_attn_Wo, L0_attn_bo, L0_attn_A, L0_attn_B, L0_attn_C,
                L0_ffn_router_weight, L0_ffn_gate_up_proj, L0_ffn_gate_up_proj_bias,
                L0_ffn_down_proj, L0_ffn_down_proj_bias):
    f = lambda a: np.ascontiguousarray(np.asarray(a), dtype=np.float32)
    x = f(L0_attn_x).reshape(16384, DM)
    Wi, Ws, Wo = f(L0_attn_Wi), f(L0_attn_Ws), f(L0_attn_Wo)
    A, B, C = f(L0_attn_A), f(L0_attn_B), f(L0_attn_C)
    bi, bs, bo = f(L0_attn_bi), f(L0_attn_bs), f(L0_attn_bo)
    Wr = f(L0_ffn_router_weight)
    gup, gub = f(L0_ffn_gate_up_proj), f(L0_ffn_gate_up_proj_bias)
    dwn, dwb = f(L0_ffn_down_proj), f(L0_ffn_down_proj_bias)

    com = {}
    com["eye"] = np.eye(128, dtype=np.float32)
    com["AT"] = np.ascontiguousarray(A.T)
    com["BT"] = np.ascontiguousarray(B.T)
    com["CT"] = np.ascontiguousarray(C.T)
    com["WiT"] = np.ascontiguousarray(Wi.T.reshape(NDC, 128, 128).transpose(1, 0, 2))
    com["WsT"] = np.ascontiguousarray(Ws.T.reshape(NDC, 128, 128).transpose(1, 0, 2))
    com["WoT"] = np.ascontiguousarray(Wo.T.reshape(128, NDC, 128))
    com["Wr"] = np.ascontiguousarray(Wr.reshape(NDC, 128, 8).transpose(1, 0, 2))
    com["bi"] = bi.reshape(128, 1)
    com["bs"] = bs.reshape(128, 1)
    com["bo"] = np.ascontiguousarray(bo.reshape(NDC, 128).T)
    gate = np.ascontiguousarray(gup[:, :, 0::2])
    up = np.ascontiguousarray(gup[:, :, 1::2])
    com["Wg"] = np.ascontiguousarray(
        gate.reshape(NEXP, NDC, 128, NFC, 128).transpose(0, 3, 2, 1, 4))
    com["Wu"] = np.ascontiguousarray(
        up.reshape(NEXP, NDC, 128, NFC, 128).transpose(0, 3, 2, 1, 4))
    com["Wd"] = np.ascontiguousarray(
        dwn.reshape(NEXP, NDC, 128, NFC, 128).transpose(0, 3, 2, 1, 4))
    gb = gub[:, 0::2].reshape(NEXP, NFC, 128)
    ub = gub[:, 1::2].reshape(NEXP, NFC, 128)
    com["bg"] = np.ascontiguousarray(gb.reshape(NEXP * NFC, 128).T)
    com["bu"] = np.ascontiguousarray(ub.reshape(NEXP * NFC, 128).T)
    com["dbr"] = np.ascontiguousarray(dwb)
    com["on1"] = np.ones((1, 128), np.float32)
    sel = np.zeros((NEXP, NEXP * 128), np.float32)
    for e_ in range(NEXP):
        sel[e_, e_ * 128:(e_ + 1) * 128] = 1.0
    com["sel"] = sel

    in_maps = []
    for c in range(8):
        m = dict(com)
        r0 = c * NTOK
        m["xm"] = np.ascontiguousarray(x[r0:r0 + NTOK])
        if c % 4 == 0:
            m["xw"] = np.zeros((W, DM), np.float32)
            m["dmask"] = np.zeros((128, W), np.float32)
        else:
            m["xw"] = np.ascontiguousarray(x[r0 - W:r0])
            m["dmask"] = np.ones((128, W), np.float32)
        in_maps.append(m)
    return in_maps


_cache = {}


def kernel(**inputs):
    in_maps = prep_inputs(**inputs)
    if "nc" not in _cache:
        _cache["nc"] = build_nc(debug=False)
    res = run_bass_kernel_spmd(_cache["nc"], in_maps, core_ids=list(range(8)))
    out = np.concatenate([np.asarray(res.results[c]["y"]) for c in range(8)], axis=0)
    return out.reshape(2, 8192, DM)


# version 21
# speedup vs baseline: 1.0374x; 1.0225x over previous
import numpy as np
import concourse.bass as bass
import concourse.mybir as mybir
from concourse.bass_utils import run_bass_kernel_spmd

F32 = mybir.dt.float32
BF16 = mybir.dt.bfloat16
AX = mybir.AxisListType.X
ALU = mybir.AluOpType
ACT = mybir.ActivationFunctionType

DM = 1024
DS = 128
NTOK = 2048
W = 128
LCH = 32
NCH = NTOK // LCH
WCH = 96
WOFF = W - WCH
NSTEP = WCH + LCH
TW = W + NTOK
NDC = DM // 128
NTC = TW // 128
WC = W // 128
NMC = NTOK // 128
NEXP = 8
NFC = 8
NST = 4
STW = 512
EPS = 1e-5
ALPHA = 1.702
LIMIT = 7.0
SSTR = LCH * (NCH - 1) + 1


def build_nc(debug=False, upto="full", ntc_run=None, skip_big=False):
    import os as _os
    LNL = int(_os.environ.get("MK_LN_LEVEL", "9"))
    PH = {"ln0": -1, "ln": 0, "usp": 1, "chain": 2, "attn": 3, "ffn": 4, "full": 5}[upto]
    nc = bass.Bass("TRN2", target_bir_lowering=False)

    d_xm = nc.dram_tensor("xm", [NTOK, DM], F32, kind="ExternalInput").ap()
    d_xw = nc.dram_tensor("xw", [W, DM], F32, kind="ExternalInput").ap()
    d_dmask = nc.dram_tensor("dmask", [128, W], F32, kind="ExternalInput").ap()
    d_eye = nc.dram_tensor("eye", [128, 128], F32, kind="ExternalInput").ap()
    d_AT = nc.dram_tensor("AT", [128, 128], F32, kind="ExternalInput").ap()
    d_BT = nc.dram_tensor("BT", [128, 128], F32, kind="ExternalInput").ap()
    d_CT = nc.dram_tensor("CT", [128, 128], F32, kind="ExternalInput").ap()
    d_WiT = nc.dram_tensor("WiT", [128, NDC, 128], F32, kind="ExternalInput").ap()
    d_WsT = nc.dram_tensor("WsT", [128, NDC, 128], F32, kind="ExternalInput").ap()
    d_WoT = nc.dram_tensor("WoT", [128, NDC, 128], F32, kind="ExternalInput").ap()
    d_Wr = nc.dram_tensor("Wr", [128, NDC, 8], F32, kind="ExternalInput").ap()
    d_bi = nc.dram_tensor("bi", [128, 1], F32, kind="ExternalInput").ap()
    d_bs = nc.dram_tensor("bs", [128, 1], F32, kind="ExternalInput").ap()
    d_bo = nc.dram_tensor("bo", [128, 8], F32, kind="ExternalInput").ap()
    d_bg = nc.dram_tensor("bg", [128, NEXP * NFC], F32, kind="ExternalInput").ap()
    d_bu = nc.dram_tensor("bu", [128, NEXP * NFC], F32, kind="ExternalInput").ap()
    d_dbr = nc.dram_tensor("dbr", [NEXP, DM], F32, kind="ExternalInput").ap()
    d_on1 = nc.dram_tensor("on1", [1, 128], F32, kind="ExternalInput").ap()
    d_sel = nc.dram_tensor("sel", [NEXP, NEXP * 128], F32, kind="ExternalInput").ap()
    if not skip_big:
        d_Wg = nc.dram_tensor("Wg", [NEXP, NFC, 128, NDC, 128], F32, kind="ExternalInput").ap()
        d_Wu = nc.dram_tensor("Wu", [NEXP, NFC, 128, NDC, 128], F32, kind="ExternalInput").ap()
        d_Wd = nc.dram_tensor("Wd", [NEXP, NFC, 128, NDC, 128], F32, kind="ExternalInput").ap()
    d_y = nc.dram_tensor("y", [NTOK, DM], F32, kind="ExternalOutput").ap()
    dbg = {}
    if debug:
        dbg["xnfm"] = nc.dram_tensor("dbg_xnfm", [128, NTC * DM], F32, kind="ExternalOutput").ap()
        dbg["D"] = nc.dram_tensor("dbg_D", [128, TW], F32, kind="ExternalOutput").ap()
        dbg["G"] = nc.dram_tensor("dbg_G", [128, NTOK], F32, kind="ExternalOutput").ap()
        dbg["wfm"] = nc.dram_tensor("dbg_wfm", [8, NTOK], F32, kind="ExternalOutput").ap()
        dbg["ys"] = nc.dram_tensor("dbg_ys", [128, NTOK], F32, kind="ExternalOutput").ap()
        dbg["acc"] = nc.dram_tensor("dbg_acc", [128, NDC * NTOK], F32, kind="ExternalOutput").ap()
        dbg["u"] = nc.dram_tensor("dbg_u", [128, TW], F32, kind="ExternalOutput").ap()
        dbg["sp"] = nc.dram_tensor("dbg_sp", [128, TW], F32, kind="ExternalOutput").ap()

    ops = {k: [] for k in ("te", "ve", "sc", "gp", "sy")}
    cnt = {}

    def _inc(sem, n=1):
        cnt[sem] = cnt.get(sem, 0) + n
        return cnt[sem]

    from contextlib import ExitStack
    with ExitStack() as _es:
        xn_fm = _es.enter_context(nc.sbuf_tensor("xn_fm", [128, NTC, NDC, 128], BF16))
        SPb = _es.enter_context(nc.sbuf_tensor("SPb", [128, 1 + TW], BF16))
        u_bf = _es.enter_context(nc.sbuf_tensor("u_bf", [128, TW], BF16))
        Dt = _es.enter_context(nc.sbuf_tensor("Dt", [128, TW], BF16))
        G = _es.enter_context(nc.sbuf_tensor("G", [128, NTOK], BF16))
        gw = _es.enter_context(nc.sbuf_tensor("gw", [128, 2 * NCH], BF16))
        gz = _es.enter_context(nc.sbuf_tensor("gz", [128, NCH], BF16))
        ys_bf = _es.enter_context(nc.sbuf_tensor("ys_bf", [128, NTOK], BF16))
        w_fmb = _es.enter_context(nc.sbuf_tensor("w_fmb", [8, NTOK], BF16))
        acc = _es.enter_context(nc.sbuf_tensor("acc", [128, NDC, NTOK], BF16))
        act_w = _es.enter_context(nc.sbuf_tensor("act_w", [128, NFC, NTOK], BF16))
        wbc = _es.enter_context(nc.sbuf_tensor("wbc", [128, 2, NTOK], BF16))
        sAT = _es.enter_context(nc.sbuf_tensor("sAT", [128, 128], BF16))
        sBT = _es.enter_context(nc.sbuf_tensor("sBT", [128, 128], BF16))
        sCT = _es.enter_context(nc.sbuf_tensor("sCT", [128, 128], BF16))
        sI16 = _es.enter_context(nc.sbuf_tensor("sI16", [128, 128], BF16))
        sI32 = _es.enter_context(nc.sbuf_tensor("sI32", [128, 128], F32))
        sWiT = _es.enter_context(nc.sbuf_tensor("sWiT", [128, NDC, 128], BF16))
        sWsT = _es.enter_context(nc.sbuf_tensor("sWsT", [128, NDC, 128], BF16))
        sWoT = _es.enter_context(nc.sbuf_tensor("sWoT", [128, NDC, 128], BF16))
        sWr = _es.enter_context(nc.sbuf_tensor("sWr", [128, NDC, 8], BF16))
        sbi = _es.enter_context(nc.sbuf_tensor("sbi", [128, 1], F32))
        sbs = _es.enter_context(nc.sbuf_tensor("sbs", [128, 1], F32))
        sbo = _es.enter_context(nc.sbuf_tensor("sbo", [128, 8], F32))
        sbg = _es.enter_context(nc.sbuf_tensor("sbg", [128, NEXP * NFC], F32))
        sbu = _es.enter_context(nc.sbuf_tensor("sbu", [128, NEXP * NFC], F32))
        sdb = _es.enter_context(nc.sbuf_tensor("sdb", [NEXP, DM], BF16))
        son1 = _es.enter_context(nc.sbuf_tensor("son1", [1, 128], BF16))
        ssel = _es.enter_context(nc.sbuf_tensor("ssel", [NEXP, NEXP * 128], BF16))
        sdm = _es.enter_context(nc.sbuf_tensor("sdm", [128, W], F32))
        xln = _es.enter_context(nc.sbuf_tensor("xln", [128, 3, DM], F32))
        xnb = _es.enter_context(nc.sbuf_tensor("xnb", [128, 2, DM], BF16))
        stat = _es.enter_context(nc.sbuf_tensor("stat", [128, 16], F32))
        lgb = _es.enter_context(nc.sbuf_tensor("lgb", [128, 2, 8], F32))
        rt = _es.enter_context(nc.sbuf_tensor("rt", [128, 2, 16], F32))
        wtm = _es.enter_context(nc.sbuf_tensor("wtm", [128, 2, 8], F32))
        wgu = _es.enter_context(nc.sbuf_tensor("wgu", [128, 4, NDC, 128], BF16))
        wdb_ = _es.enter_context(nc.sbuf_tensor("wdb_", [128, 2, NFC, 128], BF16))
        et2 = _es.enter_context(nc.sbuf_tensor("et2", [128, 2, STW], BF16))
        et2w = _es.enter_context(nc.sbuf_tensor("et2w", [128, 2, STW], BF16))
        etg = _es.enter_context(nc.sbuf_tensor("etg", [128, 2, STW], BF16))
        etu = _es.enter_context(nc.sbuf_tensor("etu", [128, 2, STW], BF16))
        xres = _es.enter_context(nc.sbuf_tensor("xres", [128, 4, DM], BF16))
        otm = _es.enter_context(nc.sbuf_tensor("otm", [128, 2, DM], F32))
        pb0 = _es.enter_context(nc.psum_tensor([128, 512], F32))
        pb1 = _es.enter_context(nc.psum_tensor([128, 512], F32))
        pb2 = _es.enter_context(nc.psum_tensor([128, 512], F32))
        pb3 = _es.enter_context(nc.psum_tensor([128, 512], F32))
        pb4 = _es.enter_context(nc.psum_tensor([128, 512], F32))
        pb5 = _es.enter_context(nc.psum_tensor([128, 512], F32))
        pbtA = _es.enter_context(nc.psum_tensor([128, 1024], BF16))
        pbtB = _es.enter_context(nc.psum_tensor([128, 1024], BF16))
        s_lc = _es.enter_context(nc.semaphore("s_lc"))
        s_lc2 = _es.enter_context(nc.semaphore("s_lc2"))
        s_lc3 = _es.enter_context(nc.semaphore("s_lc3"))
        s_lw0 = _es.enter_context(nc.semaphore("s_lw0"))
        s_lw1 = _es.enter_context(nc.semaphore("s_lw1"))
        s_ld0 = _es.enter_context(nc.semaphore("s_ld0"))
        s_ld1 = _es.enter_context(nc.semaphore("s_ld1"))
        s_lx0 = _es.enter_context(nc.semaphore("s_lx0"))
        s_lx1 = _es.enter_context(nc.semaphore("s_lx1"))
        s_lf0 = _es.enter_context(nc.semaphore("s_lf0"))
        s_lf1 = _es.enter_context(nc.semaphore("s_lf1"))
        s_te = _es.enter_context(nc.semaphore("s_te"))
        s_ve = _es.enter_context(nc.semaphore("s_ve"))
        s_sc = _es.enter_context(nc.semaphore("s_sc"))
        s_ou0 = _es.enter_context(nc.semaphore("s_ou0"))
        s_ou1 = _es.enter_context(nc.semaphore("s_ou1"))
        s_gp = _es.enter_context(nc.semaphore("s_gp"))
        block = _es.enter_context(nc.Block())
        if True:
            pb = [pb0, pb1, pb2, pb3, pb4, pb5]
            pbt = [pbtA, pbtB]
            sems = {"s_lc": s_lc, "s_lc2": s_lc2, "s_lc3": s_lc3, "s_lw0": s_lw0, "s_lw1": s_lw1,
                    "s_ld0": s_ld0, "s_ld1": s_ld1,
                    "s_lx0": s_lx0, "s_lx1": s_lx1, "s_lf0": s_lf0, "s_lf1": s_lf1, "s_te": s_te,
                    "s_ve": s_ve, "s_sc": s_sc, "s_ou0": s_ou0, "s_ou1": s_ou1,
                    "s_gp": s_gp}

            def mk(eng_key, sem_name, self_wait=False):
                def emitfn(fn, wait=(), inc=True):
                    for sn, val in wait:
                        assert val <= cnt.get(sn, 0), (sn, val, cnt.get(sn, 0), eng_key)
                    w = tuple(wait)
                    if self_wait and cnt.get(sem_name, 0) > 0:
                        w = ((sem_name, cnt[sem_name]),) + w
                    v = _inc(sem_name) if inc else None
                    def op(e, fn=fn, w=w, v=v, sem_name=sem_name):
                        for sn, val in w:
                            e.wait_ge(sems[sn], val)
                        ins = fn(e)
                        if v is not None:
                            ins.then_inc(sems[sem_name], 1)
                    ops[eng_key].append(op)
                    return cnt.get(sem_name, 0)
                return emitfn

            te = mk("te", "s_te")
            ve = mk("ve", "s_ve", self_wait=True)
            sc = mk("sc", "s_sc", self_wait=True)
            gpc = mk("gp", "s_gp", self_wait=True)

            class _Done(Exception):
                pass

            def dma(eng_key, sem_name, dst, src, wait=()):
                for sn, val in wait:
                    assert val <= cnt.get(sn, 0), (sn, val, cnt.get(sn, 0), "dma")
                w = tuple(wait)
                if cnt.get(sem_name, 0) > 0:
                    w = ((sem_name, cnt[sem_name]),) + w
                _inc(sem_name, 16)
                def op(e, dst=dst, src=src, w=w, sem_name=sem_name):
                    for sn, val in w:
                        e.wait_ge(sems[sn], val)
                    e.dma_start(dst, src).then_inc(sems[sem_name], 16)
                ops[eng_key].append(op)
                return cnt[sem_name]

            try:
                dma("gp", "s_lc2", sI16[:], d_eye)
                lc_eye = dma("gp", "s_lc2", sI32[:], d_eye)
                for j, (dst, csrc) in enumerate(
                        [(sAT[:], d_AT), (sBT[:], d_BT), (sCT[:], d_CT),
                         (sWiT[:], d_WiT), (sWsT[:], d_WsT), (sWoT[:], d_WoT),
                         (sWr[:], d_Wr), (sbi[:], d_bi), (sbs[:], d_bs),
                         (sbo[:], d_bo), (sbg[:], d_bg), (sbu[:], d_bu),
                         (sdb[:], d_dbr), (son1[:], d_on1), (ssel[:], d_sel),
                         (sdm[:], d_dmask)]):
                    dma("gp", "s_lc" if j % 2 == 0 else "s_lc3", dst, csrc)
                lc_wait = [("s_lc", cnt["s_lc"]), ("s_lc3", cnt["s_lc3"])]

                v_init = ve(lambda e: e.memset(gz[:], 0.0))
                v_init = ve(lambda e: e.memset(stat[:, 14:15], EPS))
                v_init = ve(lambda e: e.memset(SPb[:, 0:1], 0.0))

                x_srcs = [d_xw[t * 128:(t + 1) * 128, :] for t in range(WC)] + \
                         [d_xm[t * 128:(t + 1) * 128, :] for t in range(NMC)]
                ln_free = {}
                xnb_free = {}
                tr_evac = {}
                router_evac = {}
                logit_evac = {}
                wload = {}
                gu_cons = {}
                wd_cons = {}

                def sched_wdma(kind, ei, fc):
                    par = (ei * NFC + fc) % 2
                    if kind == "g":
                        dst, src, ru, sem = wgu[:, 2 * par + 0], d_Wg[ei, fc], gu_cons.get(par), "s_lw%d" % par
                    elif kind == "u":
                        dst, src, ru, sem = wgu[:, 2 * par + 1], d_Wu[ei, fc], gu_cons.get(par), "s_lw%d" % par
                    else:
                        dst, src, ru, sem = wdb_[:, par], d_Wd[ei, fc], wd_cons.get(par), "s_ld%d" % par
                    wl = [("s_te", ru)] if ru is not None else []
                    wload[(kind, ei, fc)] = (sem, dma("gp", sem, dst, src, wait=wl))

                for fc in range(2):
                    sched_wdma("g", 0, fc); sched_wdma("u", 0, fc)
                sched_wdma("d", 0, 0); sched_wdma("d", 0, 1)

                ew_done = {}
                ts_read = {}
                gtile = [0]

                def emit_gu_tile(ei, fc, stt, weighted, bc_ev_l=None):
                    for _ in gu_tile_gen(ei, fc, stt, weighted, bc_ev_l):
                        pass

                def gu_tile_gen(ei, fc, stt, weighted, bc_ev_l=None):
                    par = (ei * NFC + fc) % 2
                    lwsem, lwg = wload[("u", ei, fc)]
                    epar = ei % 2
                    tile = gtile[0]
                    gtile[0] += 1
                    tp = tile % 2
                    bg_ = pb[tp]
                    bu2 = pb[2 + tp]
                    gwl = [(lwsem, lwg), ("s_sc", st_ready_ve[stt])]
                    if tile < 2:
                        gwl.extend(lc_wait)
                    if tile >= 2:
                        gwl.append(ts_read[tp][0])
                    mg = None
                    for dc in range(NDC):
                        rhs = xn_fm[:, WC + stt * 4:WC + stt * 4 + 4, dc, :]
                        mg = te(lambda e, dc=dc, bg_=bg_, par=par, rhs=rhs:
                                e.matmul(bg_[:, 0:STW], wgu[:, 2 * par, dc, :], rhs,
                                         start=(dc == 0), stop=(dc == NDC - 1)),
                                wait=(gwl if dc == 0 else ()), inc=(dc == NDC - 1))
                        if dc % 2 == 1:
                            yield
                    uwl2 = [ts_read[tp][1]] if tile >= 2 else []
                    mu2 = None
                    for dc in range(NDC):
                        rhs = xn_fm[:, WC + stt * 4:WC + stt * 4 + 4, dc, :]
                        mu2 = te(lambda e, dc=dc, bu2=bu2, par=par, rhs=rhs:
                                 e.matmul(bu2[:, 0:STW], wgu[:, 2 * par + 1, dc, :], rhs,
                                          start=(dc == 0), stop=(dc == NDC - 1)),
                                 wait=(uwl2 if dc == 0 else ()), inc=(dc == NDC - 1))
                        if dc % 2 == 1:
                            yield
                    if stt == NST - 1:
                        gu_cons[par] = cnt["s_te"]
                        nxt = ei * NFC + fc + 2
                        if nxt < NEXP * NFC:
                            ne, nf = divmod(nxt, NFC)
                            sched_wdma("g", ne, nf)
                            sched_wdma("u", ne, nf)
                    t2 = et2[:, tp]; t2w = et2w[:, tp]
                    tg = etg[:, tp]; tu = etu[:, tp]
                    t1 = tg; t3 = t2w; t4 = tu; t5_ = tu
                    gbias = sbg[:, ei * NFC + fc: ei * NFC + fc + 1]
                    ubias = sbu[:, ei * NFC + fc: ei * NFC + fc + 1]
                    if weighted:
                        cg = sc(lambda e, tg=tg, bg_=bg_: e.copy(tg, bg_[:, 0:STW]),
                                wait=[("s_te", mg)])
                        cu = sc(lambda e, tu=tu, bu2=bu2: e.copy(tu, bu2[:, 0:STW]),
                                wait=[("s_te", mu2)])
                        ecg, ecu = ("s_sc", cg), ("s_sc", cu)
                    else:
                        cg = ve(lambda e, tg=tg, bg_=bg_: e.tensor_copy(tg, bg_[:, 0:STW]),
                                wait=[("s_te", mg)])
                        cu = ve(lambda e, tu=tu, bu2=bu2: e.tensor_copy(tu, bu2[:, 0:STW]),
                                wait=[("s_te", mu2)])
                        ecg, ecu = ("s_ve", cg), ("s_ve", cu)
                    ts_read[tp] = (ecg, ecu)
                    vg = ve(lambda e, t1=t1, tg=tg, gbias=gbias:
                            e.tensor_scalar(t1, tg, gbias, LIMIT, ALU.add, ALU.min),
                            wait=[ecg])
                    ve(lambda e, t4=t4, tu=tu, ubias=ubias:
                       e.tensor_scalar(t4, tu, ubias, LIMIT, ALU.add, ALU.min),
                       wait=[ecu])
                    ve(lambda e, t5_=t5_, t4=t4:
                       e.tensor_scalar(t5_, t4, -LIMIT, 1.0, ALU.max, ALU.add))
                    sg = sc(lambda e, t2=t2, t1=t1:
                            e.activation(t2, t1, ACT.Sigmoid, scale=ALPHA),
                            wait=[("s_ve", vg)])
                    if weighted:
                        wbs = wbc[:, epar, stt * STW:(stt + 1) * STW]
                        ve(lambda e, t2w=t2w, t2=t2, wbs=wbs:
                           e.tensor_mul(t2w, t2, wbs),
                           wait=[("s_sc", sg), ("s_sc", bc_ev_l[stt])])
                        ve(lambda e, t3=t3, t1=t1, t2w=t2w: e.tensor_mul(t3, t1, t2w))
                    else:
                        ve(lambda e, t3=t3, t1=t1, t2=t2: e.tensor_mul(t3, t1, t2),
                           wait=[("s_sc", sg)])
                    g2 = ve(lambda e, fc=fc, stt=stt, t5_=t5_, t3=t3:
                            e.tensor_mul(act_w[:, fc, stt * STW:(stt + 1) * STW],
                                         t5_, t3))
                    ew_done[(ei, fc, stt)] = g2


                pump_q = [(0, fc, stt) for fc in range(NFC) for stt in range(NST)]
                pump_cur = [None]
                ready_st = [-1]
                st_ready_ve = {}

                def pump(n):
                    k = 0
                    while k < n:
                        if pump_cur[0] is None:
                            if not pump_q:
                                return
                            ei_, fc_, stt_ = pump_q[0]
                            if stt_ > ready_st[0]:
                                return
                            pump_q.pop(0)
                            pump_cur[0] = gu_tile_gen(ei_, fc_, stt_, weighted=False)
                        try:
                            next(pump_cur[0])
                            k += 1
                        except StopIteration:
                            pump_cur[0] = None

                def pump_all():
                    for q in range(NST):
                        if q not in st_ready_ve:
                            st_ready_ve[q] = cnt["s_sc"]
                    ready_st[0] = NST - 1
                    while pump_q or pump_cur[0] is not None:
                        pump(8)

                NTCR = ntc_run if ntc_run else NTC
                lnA = {}
                lnB = {}

                def emitLoad(tc):
                    par3 = tc % 3
                    wl = []
                    if tc >= 3:
                        pv, ps = ln_free[par3]
                        wl = [("s_ve", pv), ("s_sc", ps)]
                    lxsem = "s_lx%d" % (tc % 2)
                    lxv = dma("sy", lxsem, xln[:, par3], x_srcs[tc], wait=wl)
                    lnA[tc] = (lxsem, lxv)

                def emitA(tc):
                    par = tc % 2
                    par3 = tc % 3
                    lxsem, lxv = lnA[tc]
                    xl = xln[:, par3]
                    ssum = stat[:, par * 8 + 0: par * 8 + 1]
                    negm = stat[:, par * 8 + 1: par * 8 + 2]
                    bb = stat[:, par * 8 + 5: par * 8 + 6]
                    var = stat[:, par * 8 + 2: par * 8 + 3]
                    ve(lambda e, xl=xl, ssum=ssum: e.reduce_sum(ssum, xl, AX),
                       wait=[(lxsem, lxv)])
                    ve(lambda e, ssum=ssum, negm=negm:
                       e.tensor_scalar_mul(negm, ssum, -1.0 / DM))
                    bwl2 = [("s_sc", bb_free[par])] if tc >= 2 else []
                    vb = ve(lambda e, negm=negm, bb=bb:
                            e.scalar_tensor_tensor(bb, negm, -1.0, negm,
                                                   ALU.mult, ALU.mult),
                            wait=bwl2)
                    vb = ve(lambda e, bb=bb: e.tensor_scalar_add(bb, bb, EPS))
                    s2 = sc(lambda e, xl=xl, var=var:
                            e.activation(otm[:, 0], xl, ACT.Square, accum_out=var),
                            wait=[(lxsem, lxv)])
                    lnB[tc] = (vb, s2)

                def emitB(tc):
                    par = tc % 2
                    par3 = tc % 3
                    xl = xln[:, par3]
                    xn = xnb[:, par]
                    negm = stat[:, par * 8 + 1: par * 8 + 2]
                    var = stat[:, par * 8 + 2: par * 8 + 3]
                    sq_ = stat[:, par * 8 + 3: par * 8 + 4]
                    inv = stat[:, par * 8 + 4: par * 8 + 5]
                    bb = stat[:, par * 8 + 5: par * 8 + 6]
                    vb, s2 = lnB[tc]
                    s3 = sc(lambda e, var=var, sq_=sq_, bb=bb:
                            e.activation(sq_, var, ACT.Sqrt, bias=bb,
                                         scale=1.0 / DM),
                            wait=[("s_ve", vb)])
                    bb_free[par] = s3
                    vwl = [("s_sc", s3)]
                    if tc >= 2:
                        vwl.append(("s_sc", inv_free[par]))
                    v3 = ve(lambda e, inv=inv, sq_=sq_: e.reciprocal(inv, sq_),
                            wait=vwl)
                    xwl = [("s_ve", v3)]
                    if tc >= 2:
                        xwl.append(("s_te", xnb_free[par]))
                    v4 = ve(lambda e, xn=xn, xl=xl, negm=negm, inv=inv:
                            e.tensor_scalar(xn, xl, negm, inv, ALU.add, ALU.mult),
                            wait=xwl)
                    inv_free[par] = cnt["s_sc"]
                    ln_free[par3] = (v4, s2)
                    lnB[tc] = v4

                def emitC(tc):
                    par = tc % 2
                    xn = xnb[:, par]
                    v4 = lnB[tc]
                    bank = pbt[par]
                    tw = [("s_ve", v4)]
                    if tc < 2:
                        tw.append(("s_lc2", lc_eye))
                    else:
                        tw.append(("s_sc", tr_evac[par]))
                    tvals = []
                    for dc in range(NDC):
                        t_ = te(lambda e, bank=bank, dc=dc, xn=xn:
                                e.transpose(bank[:, dc * 128:(dc + 1) * 128],
                                            xn[:, dc * 128:(dc + 1) * 128], sI16[:]),
                                wait=(tw if dc == 0 else ()), inc=(dc in (3, 7)))
                        tvals.append(t_)
                    xnb_free[par] = tvals[7]
                    sc(lambda e, bank=bank, tc=tc:
                       e.copy(xn_fm[:, tc, :, :], bank[:]),
                       wait=[("s_te", tvals[7])])
                    tr_evac[par] = cnt["s_sc"]

                twid = [512, 512, 512, 512, TW - 2048]
                d_evac = {}
                su_h, ss_h = {}, {}

                def emit_usp(t5):
                    wt_ = twid[t5]
                    c0 = t5 * 512
                    tc0 = c0 // 128
                    ntc_ = wt_ // 128
                    bu_, bs_, bd_ = pb[0], pb[1], pb[2]
                    uwl = (([("s_sc", usp_xn[t5])] + lc_wait) if t5 == 0
                           else [("s_sc", usp_xn[t5]), ("s_ve", d_evac[t5 - 1])])
                    mu = None
                    for dc in range(NDC):
                        rhs = xn_fm[:, tc0:tc0 + ntc_, dc, :]
                        mu = te(lambda e, dc=dc, bu_=bu_, rhs=rhs, wt_=wt_:
                                e.matmul(bu_[:, 0:wt_], sWiT[:, dc, :], rhs,
                                         start=(dc == 0), stop=(dc == NDC - 1)),
                                wait=(uwl if dc == 0 else ()), inc=(dc == NDC - 1))
                    su_h[t5] = sc(lambda e, bu_=bu_, c0=c0, wt_=wt_:
                                  e.activation(u_bf[:, c0:c0 + wt_], bu_[:, 0:wt_],
                                               ACT.Identity, bias=sbi[:]),
                                  wait=[("s_te", mu)])
                    swl2 = [] if t5 < 1 else [("s_sc", ss_h[t5 - 1])]
                    ms = None
                    for dc in range(NDC):
                        rhs = xn_fm[:, tc0:tc0 + ntc_, dc, :]
                        ms = te(lambda e, dc=dc, bs_=bs_, rhs=rhs, wt_=wt_:
                                e.matmul(bs_[:, 0:wt_], sWsT[:, dc, :], rhs,
                                         start=(dc == 0), stop=(dc == NDC - 1)),
                                wait=(swl2 if dc == 0 else ()), inc=(dc == NDC - 1))
                    ss_h[t5] = sc(lambda e, bs_=bs_, c0=c0, wt_=wt_:
                                  e.activation(SPb[:, 1 + c0:1 + c0 + wt_], bs_[:, 0:wt_],
                                               ACT.Identity, bias=sbs[:]),
                                  wait=[("s_te", ms)])
                    vs = ss_h[t5]
                    dwl = [("s_sc", vs), ("s_ve", v_init)]
                    if t5 >= 1:
                        dwl.append(("s_ve", d_evac[t5 - 1]))
                    te(lambda e, bd_=bd_, c0=c0, wt_=wt_:
                       e.matmul(bd_[:, 0:wt_], sAT[:], SPb[:, c0:c0 + wt_],
                                start=True, stop=False, skip_group_check=True),
                       wait=dwl, inc=False)
                    md2 = te(lambda e, bd_=bd_, c0=c0, wt_=wt_:
                             e.matmul(bd_[:, 0:wt_], sBT[:], u_bf[:, c0:c0 + wt_],
                                      start=False, stop=True, skip_group_check=True),
                             wait=[("s_sc", su_h[t5])])
                    pump(2)
                    if t5 == 0:
                        ve(lambda e, bd_=bd_: e.tensor_mul(Dt[:, 0:W], bd_[:, 0:W], sdm[:]),
                           wait=[("s_te", md2)])
                        d_evac[t5] = ve(lambda e, bd_=bd_:
                                        e.tensor_copy(Dt[:, W:512], bd_[:, W:512]))
                    else:
                        d_evac[t5] = ve(lambda e, bd_=bd_, c0=c0, wt_=wt_:
                                        e.tensor_copy(Dt[:, c0:c0 + wt_], bd_[:, 0:wt_]),
                                        wait=[("s_te", md2)])

                inv_free = {}
                bb_free = {}
                usp_xn = {}
                for col in range(NTCR + 3):
                    if col < NTCR:
                        emitLoad(col)
                        emitA(col)
                    if 1 <= col <= NTCR:
                        emitB(col - 1)
                    if 2 <= col < NTCR + 2:
                        emitC(col - 2)
                    if not ntc_run:
                        t5r = (col - 6) // 4
                        if 0 <= t5r <= 3 and (col - 6) % 4 == 0:
                            usp_xn[t5r] = cnt["s_sc"]
                            emit_usp(t5r)
                        if col == NTCR + 2:
                            usp_xn[4] = cnt["s_sc"]
                            emit_usp(4)
                ve_xn_all = cnt["s_ve"]

                if PH < 0: raise _Done()
                ve_xn_all0 = cnt["s_ve"]
                sc_xn_all0 = cnt["s_sc"]
                r_mm = {}
                r_s6 = {}
                r_t6 = {}
                r_v7 = {}

                def emitR1(mc):
                    rpar = mc % 2
                    tc = mc + WC
                    lbank = pb[4]
                    lg = lgb[:, rpar]
                    mmv = None
                    for dc in range(NDC):
                        lwl = []
                        if dc == 0:
                            if mc == 0:
                                lwl.append(("s_ve", ve_xn_all0))
                                lwl.append(("s_sc", sc_xn_all0))
                                lwl.extend(lc_wait)
                            else:
                                lwl.append(("s_sc", logit_evac[(mc - 1) % 2]))
                        mmv = te(lambda e, dc=dc, lbank=lbank, tc=tc:
                                 e.matmul(lbank[:, 0:8], xn_fm[:, tc, dc, :],
                                          sWr[:, dc, :],
                                          start=(dc == 0), stop=(dc == NDC - 1)),
                                 wait=lwl, inc=(dc == NDC - 1))
                    ewl = [("s_te", mmv)]
                    if mc >= 2:
                        ewl.append(("s_ve", r_v7[rpar]))
                    logit_evac[rpar] = sc(lambda e, lg=lg, lbank=lbank:
                                          e.copy(lg, lbank[:, 0:8]), wait=ewl)
                    r_mm[mc] = logit_evac[rpar]

                def emitR2(mc):
                    rpar = mc % 2
                    lg = lgb[:, rpar]
                    m1 = rt[:, rpar, 0:1]; nm1 = rt[:, rpar, 1:2]
                    m2 = rt[:, rpar, 2:3]; ssum2 = rt[:, rpar, 3:4]
                    sinv = rt[:, rpar, 4:5]; eq = rt[:, rpar, 5:13]
                    vwl = [("s_sc", r_mm[mc])]
                    if mc >= 2:
                        vwl.append(("s_sc", r_s6[rpar]))
                    v4 = ve(lambda e, m1=m1, lg=lg: e.reduce_max(m1, lg, AX),
                            wait=vwl)
                    ve(lambda e, nm1=nm1, m1=m1: e.tensor_scalar_mul(nm1, m1, -1.0),
                       wait=[("s_ve", v4)])
                    ve(lambda e, eq=eq, lg=lg, m1=m1:
                       e.tensor_scalar(eq, lg, m1, None, ALU.is_equal))
                    ve(lambda e, eq=eq, lg=lg:
                       e.scalar_tensor_tensor(eq, eq, -1e30, lg, ALU.mult, ALU.add))
                    vm2 = ve(lambda e, m2=m2, eq=eq: e.reduce_max(m2, eq, AX))
                    v6 = ve(lambda e, eq=eq, lg=lg, m2=m2:
                            e.tensor_scalar(eq, lg, m2, None, ALU.is_ge),
                            wait=[("s_ve", vm2)])
                    s5 = sc(lambda e, lg=lg, nm1=nm1:
                            e.activation(lg, lg, ACT.Exp, bias=nm1),
                            wait=[("s_ve", v6)])
                    v7 = ve(lambda e, lg=lg, eq=eq: e.tensor_mul(lg, eq, lg),
                            wait=[("s_sc", s5)])
                    v8 = ve(lambda e, ssum2=ssum2, lg=lg: e.reduce_sum(ssum2, lg, AX))
                    v9 = ve(lambda e, sinv=sinv, ssum2=ssum2: e.reciprocal(sinv, ssum2),
                            wait=[("s_ve", v8)])
                    wt = wtm[:, rpar]
                    swl = [("s_ve", v9)]
                    if mc >= 2:
                        swl.append(("s_te", r_t6[rpar]))
                    r_s6[rpar] = sc(lambda e, wt=wt, lg=lg, sinv=sinv:
                                    e.activation(wt, lg, ACT.Identity, scale=sinv),
                                    wait=swl)
                    r_v7[rpar] = v9

                def emitR3(mc):
                    rpar = mc % 2
                    wt = wtm[:, rpar]
                    tbank = pb[5]
                    t6w = [("s_sc", r_s6[rpar])]
                    if mc >= 1:
                        t6w.append(("s_sc", router_evac[(mc - 1) % 2]))
                    t6 = te(lambda e, tbank=tbank, wt=wt:
                            e.transpose(tbank[0:8, 0:128], wt, sI32[:]), wait=t6w)
                    r_t6[rpar] = t6
                    router_evac[rpar] = sc(
                        lambda e, mc=mc, tbank=tbank:
                        e.copy(w_fmb[0:8, mc * 128:(mc + 1) * 128], tbank[0:8, 0:128]),
                        wait=[("s_te", t6)])

                for q in range(NST):
                    if q not in st_ready_ve:
                        st_ready_ve[q] = cnt["s_sc"]
                ready_st[0] = NST - 1
                for col in range(NMC + 2):
                    if col >= 2:
                        emitR3(col - 2)
                    if 1 <= col <= NMC:
                        emitR2(col - 1)
                    if col < NMC:
                        emitR1(col)

                ve_xn_all = cnt["s_ve"]
                sc_r_all = cnt["s_sc"]

                bc0_ev = {}
                for stt in range(NST):
                    bk = pb[4 + stt % 2]
                    bwl = (([("s_sc", sc_r_all)] + (lc_wait if stt == 0 else []))
                           if stt < 2 else [("s_sc", bc0_ev[stt - 2])])
                    m_ = te(lambda e, bk=bk, stt=stt:
                            e.matmul(bk[:, 0:STW], ssel[:, 0:128],
                                     w_fmb[:, stt * STW:(stt + 1) * STW],
                                     start=True, stop=True),
                            wait=bwl)
                    bc0_ev[stt] = sc(
                        lambda e, bk=bk, stt=stt:
                        e.copy(wbc[:, 0, stt * STW:(stt + 1) * STW], bk[:, 0:STW]),
                        wait=[("s_te", m_)])
                    pump(2)
                bc0_done = cnt["s_sc"]


                ve_d_all = cnt["s_ve"]

                if PH < 2: raise _Done()
                ch_act = {}
                for s in range(NSTEP):
                    pump(2)
                    b = s % 2
                    if b == 0:
                        for k in range(2):
                            sk = s + k
                            if sk >= NSTEP:
                                break
                            wl = []
                            if k == 0:
                                wl = [("s_ve", ve_d_all),
                                      ("s_sc", ch_act[s - 1] if s > 0 else cnt["s_sc"])]
                            te(lambda e, sk=sk, bk=pb[4 + sk % 2]:
                               e.matmul(bk[:, 0:NCH], sI16[:],
                                        Dt[:, WOFF + sk:WOFF + sk + SSTR:LCH],
                                        start=True, stop=False, skip_group_check=True),
                               wait=wl, inc=False)
                    if s == 0:
                        rhs = gz[:]
                    elif s <= WCH:
                        rhs = gw[:, ((s - 1) % 2) * NCH:((s - 1) % 2) * NCH + NCH]
                    else:
                        t0 = s - 1 - WCH
                        rhs = G[:, t0:t0 + SSTR:LCH]
                    wl = [("s_sc", ch_act[s - 1])] if s > 0 else []
                    mv = te(lambda e, b=b, rhs=rhs:
                            e.matmul(pb[4 + b][:, 0:NCH], sAT[:], rhs,
                                     start=False, stop=True, skip_group_check=True),
                            wait=wl)
                    if s < WCH:
                        dst = gw[:, (s % 2) * NCH:(s % 2) * NCH + NCH]
                    else:
                        t0 = s - WCH
                        dst = G[:, t0:t0 + SSTR:LCH]
                    ch_act[s] = sc(lambda e, b=b, dst=dst:
                                   e.activation(dst, pb[4 + b][:, 0:NCH], ACT.Tanh),
                                   wait=[("s_te", mv)])
                pump_all()
                sc_chain = cnt["s_sc"]
                for stt in range(NST):
                    for fc in range(NFC):
                        fwl = [("s_sc", bc0_done)] if (stt == 0 and fc == 0) else []
                        ew_done[(0, fc, stt)] = ve(
                            lambda e, fc=fc, stt=stt:
                            e.tensor_mul(act_w[:, fc, stt * STW:(stt + 1) * STW],
                                         act_w[:, fc, stt * STW:(stt + 1) * STW],
                                         wbc[:, 0, stt * STW:(stt + 1) * STW]),
                            wait=fwl)
                wbc_cons0 = cnt["s_ve"]

                if PH < 3: raise _Done()
                vstate = {}
                for stt in range(NST):
                    c0 = stt * STW
                    vstate[stt] = ve(lambda e, c0=c0:
                                     e.tensor_add(u_bf[:, c0:c0 + STW], G[:, c0:c0 + STW],
                                                  SPb[:, 1 + W + c0:1 + W + c0 + STW]),
                                     wait=([("s_sc", sc_chain)] if stt == 0 else ()))
                ys_evac = {}
                for stt in range(NST):
                    bk = pb[4 + stt % 2]
                    ywl = [("s_ve", vstate[stt])]
                    ywl.append(("s_sc", sc_chain if stt < 2 else ys_evac[stt - 2]))
                    m_ = te(lambda e, bk=bk, stt=stt:
                            e.matmul(bk[:, 0:STW], sCT[:], u_bf[:, stt * STW:(stt + 1) * STW],
                                     start=True, stop=True),
                            wait=ywl)
                    ys_evac[stt] = sc(lambda e, bk=bk, stt=stt:
                                      e.copy(ys_bf[:, stt * STW:(stt + 1) * STW], bk[:, 0:STW]),
                                      wait=[("s_te", m_)])
                sc_ys = cnt["s_sc"]

                attn_ev = {}
                for hc in range(NDC):
                    for stt in range(NST):
                        idx = hc * NST + stt
                        bk = pb[4 + idx % 2]
                        wl = [("s_sc", sc_ys if idx < 2 else attn_ev[idx - 2])]
                        te(lambda e, bk=bk, hc=hc, stt=stt:
                           e.matmul(bk[:, 0:STW], sWoT[:, hc, :],
                                    ys_bf[:, stt * STW:(stt + 1) * STW],
                                    start=True, stop=False, skip_group_check=True),
                           wait=wl, inc=False)
                        m_ = te(lambda e, bk=bk, hc=hc, stt=stt:
                                e.matmul(bk[:, 0:STW], sdb[:, hc * 128:(hc + 1) * 128],
                                         w_fmb[:, stt * STW:(stt + 1) * STW],
                                         start=False, stop=True, skip_group_check=True))
                        attn_ev[idx] = sc(
                            lambda e, bk=bk, hc=hc, stt=stt:
                            e.activation(acc[:, hc, stt * STW:(stt + 1) * STW],
                                         bk[:, 0:STW], ACT.Identity, bias=sbo[:, hc:hc + 1]),
                            wait=[("s_te", m_)])
                sc_attn = cnt["s_sc"]

                fin_add = {}
                out_done = {}

                fxload = {}

                def emit_final_load(mc):
                    par4 = mc % 4
                    fxsem = "s_lf%d" % (mc % 2)
                    xwl = [("s_ve", fin_add[par4])] if mc >= 4 else []
                    fxload[mc] = (fxsem,
                                  dma("gp", fxsem, xres[:, par4],
                                      d_xm[mc * 128:(mc + 1) * 128, :], wait=xwl))

                def emit_final_group(stt):
                    if stt == 0:
                        for mc_ in range(4):
                            emit_final_load(mc_)
                    for mc in range(stt * 4, stt * 4 + 4):
                        par = mc % 2
                        par4 = mc % 4
                        fosem = "s_ou%d" % par
                        fxsem, lxv = fxload[mc]
                        bank = pbt[par]
                        tvals = []
                        for hc in range(NDC):
                            wl2 = []
                            if hc == 0:
                                wl2 = [("s_ve", acc_add[(NDC - 1, stt)])]
                                wl2.append(("s_ve", fin_add.get(par, cnt["s_ve"])
                                            if mc >= 2 else cnt["s_ve"]))
                            t_ = te(lambda e, bank=bank, hc=hc, mc=mc:
                                    e.transpose(bank[:, hc * 128:(hc + 1) * 128],
                                                acc[:, hc, mc * 128:(mc + 1) * 128], sI16[:]),
                                    wait=wl2, inc=(hc in (3, 7)))
                            tvals.append(t_)
                        vwl = [("s_te", tvals[7]), (fxsem, lxv)]
                        if mc >= 2:
                            vwl.append((fosem, out_done[par]))
                        ve(lambda e, bank=bank, par=par, par4=par4:
                           e.tensor_add(otm[:, par, :], bank[:], xres[:, par4, :]),
                           wait=vwl)
                        fin_add[par] = cnt["s_ve"]
                        fin_add[par4] = cnt["s_ve"]
                        if mc + 4 < NMC:
                            emit_final_load(mc + 4)
                        out_done[par] = dma("sy", fosem, d_y[mc * 128:(mc + 1) * 128, :],
                                            otm[:, par], wait=[("s_ve", fin_add[par])])


                if PH < 4: raise _Done()
                acc_add = {}
                wbc_cons = {0: wbc_cons0}
                dn_read = {}
                for ei in range(NEXP):
                    epar = ei % 2
                    if ei == 0:
                        bc_ev_l = bc0_ev
                    else:
                     bc_ev_l = {}
                     for stt in range(NST):
                        bk = pb[4 + stt % 2]
                        bwl = []
                        if stt < 2:
                            if ei == 0:
                                bwl.append(("s_sc", sc_attn))
                            else:
                                bwl.append(("s_ve", dn_read[stt % 2]))
                            if ei >= 2:
                                bwl.append(("s_ve", wbc_cons[epar]))
                        else:
                            bwl.append(("s_sc", bc_ev_l[stt - 2]))
                        m_ = te(lambda e, bk=bk, ei=ei, stt=stt:
                                e.matmul(bk[:, 0:STW], ssel[:, ei * 128:(ei + 1) * 128],
                                         w_fmb[:, stt * STW:(stt + 1) * STW],
                                         start=True, stop=True),
                                wait=bwl)
                        bc_ev_l[stt] = sc(
                            lambda e, bk=bk, epar=epar, stt=stt:
                            e.copy(wbc[:, epar, stt * STW:(stt + 1) * STW], bk[:, 0:STW]),
                            wait=[("s_te", m_)])

                    if ei > 0:
                        for fc in range(NFC):
                            for stt in range(NST):
                                emit_gu_tile(ei, fc, stt, weighted=True,
                                             bc_ev_l=bc_ev_l)
                        wbc_cons[epar] = cnt["s_ve"]

                    for fc in range(NFC):
                        par = (ei * NFC + fc) % 2
                        ldsem, lwd = wload[("d", ei, fc)]
                        for stt in range(NST):
                            idx = ei * NFC * NST + fc * NST + stt
                            dp = (fc * NST + stt) % 2
                            bk = pb[4 + dp]
                            wl = [(ldsem, lwd), ("s_ve", ew_done[(ei, NFC - 1, stt)])]
                            if ei == 0 and fc == 0 and stt < 2:
                                wl.append(("s_sc", sc_attn))
                            elif fc == 0 and stt < 2:
                                wl.append(("s_sc", bc_ev_l[stt + 2]))
                            elif idx >= 2:
                                wl.append(("s_ve", dn_read[dp]))
                            md = None
                            for ic in range(NDC):
                                md = te(lambda e, ic=ic, bk=bk, par=par, stt=stt:
                                        e.matmul(bk[:, 0:STW], wdb_[:, par, ic, :],
                                                 act_w[:, ic, stt * STW:(stt + 1) * STW],
                                                 start=(ic == 0), stop=(ic == NDC - 1),
                                                 skip_group_check=True),
                                        wait=(wl if ic == 0 else ()),
                                        inc=(ic == NDC - 1))
                            if stt == NST - 1:
                                wd_cons[par] = cnt["s_te"]
                                nxt = ei * NFC + fc + 2
                                if nxt < NEXP * NFC:
                                    ne, nf = divmod(nxt, NFC)
                                    sched_wdma("d", ne, nf)
                            awl = [("s_te", md)]
                            if ei == 0:
                                awl.append(("s_sc", attn_ev[fc * NST + stt]))
                            va = ve(lambda e, fc=fc, stt=stt, bk=bk:
                                    e.tensor_add(acc[:, fc, stt * STW:(stt + 1) * STW],
                                                 acc[:, fc, stt * STW:(stt + 1) * STW],
                                                 bk[:, 0:STW]),
                                    wait=awl)
                            dn_read[dp] = va
                            acc_add[(fc, stt)] = va
                            if PH >= 5 and ei == NEXP - 1 and fc == NFC - 1:
                                emit_final_group(stt)

            except _Done:
                pass

            if debug:
                vw_, sw_, tw_ = cnt["s_ve"], cnt["s_sc"], cnt["s_te"]
                def gdbg(name, src_ap):
                    dma("gp", "s_ou0", dbg[name], src_ap,
                        wait=[("s_ve", vw_), ("s_sc", sw_), ("s_te", tw_)])
                gdbg("xnfm", xn_fm[:].rearrange("p a b c -> p (a b c)"))
                gdbg("wfm", w_fmb[:])
                if PH >= 1:
                    gdbg("D", Dt[:]); gdbg("u", u_bf[:]); gdbg("sp", SPb[:, 1:])
                if PH >= 2:
                    gdbg("G", G[:])
                if PH >= 3:
                    gdbg("ys", ys_bf[:])
                    gdbg("acc", acc[:].rearrange("p a b -> p (a b)"))

            fin_o0 = cnt.get("s_ou0", 0)
            fin_o1 = cnt.get("s_ou1", 0)
            def fin(e):
                if fin_o0:
                    e.wait_ge(s_ou0, fin_o0)
                if fin_o1:
                    e.wait_ge(s_ou1, fin_o1)
            ops["sy"].append(fin)

            @block.gpsimd
            def _(g):
                for f in ops["gp"]:
                    f(g)

            @block.sync
            def _(s_):
                for f in ops["sy"]:
                    f(s_)

            @block.tensor
            def _(t_):
                for f in ops["te"]:
                    f(t_)

            @block.vector
            def _(v_):
                for f in ops["ve"]:
                    f(v_)

            @block.scalar
            def _(sc_):
                for f in ops["sc"]:
                    f(sc_)
    return nc


def prep_inputs(L0_attn_x, L0_attn_Wi, L0_attn_bi, L0_attn_Ws, L0_attn_bs,
                L0_attn_Wo, L0_attn_bo, L0_attn_A, L0_attn_B, L0_attn_C,
                L0_ffn_router_weight, L0_ffn_gate_up_proj, L0_ffn_gate_up_proj_bias,
                L0_ffn_down_proj, L0_ffn_down_proj_bias):
    f = lambda a: np.ascontiguousarray(np.asarray(a), dtype=np.float32)
    x = f(L0_attn_x).reshape(16384, DM)
    Wi, Ws, Wo = f(L0_attn_Wi), f(L0_attn_Ws), f(L0_attn_Wo)
    A, B, C = f(L0_attn_A), f(L0_attn_B), f(L0_attn_C)
    bi, bs, bo = f(L0_attn_bi), f(L0_attn_bs), f(L0_attn_bo)
    Wr = f(L0_ffn_router_weight)
    gup, gub = f(L0_ffn_gate_up_proj), f(L0_ffn_gate_up_proj_bias)
    dwn, dwb = f(L0_ffn_down_proj), f(L0_ffn_down_proj_bias)

    com = {}
    com["eye"] = np.eye(128, dtype=np.float32)
    com["AT"] = np.ascontiguousarray(A.T)
    com["BT"] = np.ascontiguousarray(B.T)
    com["CT"] = np.ascontiguousarray(C.T)
    com["WiT"] = np.ascontiguousarray(Wi.T.reshape(NDC, 128, 128).transpose(1, 0, 2))
    com["WsT"] = np.ascontiguousarray(Ws.T.reshape(NDC, 128, 128).transpose(1, 0, 2))
    com["WoT"] = np.ascontiguousarray(Wo.T.reshape(128, NDC, 128))
    com["Wr"] = np.ascontiguousarray(Wr.reshape(NDC, 128, 8).transpose(1, 0, 2))
    com["bi"] = bi.reshape(128, 1)
    com["bs"] = bs.reshape(128, 1)
    com["bo"] = np.ascontiguousarray(bo.reshape(NDC, 128).T)
    gate = np.ascontiguousarray(gup[:, :, 0::2])
    up = np.ascontiguousarray(gup[:, :, 1::2])
    com["Wg"] = np.ascontiguousarray(
        gate.reshape(NEXP, NDC, 128, NFC, 128).transpose(0, 3, 2, 1, 4))
    com["Wu"] = np.ascontiguousarray(
        up.reshape(NEXP, NDC, 128, NFC, 128).transpose(0, 3, 2, 1, 4))
    com["Wd"] = np.ascontiguousarray(
        dwn.reshape(NEXP, NDC, 128, NFC, 128).transpose(0, 3, 2, 1, 4))
    gb = gub[:, 0::2].reshape(NEXP, NFC, 128)
    ub = gub[:, 1::2].reshape(NEXP, NFC, 128)
    com["bg"] = np.ascontiguousarray(gb.reshape(NEXP * NFC, 128).T)
    com["bu"] = np.ascontiguousarray(ub.reshape(NEXP * NFC, 128).T)
    com["dbr"] = np.ascontiguousarray(dwb)
    com["on1"] = np.ones((1, 128), np.float32)
    sel = np.zeros((NEXP, NEXP * 128), np.float32)
    for e_ in range(NEXP):
        sel[e_, e_ * 128:(e_ + 1) * 128] = 1.0
    com["sel"] = sel

    in_maps = []
    for c in range(8):
        m = dict(com)
        r0 = c * NTOK
        m["xm"] = np.ascontiguousarray(x[r0:r0 + NTOK])
        if c % 4 == 0:
            m["xw"] = np.zeros((W, DM), np.float32)
            m["dmask"] = np.zeros((128, W), np.float32)
        else:
            m["xw"] = np.ascontiguousarray(x[r0 - W:r0])
            m["dmask"] = np.ones((128, W), np.float32)
        in_maps.append(m)
    return in_maps


_cache = {}


def kernel(**inputs):
    in_maps = prep_inputs(**inputs)
    if "nc" not in _cache:
        _cache["nc"] = build_nc(debug=False)
    res = run_bass_kernel_spmd(_cache["nc"], in_maps, core_ids=list(range(8)))
    out = np.concatenate([np.asarray(res.results[c]["y"]) for c in range(8)], axis=0)
    return out.reshape(2, 8192, DM)


# version 22
# speedup vs baseline: 1.0463x; 1.0086x over previous
import numpy as np
import concourse.bass as bass
import concourse.mybir as mybir
from concourse.bass_utils import run_bass_kernel_spmd

F32 = mybir.dt.float32
BF16 = mybir.dt.bfloat16
AX = mybir.AxisListType.X
ALU = mybir.AluOpType
ACT = mybir.ActivationFunctionType

DM = 1024
DS = 128
NTOK = 2048
W = 128
LCH = 32
NCH = NTOK // LCH
WCH = 96
WOFF = W - WCH
NSTEP = WCH + LCH
TW = W + NTOK
NDC = DM // 128
NTC = TW // 128
WC = W // 128
NMC = NTOK // 128
NEXP = 8
NFC = 8
NST = 4
STW = 512
EPS = 1e-5
ALPHA = 1.702
LIMIT = 7.0
SSTR = LCH * (NCH - 1) + 1


def build_nc(debug=False, upto="full", ntc_run=None, skip_big=False):
    import os as _os
    LNL = int(_os.environ.get("MK_LN_LEVEL", "9"))
    PH = {"ln0": -1, "ln": 0, "usp": 1, "chain": 2, "attn": 3, "ffn": 4, "full": 5}[upto]
    nc = bass.Bass("TRN2", target_bir_lowering=False)

    d_xm = nc.dram_tensor("xm", [NTOK, DM], F32, kind="ExternalInput").ap()
    d_xw = nc.dram_tensor("xw", [W, DM], F32, kind="ExternalInput").ap()
    d_dmask = nc.dram_tensor("dmask", [128, W], F32, kind="ExternalInput").ap()
    d_eye = nc.dram_tensor("eye", [128, 128], F32, kind="ExternalInput").ap()
    d_AT = nc.dram_tensor("AT", [128, 128], F32, kind="ExternalInput").ap()
    d_BT = nc.dram_tensor("BT", [128, 128], F32, kind="ExternalInput").ap()
    d_CT = nc.dram_tensor("CT", [128, 128], F32, kind="ExternalInput").ap()
    d_WiT = nc.dram_tensor("WiT", [128, NDC, 128], F32, kind="ExternalInput").ap()
    d_WsT = nc.dram_tensor("WsT", [128, NDC, 128], F32, kind="ExternalInput").ap()
    d_WoT = nc.dram_tensor("WoT", [128, NDC, 128], F32, kind="ExternalInput").ap()
    d_Wr = nc.dram_tensor("Wr", [128, NDC, 8], F32, kind="ExternalInput").ap()
    d_bi = nc.dram_tensor("bi", [128, 1], F32, kind="ExternalInput").ap()
    d_bs = nc.dram_tensor("bs", [128, 1], F32, kind="ExternalInput").ap()
    d_bo = nc.dram_tensor("bo", [128, 8], F32, kind="ExternalInput").ap()
    d_bg = nc.dram_tensor("bg", [128, NEXP * NFC], F32, kind="ExternalInput").ap()
    d_bu = nc.dram_tensor("bu", [128, NEXP * NFC], F32, kind="ExternalInput").ap()
    d_dbr = nc.dram_tensor("dbr", [NEXP, DM], F32, kind="ExternalInput").ap()
    d_on1 = nc.dram_tensor("on1", [1, 128], F32, kind="ExternalInput").ap()
    d_sel = nc.dram_tensor("sel", [NEXP, NEXP * 128], F32, kind="ExternalInput").ap()
    if not skip_big:
        d_Wg = nc.dram_tensor("Wg", [NEXP, NFC, 128, NDC, 128], F32, kind="ExternalInput").ap()
        d_Wu = nc.dram_tensor("Wu", [NEXP, NFC, 128, NDC, 128], F32, kind="ExternalInput").ap()
        d_Wd = nc.dram_tensor("Wd", [NEXP, NFC, 128, NDC, 128], F32, kind="ExternalInput").ap()
    d_y = nc.dram_tensor("y", [NTOK, DM], F32, kind="ExternalOutput").ap()
    dbg = {}
    if debug:
        dbg["xnfm"] = nc.dram_tensor("dbg_xnfm", [128, NTC * DM], F32, kind="ExternalOutput").ap()
        dbg["D"] = nc.dram_tensor("dbg_D", [128, TW], F32, kind="ExternalOutput").ap()
        dbg["G"] = nc.dram_tensor("dbg_G", [128, NTOK], F32, kind="ExternalOutput").ap()
        dbg["wfm"] = nc.dram_tensor("dbg_wfm", [8, NTOK], F32, kind="ExternalOutput").ap()
        dbg["ys"] = nc.dram_tensor("dbg_ys", [128, NTOK], F32, kind="ExternalOutput").ap()
        dbg["acc"] = nc.dram_tensor("dbg_acc", [128, NDC * NTOK], F32, kind="ExternalOutput").ap()
        dbg["u"] = nc.dram_tensor("dbg_u", [128, TW], F32, kind="ExternalOutput").ap()
        dbg["sp"] = nc.dram_tensor("dbg_sp", [128, TW], F32, kind="ExternalOutput").ap()

    ops = {k: [] for k in ("te", "ve", "sc", "gp", "sy")}
    cnt = {}

    def _inc(sem, n=1):
        cnt[sem] = cnt.get(sem, 0) + n
        return cnt[sem]

    from contextlib import ExitStack
    with ExitStack() as _es:
        xn_fm = _es.enter_context(nc.sbuf_tensor("xn_fm", [128, NTC, NDC, 128], BF16))
        SPb = _es.enter_context(nc.sbuf_tensor("SPb", [128, 1 + TW], BF16))
        u_bf = _es.enter_context(nc.sbuf_tensor("u_bf", [128, TW], BF16))
        Dt = _es.enter_context(nc.sbuf_tensor("Dt", [128, TW], BF16))
        G = _es.enter_context(nc.sbuf_tensor("G", [128, NTOK], BF16))
        gw = _es.enter_context(nc.sbuf_tensor("gw", [128, 2 * NCH], BF16))
        gz = _es.enter_context(nc.sbuf_tensor("gz", [128, NCH], BF16))
        ys_bf = _es.enter_context(nc.sbuf_tensor("ys_bf", [128, NTOK], BF16))
        w_fmb = _es.enter_context(nc.sbuf_tensor("w_fmb", [8, NTOK], BF16))
        acc = _es.enter_context(nc.sbuf_tensor("acc", [128, NDC, NTOK], BF16))
        act_w = _es.enter_context(nc.sbuf_tensor("act_w", [128, NFC, NTOK], BF16))
        wbc = _es.enter_context(nc.sbuf_tensor("wbc", [128, 2, NTOK], BF16))
        sAT = _es.enter_context(nc.sbuf_tensor("sAT", [128, 128], BF16))
        sBT = _es.enter_context(nc.sbuf_tensor("sBT", [128, 128], BF16))
        sCT = _es.enter_context(nc.sbuf_tensor("sCT", [128, 128], BF16))
        sI16 = _es.enter_context(nc.sbuf_tensor("sI16", [128, 128], BF16))
        sI32 = _es.enter_context(nc.sbuf_tensor("sI32", [128, 128], F32))
        sWiT = _es.enter_context(nc.sbuf_tensor("sWiT", [128, NDC, 128], BF16))
        sWsT = _es.enter_context(nc.sbuf_tensor("sWsT", [128, NDC, 128], BF16))
        sWoT = _es.enter_context(nc.sbuf_tensor("sWoT", [128, NDC, 128], BF16))
        sWr = _es.enter_context(nc.sbuf_tensor("sWr", [128, NDC, 8], BF16))
        sbi = _es.enter_context(nc.sbuf_tensor("sbi", [128, 1], F32))
        sbs = _es.enter_context(nc.sbuf_tensor("sbs", [128, 1], F32))
        sbo = _es.enter_context(nc.sbuf_tensor("sbo", [128, 8], F32))
        sbg = _es.enter_context(nc.sbuf_tensor("sbg", [128, NEXP * NFC], F32))
        sbu = _es.enter_context(nc.sbuf_tensor("sbu", [128, NEXP * NFC], F32))
        sdb = _es.enter_context(nc.sbuf_tensor("sdb", [NEXP, DM], BF16))
        son1 = _es.enter_context(nc.sbuf_tensor("son1", [1, 128], BF16))
        ssel = _es.enter_context(nc.sbuf_tensor("ssel", [NEXP, NEXP * 128], BF16))
        sdm = _es.enter_context(nc.sbuf_tensor("sdm", [128, W], F32))
        xln = _es.enter_context(nc.sbuf_tensor("xln", [128, 3, DM], F32))
        xnb = _es.enter_context(nc.sbuf_tensor("xnb", [128, 2, DM], BF16))
        stat = _es.enter_context(nc.sbuf_tensor("stat", [128, 16], F32))
        lgb = _es.enter_context(nc.sbuf_tensor("lgb", [128, 2, 8], F32))
        rt = _es.enter_context(nc.sbuf_tensor("rt", [128, 2, 16], F32))
        wtm = _es.enter_context(nc.sbuf_tensor("wtm", [128, 2, 8], F32))
        wgu = _es.enter_context(nc.sbuf_tensor("wgu", [128, 4, NDC, 128], BF16))
        wdb_ = _es.enter_context(nc.sbuf_tensor("wdb_", [128, 2, NFC, 128], BF16))
        et2 = _es.enter_context(nc.sbuf_tensor("et2", [128, 2, STW], BF16))
        et2w = _es.enter_context(nc.sbuf_tensor("et2w", [128, 2, STW], BF16))
        etg = _es.enter_context(nc.sbuf_tensor("etg", [128, 2, STW], BF16))
        etu = _es.enter_context(nc.sbuf_tensor("etu", [128, 2, STW], BF16))
        xres = _es.enter_context(nc.sbuf_tensor("xres", [128, 4, DM], BF16))
        otm = _es.enter_context(nc.sbuf_tensor("otm", [128, 2, DM], F32))
        pb0 = _es.enter_context(nc.psum_tensor([128, 512], F32))
        pb1 = _es.enter_context(nc.psum_tensor([128, 512], F32))
        pb2 = _es.enter_context(nc.psum_tensor([128, 512], F32))
        pb3 = _es.enter_context(nc.psum_tensor([128, 512], F32))
        pb4 = _es.enter_context(nc.psum_tensor([128, 512], F32))
        pb5 = _es.enter_context(nc.psum_tensor([128, 512], F32))
        pbtA = _es.enter_context(nc.psum_tensor([128, 1024], BF16))
        pbtB = _es.enter_context(nc.psum_tensor([128, 1024], BF16))
        s_lc = _es.enter_context(nc.semaphore("s_lc"))
        s_lc2 = _es.enter_context(nc.semaphore("s_lc2"))
        s_lc3 = _es.enter_context(nc.semaphore("s_lc3"))
        s_lw0 = _es.enter_context(nc.semaphore("s_lw0"))
        s_lw1 = _es.enter_context(nc.semaphore("s_lw1"))
        s_ld0 = _es.enter_context(nc.semaphore("s_ld0"))
        s_ld1 = _es.enter_context(nc.semaphore("s_ld1"))
        s_lx0 = _es.enter_context(nc.semaphore("s_lx0"))
        s_lx1 = _es.enter_context(nc.semaphore("s_lx1"))
        s_lf0 = _es.enter_context(nc.semaphore("s_lf0"))
        s_lf1 = _es.enter_context(nc.semaphore("s_lf1"))
        s_te = _es.enter_context(nc.semaphore("s_te"))
        s_ve = _es.enter_context(nc.semaphore("s_ve"))
        s_sc = _es.enter_context(nc.semaphore("s_sc"))
        s_ou0 = _es.enter_context(nc.semaphore("s_ou0"))
        s_ou1 = _es.enter_context(nc.semaphore("s_ou1"))
        s_gp = _es.enter_context(nc.semaphore("s_gp"))
        block = _es.enter_context(nc.Block())
        if True:
            pb = [pb0, pb1, pb2, pb3, pb4, pb5]
            pbt = [pbtA, pbtB]
            sems = {"s_lc": s_lc, "s_lc2": s_lc2, "s_lc3": s_lc3, "s_lw0": s_lw0, "s_lw1": s_lw1,
                    "s_ld0": s_ld0, "s_ld1": s_ld1,
                    "s_lx0": s_lx0, "s_lx1": s_lx1, "s_lf0": s_lf0, "s_lf1": s_lf1, "s_te": s_te,
                    "s_ve": s_ve, "s_sc": s_sc, "s_ou0": s_ou0, "s_ou1": s_ou1,
                    "s_gp": s_gp}

            def mk(eng_key, sem_name, self_wait=False):
                def emitfn(fn, wait=(), inc=True):
                    for sn, val in wait:
                        assert val <= cnt.get(sn, 0), (sn, val, cnt.get(sn, 0), eng_key)
                    w = tuple(wait)
                    if self_wait and cnt.get(sem_name, 0) > 0:
                        w = ((sem_name, cnt[sem_name]),) + w
                    v = _inc(sem_name) if inc else None
                    def op(e, fn=fn, w=w, v=v, sem_name=sem_name):
                        for sn, val in w:
                            e.wait_ge(sems[sn], val)
                        ins = fn(e)
                        if v is not None:
                            ins.then_inc(sems[sem_name], 1)
                    ops[eng_key].append(op)
                    return cnt.get(sem_name, 0)
                return emitfn

            te = mk("te", "s_te")
            ve = mk("ve", "s_ve", self_wait=True)
            sc = mk("sc", "s_sc", self_wait=True)
            gpc = mk("gp", "s_gp", self_wait=True)

            class _Done(Exception):
                pass

            def dma(eng_key, sem_name, dst, src, wait=()):
                for sn, val in wait:
                    assert val <= cnt.get(sn, 0), (sn, val, cnt.get(sn, 0), "dma")
                w = tuple(wait)
                if cnt.get(sem_name, 0) > 0:
                    w = ((sem_name, cnt[sem_name]),) + w
                _inc(sem_name, 16)
                def op(e, dst=dst, src=src, w=w, sem_name=sem_name):
                    for sn, val in w:
                        e.wait_ge(sems[sn], val)
                    e.dma_start(dst, src).then_inc(sems[sem_name], 16)
                ops[eng_key].append(op)
                return cnt[sem_name]

            try:
                dma("gp", "s_lc2", sI16[:], d_eye)
                lc_eye = dma("gp", "s_lc2", sI32[:], d_eye)
                for j, (dst, csrc) in enumerate(
                        [(sAT[:], d_AT), (sBT[:], d_BT), (sCT[:], d_CT),
                         (sWiT[:], d_WiT), (sWsT[:], d_WsT), (sWoT[:], d_WoT),
                         (sWr[:], d_Wr), (sbi[:], d_bi), (sbs[:], d_bs),
                         (sbo[:], d_bo), (sbg[:], d_bg), (sbu[:], d_bu),
                         (sdb[:], d_dbr), (son1[:], d_on1), (ssel[:], d_sel),
                         (sdm[:], d_dmask)]):
                    dma("gp", "s_lc" if j % 2 == 0 else "s_lc3", dst, csrc)
                lc_wait = [("s_lc", cnt["s_lc"]), ("s_lc3", cnt["s_lc3"])]

                v_init = ve(lambda e: e.memset(gz[:], 0.0))
                v_init = ve(lambda e: e.memset(stat[:, 14:15], EPS))
                v_init = ve(lambda e: e.memset(SPb[:, 0:1], 0.0))

                x_srcs = [d_xw[t * 128:(t + 1) * 128, :] for t in range(WC)] + \
                         [d_xm[t * 128:(t + 1) * 128, :] for t in range(NMC)]
                ln_free = {}
                xnb_free = {}
                tr_evac = {}
                router_evac = {}
                logit_evac = {}
                wload = {}
                gu_cons = {}
                wd_cons = {}

                def sched_wdma(kind, ei, fc):
                    par = (ei * NFC + fc) % 2
                    if kind == "g":
                        dst, src, ru, sem = wgu[:, 2 * par + 0], d_Wg[ei, fc], gu_cons.get(par), "s_lw%d" % par
                    elif kind == "u":
                        dst, src, ru, sem = wgu[:, 2 * par + 1], d_Wu[ei, fc], gu_cons.get(par), "s_lw%d" % par
                    else:
                        dst, src, ru, sem = wdb_[:, par], d_Wd[ei, fc], wd_cons.get(par), "s_ld%d" % par
                    wl = [("s_te", ru)] if ru is not None else []
                    wload[(kind, ei, fc)] = (sem, dma("gp", sem, dst, src, wait=wl))

                for fc in range(2):
                    sched_wdma("g", 0, fc); sched_wdma("u", 0, fc)
                sched_wdma("d", 0, 0); sched_wdma("d", 0, 1)

                ew_done = {}
                ts_read = {}
                gtile = [0]

                def emit_gu_tile(ei, fc, stt, weighted, bc_ev_l=None):
                    for _ in gu_tile_gen(ei, fc, stt, weighted, bc_ev_l):
                        pass

                def gu_tile_gen(ei, fc, stt, weighted, bc_ev_l=None):
                    par = (ei * NFC + fc) % 2
                    lwsem, lwg = wload[("u", ei, fc)]
                    epar = ei % 2
                    tile = gtile[0]
                    gtile[0] += 1
                    tp = tile % 2
                    bg_ = pb[tp]
                    bu2 = pb[2 + tp]
                    gwl = [(lwsem, lwg), ("s_sc", st_ready_ve[stt])]
                    if tile < 2:
                        gwl.extend(lc_wait)
                    if tile >= 2:
                        gwl.append(ts_read[tp][0])
                    mg = None
                    for dc in range(NDC):
                        rhs = xn_fm[:, WC + stt * 4:WC + stt * 4 + 4, dc, :]
                        mg = te(lambda e, dc=dc, bg_=bg_, par=par, rhs=rhs:
                                e.matmul(bg_[:, 0:STW], wgu[:, 2 * par, dc, :], rhs,
                                         start=(dc == 0), stop=(dc == NDC - 1)),
                                wait=(gwl if dc == 0 else ()), inc=(dc == NDC - 1))
                        if dc % 2 == 1:
                            yield
                    uwl2 = [ts_read[tp][1]] if tile >= 2 else []
                    mu2 = None
                    for dc in range(NDC):
                        rhs = xn_fm[:, WC + stt * 4:WC + stt * 4 + 4, dc, :]
                        mu2 = te(lambda e, dc=dc, bu2=bu2, par=par, rhs=rhs:
                                 e.matmul(bu2[:, 0:STW], wgu[:, 2 * par + 1, dc, :], rhs,
                                          start=(dc == 0), stop=(dc == NDC - 1)),
                                 wait=(uwl2 if dc == 0 else ()), inc=(dc == NDC - 1))
                        if dc % 2 == 1:
                            yield
                    if stt == NST - 1:
                        gu_cons[par] = cnt["s_te"]
                        nxt = ei * NFC + fc + 2
                        if nxt < NEXP * NFC:
                            ne, nf = divmod(nxt, NFC)
                            sched_wdma("g", ne, nf)
                            sched_wdma("u", ne, nf)
                    t2 = et2[:, tp]; t2w = et2w[:, tp]
                    tg = etg[:, tp]; tu = etu[:, tp]
                    t1 = tg; t3 = t2w; t4 = tu; t5_ = tu
                    gbias = sbg[:, ei * NFC + fc: ei * NFC + fc + 1]
                    ubias = sbu[:, ei * NFC + fc: ei * NFC + fc + 1]
                    if weighted:
                        cg = sc(lambda e, tg=tg, bg_=bg_: e.copy(tg, bg_[:, 0:STW]),
                                wait=[("s_te", mg)])
                        cu = sc(lambda e, tu=tu, bu2=bu2: e.copy(tu, bu2[:, 0:STW]),
                                wait=[("s_te", mu2)])
                        ecg, ecu = ("s_sc", cg), ("s_sc", cu)
                    else:
                        cg = ve(lambda e, tg=tg, bg_=bg_: e.tensor_copy(tg, bg_[:, 0:STW]),
                                wait=[("s_te", mg)])
                        cu = ve(lambda e, tu=tu, bu2=bu2: e.tensor_copy(tu, bu2[:, 0:STW]),
                                wait=[("s_te", mu2)])
                        ecg, ecu = ("s_ve", cg), ("s_ve", cu)
                    ts_read[tp] = (ecg, ecu)
                    vg = ve(lambda e, t1=t1, tg=tg, gbias=gbias:
                            e.tensor_scalar(t1, tg, gbias, LIMIT, ALU.add, ALU.min),
                            wait=[ecg])
                    ve(lambda e, t4=t4, tu=tu, ubias=ubias:
                       e.tensor_scalar(t4, tu, ubias, LIMIT, ALU.add, ALU.min),
                       wait=[ecu])
                    ve(lambda e, t5_=t5_, t4=t4:
                       e.tensor_scalar(t5_, t4, -LIMIT, 1.0, ALU.max, ALU.add))
                    sg = sc(lambda e, t2=t2, t1=t1:
                            e.activation(t2, t1, ACT.Sigmoid, scale=ALPHA),
                            wait=[("s_ve", vg)])
                    if weighted:
                        wbs = wbc[:, epar, stt * STW:(stt + 1) * STW]
                        ve(lambda e, t2w=t2w, t2=t2, wbs=wbs:
                           e.tensor_mul(t2w, t2, wbs),
                           wait=[("s_sc", sg), ("s_sc", bc_ev_l[stt])])
                        ve(lambda e, t3=t3, t1=t1, t2w=t2w: e.tensor_mul(t3, t1, t2w))
                    else:
                        ve(lambda e, t3=t3, t1=t1, t2=t2: e.tensor_mul(t3, t1, t2),
                           wait=[("s_sc", sg)])
                    g2 = ve(lambda e, fc=fc, stt=stt, t5_=t5_, t3=t3:
                            e.tensor_mul(act_w[:, fc, stt * STW:(stt + 1) * STW],
                                         t5_, t3))
                    ew_done[(ei, fc, stt)] = g2


                pump_q = [(0, fc, stt) for fc in range(NFC) for stt in range(NST)]
                pump_cur = [None]
                ready_st = [-1]
                st_ready_ve = {}

                def pump(n):
                    k = 0
                    while k < n:
                        if pump_cur[0] is None:
                            if not pump_q:
                                return
                            ei_, fc_, stt_ = pump_q[0]
                            if stt_ > ready_st[0]:
                                return
                            pump_q.pop(0)
                            pump_cur[0] = gu_tile_gen(ei_, fc_, stt_, weighted=False)
                        try:
                            next(pump_cur[0])
                            k += 1
                        except StopIteration:
                            pump_cur[0] = None

                def pump_all():
                    for q in range(NST):
                        if q not in st_ready_ve:
                            st_ready_ve[q] = cnt["s_sc"]
                    ready_st[0] = NST - 1
                    while pump_q or pump_cur[0] is not None:
                        pump(8)

                NTCR = ntc_run if ntc_run else NTC
                lnA = {}
                lnB = {}

                def emitLoad(tc):
                    par3 = tc % 3
                    wl = []
                    if tc >= 3:
                        pv, ps = ln_free[par3]
                        wl = [("s_ve", pv), ("s_sc", ps)]
                    lxsem = "s_lx%d" % (tc % 2)
                    lxv = dma("sy", lxsem, xln[:, par3], x_srcs[tc], wait=wl)
                    lnA[tc] = (lxsem, lxv)

                def emitA(tc):
                    par = tc % 2
                    par3 = tc % 3
                    lxsem, lxv = lnA[tc]
                    xl = xln[:, par3]
                    ssum = stat[:, par * 8 + 0: par * 8 + 1]
                    negm = stat[:, par * 8 + 1: par * 8 + 2]
                    bb = stat[:, par * 8 + 5: par * 8 + 6]
                    var = stat[:, par * 8 + 2: par * 8 + 3]
                    ve(lambda e, xl=xl, ssum=ssum: e.reduce_sum(ssum, xl, AX),
                       wait=[(lxsem, lxv)])
                    ve(lambda e, ssum=ssum, negm=negm:
                       e.tensor_scalar_mul(negm, ssum, -1.0 / DM))
                    bwl2 = [("s_sc", bb_free[par])] if tc >= 2 else []
                    vb = ve(lambda e, negm=negm, bb=bb:
                            e.scalar_tensor_tensor(bb, negm, -1.0, negm,
                                                   ALU.mult, ALU.mult),
                            wait=bwl2)
                    vb = ve(lambda e, bb=bb: e.tensor_scalar_add(bb, bb, EPS))
                    s2 = sc(lambda e, xl=xl, var=var:
                            e.activation(otm[:, 0], xl, ACT.Square, accum_out=var),
                            wait=[(lxsem, lxv)])
                    lnB[tc] = (vb, s2)

                def emitB(tc):
                    par = tc % 2
                    par3 = tc % 3
                    xl = xln[:, par3]
                    xn = xnb[:, par]
                    negm = stat[:, par * 8 + 1: par * 8 + 2]
                    var = stat[:, par * 8 + 2: par * 8 + 3]
                    sq_ = stat[:, par * 8 + 3: par * 8 + 4]
                    inv = stat[:, par * 8 + 4: par * 8 + 5]
                    bb = stat[:, par * 8 + 5: par * 8 + 6]
                    vb, s2 = lnB[tc]
                    s3 = sc(lambda e, var=var, sq_=sq_, bb=bb:
                            e.activation(sq_, var, ACT.Sqrt, bias=bb,
                                         scale=1.0 / DM),
                            wait=[("s_ve", vb)])
                    bb_free[par] = s3
                    vwl = [("s_sc", s3)]
                    if tc >= 2:
                        vwl.append(("s_sc", inv_free[par]))
                    v3 = ve(lambda e, inv=inv, sq_=sq_: e.reciprocal(inv, sq_),
                            wait=vwl)
                    xwl = [("s_ve", v3)]
                    if tc >= 2:
                        xwl.append(("s_te", xnb_free[par]))
                    v4 = ve(lambda e, xn=xn, xl=xl, negm=negm, inv=inv:
                            e.tensor_scalar(xn, xl, negm, inv, ALU.add, ALU.mult),
                            wait=xwl)
                    inv_free[par] = cnt["s_sc"]
                    ln_free[par3] = (v4, s2)
                    lnB[tc] = v4

                def emitC(tc):
                    par = tc % 2
                    xn = xnb[:, par]
                    v4 = lnB[tc]
                    bank = pbt[par]
                    tw = [("s_ve", v4)]
                    if tc < 2:
                        tw.append(("s_lc2", lc_eye))
                    else:
                        tw.append(("s_sc", tr_evac[par]))
                    tvals = []
                    for dc in range(NDC):
                        t_ = te(lambda e, bank=bank, dc=dc, xn=xn:
                                e.transpose(bank[:, dc * 128:(dc + 1) * 128],
                                            xn[:, dc * 128:(dc + 1) * 128], sI16[:]),
                                wait=(tw if dc == 0 else ()), inc=(dc in (3, 7)))
                        tvals.append(t_)
                    xnb_free[par] = tvals[7]
                    sc(lambda e, bank=bank, tc=tc:
                       e.copy(xn_fm[:, tc, :, :], bank[:]),
                       wait=[("s_te", tvals[7])])
                    tr_evac[par] = cnt["s_sc"]

                twid = [512, 512, 512, 512, TW - 2048]
                d_evac = {}
                su_h, ss_h = {}, {}

                def emit_usp(t5):
                    wt_ = twid[t5]
                    c0 = t5 * 512
                    tc0 = c0 // 128
                    ntc_ = wt_ // 128
                    bu_, bs_, bd_ = pb[0], pb[1], pb[2]
                    uwl = (([("s_sc", usp_xn[t5])] + lc_wait) if t5 == 0
                           else [("s_sc", usp_xn[t5]), ("s_ve", d_evac[t5 - 1])])
                    mu = None
                    for dc in range(NDC):
                        rhs = xn_fm[:, tc0:tc0 + ntc_, dc, :]
                        mu = te(lambda e, dc=dc, bu_=bu_, rhs=rhs, wt_=wt_:
                                e.matmul(bu_[:, 0:wt_], sWiT[:, dc, :], rhs,
                                         start=(dc == 0), stop=(dc == NDC - 1)),
                                wait=(uwl if dc == 0 else ()), inc=(dc == NDC - 1))
                    su_h[t5] = sc(lambda e, bu_=bu_, c0=c0, wt_=wt_:
                                  e.activation(u_bf[:, c0:c0 + wt_], bu_[:, 0:wt_],
                                               ACT.Identity, bias=sbi[:]),
                                  wait=[("s_te", mu)])
                    swl2 = [] if t5 < 1 else [("s_sc", ss_h[t5 - 1])]
                    ms = None
                    for dc in range(NDC):
                        rhs = xn_fm[:, tc0:tc0 + ntc_, dc, :]
                        ms = te(lambda e, dc=dc, bs_=bs_, rhs=rhs, wt_=wt_:
                                e.matmul(bs_[:, 0:wt_], sWsT[:, dc, :], rhs,
                                         start=(dc == 0), stop=(dc == NDC - 1)),
                                wait=(swl2 if dc == 0 else ()), inc=(dc == NDC - 1))
                    ss_h[t5] = sc(lambda e, bs_=bs_, c0=c0, wt_=wt_:
                                  e.activation(SPb[:, 1 + c0:1 + c0 + wt_], bs_[:, 0:wt_],
                                               ACT.Identity, bias=sbs[:]),
                                  wait=[("s_te", ms)])
                    vs = ss_h[t5]
                    dwl = [("s_sc", vs), ("s_ve", v_init)]
                    if t5 >= 1:
                        dwl.append(("s_ve", d_evac[t5 - 1]))
                    te(lambda e, bd_=bd_, c0=c0, wt_=wt_:
                       e.matmul(bd_[:, 0:wt_], sAT[:], SPb[:, c0:c0 + wt_],
                                start=True, stop=False, skip_group_check=True),
                       wait=dwl, inc=False)
                    md2 = te(lambda e, bd_=bd_, c0=c0, wt_=wt_:
                             e.matmul(bd_[:, 0:wt_], sBT[:], u_bf[:, c0:c0 + wt_],
                                      start=False, stop=True, skip_group_check=True),
                             wait=[("s_sc", su_h[t5])])
                    pump(2)
                    if t5 == 0:
                        ve(lambda e, bd_=bd_: e.tensor_mul(Dt[:, 0:W], bd_[:, 0:W], sdm[:]),
                           wait=[("s_te", md2)])
                        d_evac[t5] = ve(lambda e, bd_=bd_:
                                        e.tensor_copy(Dt[:, W:512], bd_[:, W:512]))
                    else:
                        d_evac[t5] = ve(lambda e, bd_=bd_, c0=c0, wt_=wt_:
                                        e.tensor_copy(Dt[:, c0:c0 + wt_], bd_[:, 0:wt_]),
                                        wait=[("s_te", md2)])

                inv_free = {}
                bb_free = {}
                usp_xn = {}
                for col in range(NTCR + 3):
                    if col < NTCR:
                        emitLoad(col)
                        emitA(col)
                    if 1 <= col <= NTCR:
                        emitB(col - 1)
                    if 2 <= col < NTCR + 2:
                        emitC(col - 2)
                    if not ntc_run:
                        t5r = (col - 6) // 4
                        if 0 <= t5r <= 3 and (col - 6) % 4 == 0:
                            usp_xn[t5r] = cnt["s_sc"]
                            emit_usp(t5r)
                        if col == NTCR + 2:
                            usp_xn[4] = cnt["s_sc"]
                            emit_usp(4)
                ve_xn_all = cnt["s_ve"]

                if PH < 0: raise _Done()
                ve_xn_all0 = cnt["s_ve"]
                sc_xn_all0 = cnt["s_sc"]
                r_mm = {}
                r_s6 = {}
                r_t6 = {}
                r_v7 = {}

                def emitR1(mc):
                    rpar = mc % 2
                    tc = mc + WC
                    lbank = pb[4]
                    lg = lgb[:, rpar]
                    mmv = None
                    for dc in range(NDC):
                        lwl = []
                        if dc == 0:
                            if mc == 0:
                                lwl.append(("s_ve", ve_xn_all0))
                                lwl.append(("s_sc", sc_xn_all0))
                                lwl.extend(lc_wait)
                            else:
                                lwl.append(("s_sc", logit_evac[(mc - 1) % 2]))
                        mmv = te(lambda e, dc=dc, lbank=lbank, tc=tc:
                                 e.matmul(lbank[:, 0:8], xn_fm[:, tc, dc, :],
                                          sWr[:, dc, :],
                                          start=(dc == 0), stop=(dc == NDC - 1)),
                                 wait=lwl, inc=(dc == NDC - 1))
                    ewl = [("s_te", mmv)]
                    if mc >= 2:
                        ewl.append(("s_ve", r_v7[rpar]))
                    logit_evac[rpar] = sc(lambda e, lg=lg, lbank=lbank:
                                          e.copy(lg, lbank[:, 0:8]), wait=ewl)
                    r_mm[mc] = logit_evac[rpar]

                def emitR2(mc):
                    rpar = mc % 2
                    lg = lgb[:, rpar]
                    m1 = rt[:, rpar, 0:1]; nm1 = rt[:, rpar, 1:2]
                    m2 = rt[:, rpar, 2:3]; ssum2 = rt[:, rpar, 3:4]
                    sinv = rt[:, rpar, 4:5]; eq = rt[:, rpar, 5:13]
                    vwl = [("s_sc", r_mm[mc])]
                    if mc >= 2:
                        vwl.append(("s_sc", r_s6[rpar]))
                    v4 = ve(lambda e, m1=m1, lg=lg: e.reduce_max(m1, lg, AX),
                            wait=vwl)
                    ve(lambda e, nm1=nm1, m1=m1: e.tensor_scalar_mul(nm1, m1, -1.0),
                       wait=[("s_ve", v4)])
                    ve(lambda e, eq=eq, lg=lg, m1=m1:
                       e.tensor_scalar(eq, lg, m1, None, ALU.is_equal))
                    ve(lambda e, eq=eq, lg=lg:
                       e.scalar_tensor_tensor(eq, eq, -1e30, lg, ALU.mult, ALU.add))
                    vm2 = ve(lambda e, m2=m2, eq=eq: e.reduce_max(m2, eq, AX))
                    v6 = ve(lambda e, eq=eq, lg=lg, m2=m2:
                            e.tensor_scalar(eq, lg, m2, None, ALU.is_ge),
                            wait=[("s_ve", vm2)])
                    s5 = sc(lambda e, lg=lg, nm1=nm1:
                            e.activation(lg, lg, ACT.Exp, bias=nm1),
                            wait=[("s_ve", v6)])
                    v7 = ve(lambda e, lg=lg, eq=eq: e.tensor_mul(lg, eq, lg),
                            wait=[("s_sc", s5)])
                    v8 = ve(lambda e, ssum2=ssum2, lg=lg: e.reduce_sum(ssum2, lg, AX))
                    v9 = ve(lambda e, sinv=sinv, ssum2=ssum2: e.reciprocal(sinv, ssum2),
                            wait=[("s_ve", v8)])
                    wt = wtm[:, rpar]
                    swl = [("s_ve", v9)]
                    if mc >= 2:
                        swl.append(("s_te", r_t6[rpar]))
                    r_s6[rpar] = sc(lambda e, wt=wt, lg=lg, sinv=sinv:
                                    e.activation(wt, lg, ACT.Identity, scale=sinv),
                                    wait=swl)
                    r_v7[rpar] = v9

                def emitR3(mc):
                    rpar = mc % 2
                    wt = wtm[:, rpar]
                    tbank = pb[5]
                    t6w = [("s_sc", r_s6[rpar])]
                    if mc >= 1:
                        t6w.append(("s_sc", router_evac[(mc - 1) % 2]))
                    t6 = te(lambda e, tbank=tbank, wt=wt:
                            e.transpose(tbank[0:8, 0:128], wt, sI32[:]), wait=t6w)
                    r_t6[rpar] = t6
                    router_evac[rpar] = sc(
                        lambda e, mc=mc, tbank=tbank:
                        e.copy(w_fmb[0:8, mc * 128:(mc + 1) * 128], tbank[0:8, 0:128]),
                        wait=[("s_te", t6)])

                for q in range(NST):
                    if q not in st_ready_ve:
                        st_ready_ve[q] = cnt["s_sc"]
                ready_st[0] = NST - 1
                for col in range(NMC + 2):
                    if col >= 2:
                        emitR3(col - 2)
                    if 1 <= col <= NMC:
                        emitR2(col - 1)
                    if col < NMC:
                        emitR1(col)

                ve_xn_all = cnt["s_ve"]
                sc_r_all = cnt["s_sc"]

                bc0_ev = {}
                for stt in range(NST):
                    bk = pb[4 + stt % 2]
                    bwl = (([("s_sc", sc_r_all)] + (lc_wait if stt == 0 else []))
                           if stt < 2 else [("s_sc", bc0_ev[stt - 2])])
                    m_ = te(lambda e, bk=bk, stt=stt:
                            e.matmul(bk[:, 0:STW], ssel[:, 0:128],
                                     w_fmb[:, stt * STW:(stt + 1) * STW],
                                     start=True, stop=True),
                            wait=bwl)
                    bc0_ev[stt] = sc(
                        lambda e, bk=bk, stt=stt:
                        e.copy(wbc[:, 0, stt * STW:(stt + 1) * STW], bk[:, 0:STW]),
                        wait=[("s_te", m_)])
                    pump(2)
                bc0_done = cnt["s_sc"]


                ve_d_all = cnt["s_ve"]

                if PH < 2: raise _Done()
                ch_act = {}
                for s in range(NSTEP):
                    pump(2)
                    b = s % 2
                    if b == 0:
                        for k in range(2):
                            sk = s + k
                            if sk >= NSTEP:
                                break
                            wl = []
                            if k == 0:
                                wl = [("s_ve", ve_d_all),
                                      ("s_sc", ch_act[s - 1] if s > 0 else cnt["s_sc"])]
                            te(lambda e, sk=sk, bk=pb[4 + sk % 2]:
                               e.matmul(bk[:, 0:NCH], sI16[:],
                                        Dt[:, WOFF + sk:WOFF + sk + SSTR:LCH],
                                        start=True, stop=False, skip_group_check=True),
                               wait=wl, inc=False)
                    if s == 0:
                        rhs = gz[:]
                    elif s <= WCH:
                        rhs = gw[:, ((s - 1) % 2) * NCH:((s - 1) % 2) * NCH + NCH]
                    else:
                        t0 = s - 1 - WCH
                        rhs = G[:, t0:t0 + SSTR:LCH]
                    wl = [("s_sc", ch_act[s - 1])] if s > 0 else []
                    mv = te(lambda e, b=b, rhs=rhs:
                            e.matmul(pb[4 + b][:, 0:NCH], sAT[:], rhs,
                                     start=False, stop=True, skip_group_check=True),
                            wait=wl)
                    if s < WCH:
                        dst = gw[:, (s % 2) * NCH:(s % 2) * NCH + NCH]
                    else:
                        t0 = s - WCH
                        dst = G[:, t0:t0 + SSTR:LCH]
                    ch_act[s] = sc(lambda e, b=b, dst=dst:
                                   e.activation(dst, pb[4 + b][:, 0:NCH], ACT.Tanh),
                                   wait=[("s_te", mv)])
                pump_all()
                sc_chain = cnt["s_sc"]

                if PH < 3: raise _Done()
                vstate = {}
                for stt in range(NST):
                    c0 = stt * STW
                    vstate[stt] = ve(lambda e, c0=c0:
                                     e.tensor_add(u_bf[:, c0:c0 + STW], G[:, c0:c0 + STW],
                                                  SPb[:, 1 + W + c0:1 + W + c0 + STW]),
                                     wait=([("s_sc", sc_chain)] if stt == 0 else ()))
                ys_evac = {}
                for stt in range(NST):
                    bk = pb[4 + stt % 2]
                    ywl = [("s_ve", vstate[stt])]
                    ywl.append(("s_sc", sc_chain if stt < 2 else ys_evac[stt - 2]))
                    m_ = te(lambda e, bk=bk, stt=stt:
                            e.matmul(bk[:, 0:STW], sCT[:], u_bf[:, stt * STW:(stt + 1) * STW],
                                     start=True, stop=True),
                            wait=ywl)
                    ys_evac[stt] = sc(lambda e, bk=bk, stt=stt:
                                      e.copy(ys_bf[:, stt * STW:(stt + 1) * STW], bk[:, 0:STW]),
                                      wait=[("s_te", m_)])
                sc_ys = cnt["s_sc"]

                attn_ev = {}
                for hc in range(NDC):
                    for stt in range(NST):
                        idx = hc * NST + stt
                        bk = pb[4 + idx % 2]
                        wl = [("s_sc", sc_ys if idx < 2 else attn_ev[idx - 2])]
                        te(lambda e, bk=bk, hc=hc, stt=stt:
                           e.matmul(bk[:, 0:STW], sWoT[:, hc, :],
                                    ys_bf[:, stt * STW:(stt + 1) * STW],
                                    start=True, stop=False, skip_group_check=True),
                           wait=wl, inc=False)
                        m_ = te(lambda e, bk=bk, hc=hc, stt=stt:
                                e.matmul(bk[:, 0:STW], sdb[:, hc * 128:(hc + 1) * 128],
                                         w_fmb[:, stt * STW:(stt + 1) * STW],
                                         start=False, stop=True, skip_group_check=True))
                        attn_ev[idx] = sc(
                            lambda e, bk=bk, hc=hc, stt=stt:
                            e.activation(acc[:, hc, stt * STW:(stt + 1) * STW],
                                         bk[:, 0:STW], ACT.Identity, bias=sbo[:, hc:hc + 1]),
                            wait=[("s_te", m_)])
                sc_attn = cnt["s_sc"]
                for stt in range(NST):
                    for fc in range(NFC):
                        fwl = [("s_sc", bc0_done)] if (stt == 0 and fc == 0) else []
                        ew_done[(0, fc, stt)] = ve(
                            lambda e, fc=fc, stt=stt:
                            e.tensor_mul(act_w[:, fc, stt * STW:(stt + 1) * STW],
                                         act_w[:, fc, stt * STW:(stt + 1) * STW],
                                         wbc[:, 0, stt * STW:(stt + 1) * STW]),
                            wait=fwl)
                wbc_cons0 = cnt["s_ve"]

                fin_add = {}
                out_done = {}

                fxload = {}

                def emit_final_load(mc):
                    par4 = mc % 4
                    fxsem = "s_lf%d" % (mc % 2)
                    xwl = [("s_ve", fin_add[par4])] if mc >= 4 else []
                    fxload[mc] = (fxsem,
                                  dma("gp", fxsem, xres[:, par4],
                                      d_xm[mc * 128:(mc + 1) * 128, :], wait=xwl))

                def emit_final_group(stt):
                    if stt == 0:
                        for mc_ in range(4):
                            emit_final_load(mc_)
                    for mc in range(stt * 4, stt * 4 + 4):
                        par = mc % 2
                        par4 = mc % 4
                        fosem = "s_ou%d" % par
                        fxsem, lxv = fxload[mc]
                        bank = pbt[par]
                        tvals = []
                        for hc in range(NDC):
                            wl2 = []
                            if hc == 0:
                                wl2 = [("s_ve", acc_add[(NDC - 1, stt)])]
                                wl2.append(("s_ve", fin_add.get(par, cnt["s_ve"])
                                            if mc >= 2 else cnt["s_ve"]))
                            t_ = te(lambda e, bank=bank, hc=hc, mc=mc:
                                    e.transpose(bank[:, hc * 128:(hc + 1) * 128],
                                                acc[:, hc, mc * 128:(mc + 1) * 128], sI16[:]),
                                    wait=wl2, inc=(hc in (3, 7)))
                            tvals.append(t_)
                        vwl = [("s_te", tvals[7]), (fxsem, lxv)]
                        if mc >= 2:
                            vwl.append((fosem, out_done[par]))
                        ve(lambda e, bank=bank, par=par, par4=par4:
                           e.tensor_add(otm[:, par, :], bank[:], xres[:, par4, :]),
                           wait=vwl)
                        fin_add[par] = cnt["s_ve"]
                        fin_add[par4] = cnt["s_ve"]
                        if mc + 4 < NMC:
                            emit_final_load(mc + 4)
                        out_done[par] = dma("sy", fosem, d_y[mc * 128:(mc + 1) * 128, :],
                                            otm[:, par], wait=[("s_ve", fin_add[par])])


                if PH < 4: raise _Done()
                acc_add = {}
                wbc_cons = {0: wbc_cons0}
                dn_read = {}
                for ei in range(NEXP):
                    epar = ei % 2
                    if ei == 0:
                        bc_ev_l = bc0_ev
                    else:
                     bc_ev_l = {}
                     for stt in range(NST):
                        bk = pb[4 + stt % 2]
                        bwl = []
                        if stt < 2:
                            if ei == 0:
                                bwl.append(("s_sc", sc_attn))
                            else:
                                bwl.append(("s_ve", dn_read[stt % 2]))
                            if ei >= 2:
                                bwl.append(("s_ve", wbc_cons[epar]))
                        else:
                            bwl.append(("s_sc", bc_ev_l[stt - 2]))
                        m_ = te(lambda e, bk=bk, ei=ei, stt=stt:
                                e.matmul(bk[:, 0:STW], ssel[:, ei * 128:(ei + 1) * 128],
                                         w_fmb[:, stt * STW:(stt + 1) * STW],
                                         start=True, stop=True),
                                wait=bwl)
                        bc_ev_l[stt] = sc(
                            lambda e, bk=bk, epar=epar, stt=stt:
                            e.copy(wbc[:, epar, stt * STW:(stt + 1) * STW], bk[:, 0:STW]),
                            wait=[("s_te", m_)])

                    if ei > 0:
                        for fc in range(NFC):
                            for stt in range(NST):
                                emit_gu_tile(ei, fc, stt, weighted=True,
                                             bc_ev_l=bc_ev_l)
                        wbc_cons[epar] = cnt["s_ve"]

                    for fc in range(NFC):
                        par = (ei * NFC + fc) % 2
                        ldsem, lwd = wload[("d", ei, fc)]
                        for stt in range(NST):
                            idx = ei * NFC * NST + fc * NST + stt
                            dp = (fc * NST + stt) % 2
                            bk = pb[4 + dp]
                            wl = [(ldsem, lwd), ("s_ve", ew_done[(ei, NFC - 1, stt)])]
                            if ei == 0 and fc == 0 and stt < 2:
                                wl.append(("s_sc", sc_attn))
                            elif fc == 0 and stt < 2:
                                wl.append(("s_sc", bc_ev_l[stt + 2]))
                            elif idx >= 2:
                                wl.append(("s_ve", dn_read[dp]))
                            md = None
                            for ic in range(NDC):
                                md = te(lambda e, ic=ic, bk=bk, par=par, stt=stt:
                                        e.matmul(bk[:, 0:STW], wdb_[:, par, ic, :],
                                                 act_w[:, ic, stt * STW:(stt + 1) * STW],
                                                 start=(ic == 0), stop=(ic == NDC - 1),
                                                 skip_group_check=True),
                                        wait=(wl if ic == 0 else ()),
                                        inc=(ic == NDC - 1))
                            if stt == NST - 1:
                                wd_cons[par] = cnt["s_te"]
                                nxt = ei * NFC + fc + 2
                                if nxt < NEXP * NFC:
                                    ne, nf = divmod(nxt, NFC)
                                    sched_wdma("d", ne, nf)
                            awl = [("s_te", md)]
                            if ei == 0:
                                awl.append(("s_sc", attn_ev[fc * NST + stt]))
                            va = ve(lambda e, fc=fc, stt=stt, bk=bk:
                                    e.tensor_add(acc[:, fc, stt * STW:(stt + 1) * STW],
                                                 acc[:, fc, stt * STW:(stt + 1) * STW],
                                                 bk[:, 0:STW]),
                                    wait=awl)
                            dn_read[dp] = va
                            acc_add[(fc, stt)] = va
                            if PH >= 5 and ei == NEXP - 1 and fc == NFC - 1:
                                emit_final_group(stt)

            except _Done:
                pass

            if debug:
                vw_, sw_, tw_ = cnt["s_ve"], cnt["s_sc"], cnt["s_te"]
                def gdbg(name, src_ap):
                    dma("gp", "s_ou0", dbg[name], src_ap,
                        wait=[("s_ve", vw_), ("s_sc", sw_), ("s_te", tw_)])
                gdbg("xnfm", xn_fm[:].rearrange("p a b c -> p (a b c)"))
                gdbg("wfm", w_fmb[:])
                if PH >= 1:
                    gdbg("D", Dt[:]); gdbg("u", u_bf[:]); gdbg("sp", SPb[:, 1:])
                if PH >= 2:
                    gdbg("G", G[:])
                if PH >= 3:
                    gdbg("ys", ys_bf[:])
                    gdbg("acc", acc[:].rearrange("p a b -> p (a b)"))

            fin_o0 = cnt.get("s_ou0", 0)
            fin_o1 = cnt.get("s_ou1", 0)
            def fin(e):
                if fin_o0:
                    e.wait_ge(s_ou0, fin_o0)
                if fin_o1:
                    e.wait_ge(s_ou1, fin_o1)
            ops["sy"].append(fin)

            @block.gpsimd
            def _(g):
                for f in ops["gp"]:
                    f(g)

            @block.sync
            def _(s_):
                for f in ops["sy"]:
                    f(s_)

            @block.tensor
            def _(t_):
                for f in ops["te"]:
                    f(t_)

            @block.vector
            def _(v_):
                for f in ops["ve"]:
                    f(v_)

            @block.scalar
            def _(sc_):
                for f in ops["sc"]:
                    f(sc_)
    return nc


def prep_inputs(L0_attn_x, L0_attn_Wi, L0_attn_bi, L0_attn_Ws, L0_attn_bs,
                L0_attn_Wo, L0_attn_bo, L0_attn_A, L0_attn_B, L0_attn_C,
                L0_ffn_router_weight, L0_ffn_gate_up_proj, L0_ffn_gate_up_proj_bias,
                L0_ffn_down_proj, L0_ffn_down_proj_bias):
    f = lambda a: np.ascontiguousarray(np.asarray(a), dtype=np.float32)
    x = f(L0_attn_x).reshape(16384, DM)
    Wi, Ws, Wo = f(L0_attn_Wi), f(L0_attn_Ws), f(L0_attn_Wo)
    A, B, C = f(L0_attn_A), f(L0_attn_B), f(L0_attn_C)
    bi, bs, bo = f(L0_attn_bi), f(L0_attn_bs), f(L0_attn_bo)
    Wr = f(L0_ffn_router_weight)
    gup, gub = f(L0_ffn_gate_up_proj), f(L0_ffn_gate_up_proj_bias)
    dwn, dwb = f(L0_ffn_down_proj), f(L0_ffn_down_proj_bias)

    com = {}
    com["eye"] = np.eye(128, dtype=np.float32)
    com["AT"] = np.ascontiguousarray(A.T)
    com["BT"] = np.ascontiguousarray(B.T)
    com["CT"] = np.ascontiguousarray(C.T)
    com["WiT"] = np.ascontiguousarray(Wi.T.reshape(NDC, 128, 128).transpose(1, 0, 2))
    com["WsT"] = np.ascontiguousarray(Ws.T.reshape(NDC, 128, 128).transpose(1, 0, 2))
    com["WoT"] = np.ascontiguousarray(Wo.T.reshape(128, NDC, 128))
    com["Wr"] = np.ascontiguousarray(Wr.reshape(NDC, 128, 8).transpose(1, 0, 2))
    com["bi"] = bi.reshape(128, 1)
    com["bs"] = bs.reshape(128, 1)
    com["bo"] = np.ascontiguousarray(bo.reshape(NDC, 128).T)
    gate = np.ascontiguousarray(gup[:, :, 0::2])
    up = np.ascontiguousarray(gup[:, :, 1::2])
    com["Wg"] = np.ascontiguousarray(
        gate.reshape(NEXP, NDC, 128, NFC, 128).transpose(0, 3, 2, 1, 4))
    com["Wu"] = np.ascontiguousarray(
        up.reshape(NEXP, NDC, 128, NFC, 128).transpose(0, 3, 2, 1, 4))
    com["Wd"] = np.ascontiguousarray(
        dwn.reshape(NEXP, NDC, 128, NFC, 128).transpose(0, 3, 2, 1, 4))
    gb = gub[:, 0::2].reshape(NEXP, NFC, 128)
    ub = gub[:, 1::2].reshape(NEXP, NFC, 128)
    com["bg"] = np.ascontiguousarray(gb.reshape(NEXP * NFC, 128).T)
    com["bu"] = np.ascontiguousarray(ub.reshape(NEXP * NFC, 128).T)
    com["dbr"] = np.ascontiguousarray(dwb)
    com["on1"] = np.ones((1, 128), np.float32)
    sel = np.zeros((NEXP, NEXP * 128), np.float32)
    for e_ in range(NEXP):
        sel[e_, e_ * 128:(e_ + 1) * 128] = 1.0
    com["sel"] = sel

    in_maps = []
    for c in range(8):
        m = dict(com)
        r0 = c * NTOK
        m["xm"] = np.ascontiguousarray(x[r0:r0 + NTOK])
        if c % 4 == 0:
            m["xw"] = np.zeros((W, DM), np.float32)
            m["dmask"] = np.zeros((128, W), np.float32)
        else:
            m["xw"] = np.ascontiguousarray(x[r0 - W:r0])
            m["dmask"] = np.ones((128, W), np.float32)
        in_maps.append(m)
    return in_maps


_cache = {}


def kernel(**inputs):
    in_maps = prep_inputs(**inputs)
    if "nc" not in _cache:
        _cache["nc"] = build_nc(debug=False)
    res = run_bass_kernel_spmd(_cache["nc"], in_maps, core_ids=list(range(8)))
    out = np.concatenate([np.asarray(res.results[c]["y"]) for c in range(8)], axis=0)
    return out.reshape(2, 8192, DM)


# version 23
# speedup vs baseline: 1.0709x; 1.0235x over previous
import numpy as np
import concourse.bass as bass
import concourse.mybir as mybir
from concourse.bass_utils import run_bass_kernel_spmd

F32 = mybir.dt.float32
BF16 = mybir.dt.bfloat16
AX = mybir.AxisListType.X
ALU = mybir.AluOpType
ACT = mybir.ActivationFunctionType

DM = 1024
DS = 128
NTOK = 2048
W = 128
LCH = 32
NCH = NTOK // LCH
WCH = 96
WOFF = W - WCH
NSTEP = WCH + LCH
TW = W + NTOK
NDC = DM // 128
NTC = TW // 128
WC = W // 128
NMC = NTOK // 128
NEXP = 8
NFC = 8
NST = 4
STW = 512
EPS = 1e-5
ALPHA = 1.702
LIMIT = 7.0
SSTR = LCH * (NCH - 1) + 1


def build_nc(debug=False, upto="full", ntc_run=None, skip_big=False):
    import os as _os
    LNL = int(_os.environ.get("MK_LN_LEVEL", "9"))
    PH = {"ln0": -1, "ln": 0, "usp": 1, "chain": 2, "attn": 3, "ffn": 4, "full": 5}[upto]
    nc = bass.Bass("TRN2", target_bir_lowering=False)

    d_xm = nc.dram_tensor("xm", [NTOK, DM], F32, kind="ExternalInput").ap()
    d_xw = nc.dram_tensor("xw", [W, DM], F32, kind="ExternalInput").ap()
    d_dmask = nc.dram_tensor("dmask", [128, W], F32, kind="ExternalInput").ap()
    d_eye = nc.dram_tensor("eye", [128, 128], F32, kind="ExternalInput").ap()
    d_AT = nc.dram_tensor("AT", [128, 128], F32, kind="ExternalInput").ap()
    d_BT = nc.dram_tensor("BT", [128, 128], F32, kind="ExternalInput").ap()
    d_CT = nc.dram_tensor("CT", [128, 128], F32, kind="ExternalInput").ap()
    d_WiT = nc.dram_tensor("WiT", [128, NDC, 128], F32, kind="ExternalInput").ap()
    d_WsT = nc.dram_tensor("WsT", [128, NDC, 128], F32, kind="ExternalInput").ap()
    d_WoT = nc.dram_tensor("WoT", [128, NDC, 128], F32, kind="ExternalInput").ap()
    d_Wr = nc.dram_tensor("Wr", [128, NDC, 8], F32, kind="ExternalInput").ap()
    d_bi = nc.dram_tensor("bi", [128, 1], F32, kind="ExternalInput").ap()
    d_bs = nc.dram_tensor("bs", [128, 1], F32, kind="ExternalInput").ap()
    d_bo = nc.dram_tensor("bo", [128, 8], F32, kind="ExternalInput").ap()
    d_bg = nc.dram_tensor("bg", [128, NEXP * NFC], F32, kind="ExternalInput").ap()
    d_bu = nc.dram_tensor("bu", [128, NEXP * NFC], F32, kind="ExternalInput").ap()
    d_dbr = nc.dram_tensor("dbr", [NEXP, DM], F32, kind="ExternalInput").ap()
    d_on1 = nc.dram_tensor("on1", [1, 128], F32, kind="ExternalInput").ap()
    d_sel = nc.dram_tensor("sel", [NEXP, NEXP * 128], F32, kind="ExternalInput").ap()
    if not skip_big:
        d_Wg = nc.dram_tensor("Wg", [NEXP, NFC, 128, NDC, 128], F32, kind="ExternalInput").ap()
        d_Wu = nc.dram_tensor("Wu", [NEXP, NFC, 128, NDC, 128], F32, kind="ExternalInput").ap()
        d_Wd = nc.dram_tensor("Wd", [NEXP, NFC, 128, NDC, 128], F32, kind="ExternalInput").ap()
    d_y = nc.dram_tensor("y", [NTOK, DM], F32, kind="ExternalOutput").ap()
    dbg = {}
    if debug:
        dbg["xnfm"] = nc.dram_tensor("dbg_xnfm", [128, NTC * DM], F32, kind="ExternalOutput").ap()
        dbg["D"] = nc.dram_tensor("dbg_D", [128, TW], F32, kind="ExternalOutput").ap()
        dbg["G"] = nc.dram_tensor("dbg_G", [128, NTOK], F32, kind="ExternalOutput").ap()
        dbg["wfm"] = nc.dram_tensor("dbg_wfm", [8, NTOK], F32, kind="ExternalOutput").ap()
        dbg["ys"] = nc.dram_tensor("dbg_ys", [128, NTOK], F32, kind="ExternalOutput").ap()
        dbg["acc"] = nc.dram_tensor("dbg_acc", [128, NDC * NTOK], F32, kind="ExternalOutput").ap()
        dbg["u"] = nc.dram_tensor("dbg_u", [128, TW], F32, kind="ExternalOutput").ap()
        dbg["sp"] = nc.dram_tensor("dbg_sp", [128, TW], F32, kind="ExternalOutput").ap()

    ops = {k: [] for k in ("te", "ve", "sc", "gp", "sy")}
    cnt = {}

    def _inc(sem, n=1):
        cnt[sem] = cnt.get(sem, 0) + n
        return cnt[sem]

    from contextlib import ExitStack
    with ExitStack() as _es:
        xn_fm = _es.enter_context(nc.sbuf_tensor("xn_fm", [128, NTC, NDC, 128], BF16))
        SPb = _es.enter_context(nc.sbuf_tensor("SPb", [128, 1 + TW], BF16))
        u_bf = _es.enter_context(nc.sbuf_tensor("u_bf", [128, TW], BF16))
        Dt = _es.enter_context(nc.sbuf_tensor("Dt", [128, TW], BF16))
        G = _es.enter_context(nc.sbuf_tensor("G", [128, NTOK], BF16))
        gw = _es.enter_context(nc.sbuf_tensor("gw", [128, 2 * NCH], BF16))
        gz = _es.enter_context(nc.sbuf_tensor("gz", [128, NCH], BF16))
        ys_bf = _es.enter_context(nc.sbuf_tensor("ys_bf", [128, NTOK], BF16))
        w_fmb = _es.enter_context(nc.sbuf_tensor("w_fmb", [8, NTOK], BF16))
        acc = _es.enter_context(nc.sbuf_tensor("acc", [128, NDC, NTOK], BF16))
        act_w = _es.enter_context(nc.sbuf_tensor("act_w", [128, NFC, NTOK], BF16))
        wbc = _es.enter_context(nc.sbuf_tensor("wbc", [128, 2, NTOK], BF16))
        sAT = _es.enter_context(nc.sbuf_tensor("sAT", [128, 128], BF16))
        sBT = _es.enter_context(nc.sbuf_tensor("sBT", [128, 128], BF16))
        sCT = _es.enter_context(nc.sbuf_tensor("sCT", [128, 128], BF16))
        sI16 = _es.enter_context(nc.sbuf_tensor("sI16", [128, 128], BF16))
        sI32 = _es.enter_context(nc.sbuf_tensor("sI32", [128, 128], F32))
        sWiT = _es.enter_context(nc.sbuf_tensor("sWiT", [128, NDC, 128], BF16))
        sWsT = _es.enter_context(nc.sbuf_tensor("sWsT", [128, NDC, 128], BF16))
        sWoT = _es.enter_context(nc.sbuf_tensor("sWoT", [128, NDC, 128], BF16))
        sWr = _es.enter_context(nc.sbuf_tensor("sWr", [128, NDC, 8], BF16))
        sbi = _es.enter_context(nc.sbuf_tensor("sbi", [128, 1], F32))
        sbs = _es.enter_context(nc.sbuf_tensor("sbs", [128, 1], F32))
        sbo = _es.enter_context(nc.sbuf_tensor("sbo", [128, 8], F32))
        sbg = _es.enter_context(nc.sbuf_tensor("sbg", [128, NEXP * NFC], F32))
        sbu = _es.enter_context(nc.sbuf_tensor("sbu", [128, NEXP * NFC], F32))
        sdb = _es.enter_context(nc.sbuf_tensor("sdb", [NEXP, DM], BF16))
        son1 = _es.enter_context(nc.sbuf_tensor("son1", [1, 128], BF16))
        ssel = _es.enter_context(nc.sbuf_tensor("ssel", [NEXP, NEXP * 128], BF16))
        sdm = _es.enter_context(nc.sbuf_tensor("sdm", [128, W], F32))
        xln = _es.enter_context(nc.sbuf_tensor("xln", [128, 3, DM], F32))
        xnb = _es.enter_context(nc.sbuf_tensor("xnb", [128, 2, DM], BF16))
        stat = _es.enter_context(nc.sbuf_tensor("stat", [128, 16], F32))
        lgb = _es.enter_context(nc.sbuf_tensor("lgb", [128, 2, 8], F32))
        rt = _es.enter_context(nc.sbuf_tensor("rt", [128, 2, 16], F32))
        wtm = _es.enter_context(nc.sbuf_tensor("wtm", [128, 2, 8], F32))
        rsg = _es.enter_context(nc.sbuf_tensor("rsg", [128, 4, NMC], F32))
        eqmb = _es.enter_context(nc.sbuf_tensor("eqmb", [128, NMC, 8], BF16))
        eq2b = _es.enter_context(nc.sbuf_tensor("eq2b", [128, NMC, 8], BF16))
        wgu = _es.enter_context(nc.sbuf_tensor("wgu", [128, 4, NDC, 128], BF16))
        wdb_ = _es.enter_context(nc.sbuf_tensor("wdb_", [128, 2, NFC, 128], BF16))
        et2 = _es.enter_context(nc.sbuf_tensor("et2", [128, 2, STW], BF16))
        et2w = _es.enter_context(nc.sbuf_tensor("et2w", [128, 2, STW], BF16))
        etg = _es.enter_context(nc.sbuf_tensor("etg", [128, 2, STW], BF16))
        etu = _es.enter_context(nc.sbuf_tensor("etu", [128, 2, STW], BF16))
        xres = _es.enter_context(nc.sbuf_tensor("xres", [128, 4, DM], BF16))
        otm = _es.enter_context(nc.sbuf_tensor("otm", [128, 2, DM], F32))
        pb0 = _es.enter_context(nc.psum_tensor([128, 512], F32))
        pb1 = _es.enter_context(nc.psum_tensor([128, 512], F32))
        pb2 = _es.enter_context(nc.psum_tensor([128, 512], F32))
        pb3 = _es.enter_context(nc.psum_tensor([128, 512], F32))
        pb4 = _es.enter_context(nc.psum_tensor([128, 512], F32))
        pb5 = _es.enter_context(nc.psum_tensor([128, 512], F32))
        pbtA = _es.enter_context(nc.psum_tensor([128, 1024], BF16))
        pbtB = _es.enter_context(nc.psum_tensor([128, 1024], BF16))
        s_lc = _es.enter_context(nc.semaphore("s_lc"))
        s_lc2 = _es.enter_context(nc.semaphore("s_lc2"))
        s_lc3 = _es.enter_context(nc.semaphore("s_lc3"))
        s_lw0 = _es.enter_context(nc.semaphore("s_lw0"))
        s_lw1 = _es.enter_context(nc.semaphore("s_lw1"))
        s_ld0 = _es.enter_context(nc.semaphore("s_ld0"))
        s_ld1 = _es.enter_context(nc.semaphore("s_ld1"))
        s_lx0 = _es.enter_context(nc.semaphore("s_lx0"))
        s_lx1 = _es.enter_context(nc.semaphore("s_lx1"))
        s_lf0 = _es.enter_context(nc.semaphore("s_lf0"))
        s_lf1 = _es.enter_context(nc.semaphore("s_lf1"))
        s_te = _es.enter_context(nc.semaphore("s_te"))
        s_ve = _es.enter_context(nc.semaphore("s_ve"))
        s_sc = _es.enter_context(nc.semaphore("s_sc"))
        s_ou0 = _es.enter_context(nc.semaphore("s_ou0"))
        s_ou1 = _es.enter_context(nc.semaphore("s_ou1"))
        s_gp = _es.enter_context(nc.semaphore("s_gp"))
        block = _es.enter_context(nc.Block())
        if True:
            pb = [pb0, pb1, pb2, pb3, pb4, pb5]
            pbt = [pbtA, pbtB]
            sems = {"s_lc": s_lc, "s_lc2": s_lc2, "s_lc3": s_lc3, "s_lw0": s_lw0, "s_lw1": s_lw1,
                    "s_ld0": s_ld0, "s_ld1": s_ld1,
                    "s_lx0": s_lx0, "s_lx1": s_lx1, "s_lf0": s_lf0, "s_lf1": s_lf1, "s_te": s_te,
                    "s_ve": s_ve, "s_sc": s_sc, "s_ou0": s_ou0, "s_ou1": s_ou1,
                    "s_gp": s_gp}

            def mk(eng_key, sem_name, self_wait=False):
                def emitfn(fn, wait=(), inc=True):
                    for sn, val in wait:
                        assert val <= cnt.get(sn, 0), (sn, val, cnt.get(sn, 0), eng_key)
                    w = tuple(wait)
                    if self_wait and cnt.get(sem_name, 0) > 0:
                        w = ((sem_name, cnt[sem_name]),) + w
                    v = _inc(sem_name) if inc else None
                    def op(e, fn=fn, w=w, v=v, sem_name=sem_name):
                        for sn, val in w:
                            e.wait_ge(sems[sn], val)
                        ins = fn(e)
                        if v is not None:
                            ins.then_inc(sems[sem_name], 1)
                    ops[eng_key].append(op)
                    return cnt.get(sem_name, 0)
                return emitfn

            te = mk("te", "s_te")
            ve = mk("ve", "s_ve", self_wait=True)
            sc = mk("sc", "s_sc", self_wait=True)
            gpc = mk("gp", "s_gp", self_wait=True)

            class _Done(Exception):
                pass

            def dma(eng_key, sem_name, dst, src, wait=()):
                for sn, val in wait:
                    assert val <= cnt.get(sn, 0), (sn, val, cnt.get(sn, 0), "dma")
                w = tuple(wait)
                if cnt.get(sem_name, 0) > 0:
                    w = ((sem_name, cnt[sem_name]),) + w
                _inc(sem_name, 16)
                def op(e, dst=dst, src=src, w=w, sem_name=sem_name):
                    for sn, val in w:
                        e.wait_ge(sems[sn], val)
                    e.dma_start(dst, src).then_inc(sems[sem_name], 16)
                ops[eng_key].append(op)
                return cnt[sem_name]

            try:
                dma("gp", "s_lc2", sI16[:], d_eye)
                lc_eye = dma("gp", "s_lc2", sI32[:], d_eye)
                for j, (dst, csrc) in enumerate(
                        [(sAT[:], d_AT), (sBT[:], d_BT), (sCT[:], d_CT),
                         (sWiT[:], d_WiT), (sWsT[:], d_WsT), (sWoT[:], d_WoT),
                         (sWr[:], d_Wr), (sbi[:], d_bi), (sbs[:], d_bs),
                         (sbo[:], d_bo), (sbg[:], d_bg), (sbu[:], d_bu),
                         (sdb[:], d_dbr), (son1[:], d_on1), (ssel[:], d_sel),
                         (sdm[:], d_dmask)]):
                    dma("gp", "s_lc" if j % 2 == 0 else "s_lc3", dst, csrc)
                lc_wait = [("s_lc", cnt["s_lc"]), ("s_lc3", cnt["s_lc3"])]

                v_init = ve(lambda e: e.memset(gz[:], 0.0))
                v_init = ve(lambda e: e.memset(stat[:, 14:15], EPS))
                v_init = ve(lambda e: e.memset(SPb[:, 0:1], 0.0))

                x_srcs = [d_xw[t * 128:(t + 1) * 128, :] for t in range(WC)] + \
                         [d_xm[t * 128:(t + 1) * 128, :] for t in range(NMC)]
                ln_free = {}
                xnb_free = {}
                tr_evac = {}
                router_evac = {}
                logit_evac = {}
                wload = {}
                gu_cons = {}
                wd_cons = {}

                def sched_wdma(kind, ei, fc):
                    par = (ei * NFC + fc) % 2
                    if kind == "g":
                        dst, src, ru, sem = wgu[:, 2 * par + 0], d_Wg[ei, fc], gu_cons.get(par), "s_lw%d" % par
                    elif kind == "u":
                        dst, src, ru, sem = wgu[:, 2 * par + 1], d_Wu[ei, fc], gu_cons.get(par), "s_lw%d" % par
                    else:
                        dst, src, ru, sem = wdb_[:, par], d_Wd[ei, fc], wd_cons.get(par), "s_ld%d" % par
                    wl = [("s_te", ru)] if ru is not None else []
                    wload[(kind, ei, fc)] = (sem, dma("gp", sem, dst, src, wait=wl))

                for fc in range(2):
                    sched_wdma("g", 0, fc); sched_wdma("u", 0, fc)
                sched_wdma("d", 0, 0); sched_wdma("d", 0, 1)

                ew_done = {}
                ts_read = {}
                gtile = [0]

                def emit_gu_tile(ei, fc, stt, weighted, bc_ev_l=None):
                    for _ in gu_tile_gen(ei, fc, stt, weighted, bc_ev_l):
                        pass

                def gu_tile_gen(ei, fc, stt, weighted, bc_ev_l=None):
                    par = (ei * NFC + fc) % 2
                    lwsem, lwg = wload[("u", ei, fc)]
                    epar = ei % 2
                    tile = gtile[0]
                    gtile[0] += 1
                    tp = tile % 2
                    bg_ = pb[tp]
                    bu2 = pb[2 + tp]
                    gwl = [(lwsem, lwg), ("s_sc", st_ready_ve[stt])]
                    if tile < 2:
                        gwl.extend(lc_wait)
                    if tile >= 2:
                        gwl.append(ts_read[tp][0])
                    mg = None
                    for dc in range(NDC):
                        rhs = xn_fm[:, WC + stt * 4:WC + stt * 4 + 4, dc, :]
                        mg = te(lambda e, dc=dc, bg_=bg_, par=par, rhs=rhs:
                                e.matmul(bg_[:, 0:STW], wgu[:, 2 * par, dc, :], rhs,
                                         start=(dc == 0), stop=(dc == NDC - 1)),
                                wait=(gwl if dc == 0 else ()), inc=(dc == NDC - 1))
                        if dc % 2 == 1:
                            yield
                    uwl2 = [ts_read[tp][1]] if tile >= 2 else []
                    mu2 = None
                    for dc in range(NDC):
                        rhs = xn_fm[:, WC + stt * 4:WC + stt * 4 + 4, dc, :]
                        mu2 = te(lambda e, dc=dc, bu2=bu2, par=par, rhs=rhs:
                                 e.matmul(bu2[:, 0:STW], wgu[:, 2 * par + 1, dc, :], rhs,
                                          start=(dc == 0), stop=(dc == NDC - 1)),
                                 wait=(uwl2 if dc == 0 else ()), inc=(dc == NDC - 1))
                        if dc % 2 == 1:
                            yield
                    if stt == NST - 1:
                        gu_cons[par] = cnt["s_te"]
                        nxt = ei * NFC + fc + 2
                        if nxt < NEXP * NFC:
                            ne, nf = divmod(nxt, NFC)
                            sched_wdma("g", ne, nf)
                            sched_wdma("u", ne, nf)
                    t2 = et2[:, tp]; t2w = et2w[:, tp]
                    tg = etg[:, tp]; tu = etu[:, tp]
                    t1 = tg; t3 = t2w; t4 = tu; t5_ = tu
                    gbias = sbg[:, ei * NFC + fc: ei * NFC + fc + 1]
                    ubias = sbu[:, ei * NFC + fc: ei * NFC + fc + 1]
                    if weighted:
                        cg = sc(lambda e, tg=tg, bg_=bg_: e.copy(tg, bg_[:, 0:STW]),
                                wait=[("s_te", mg)])
                        cu = sc(lambda e, tu=tu, bu2=bu2: e.copy(tu, bu2[:, 0:STW]),
                                wait=[("s_te", mu2)])
                        ecg, ecu = ("s_sc", cg), ("s_sc", cu)
                    else:
                        cg = ve(lambda e, tg=tg, bg_=bg_: e.tensor_copy(tg, bg_[:, 0:STW]),
                                wait=[("s_te", mg)])
                        cu = ve(lambda e, tu=tu, bu2=bu2: e.tensor_copy(tu, bu2[:, 0:STW]),
                                wait=[("s_te", mu2)])
                        ecg, ecu = ("s_ve", cg), ("s_ve", cu)
                    ts_read[tp] = (ecg, ecu)
                    vg = ve(lambda e, t1=t1, tg=tg, gbias=gbias:
                            e.tensor_scalar(t1, tg, gbias, LIMIT, ALU.add, ALU.min),
                            wait=[ecg])
                    ve(lambda e, t4=t4, tu=tu, ubias=ubias:
                       e.tensor_scalar(t4, tu, ubias, LIMIT, ALU.add, ALU.min),
                       wait=[ecu])
                    ve(lambda e, t5_=t5_, t4=t4:
                       e.tensor_scalar(t5_, t4, -LIMIT, 1.0, ALU.max, ALU.add))
                    sg = sc(lambda e, t2=t2, t1=t1:
                            e.activation(t2, t1, ACT.Sigmoid, scale=ALPHA),
                            wait=[("s_ve", vg)])
                    if weighted:
                        wbs = wbc[:, epar, stt * STW:(stt + 1) * STW]
                        ve(lambda e, t2w=t2w, t2=t2, wbs=wbs:
                           e.tensor_mul(t2w, t2, wbs),
                           wait=[("s_sc", sg), ("s_sc", bc_ev_l[stt])])
                        ve(lambda e, t3=t3, t1=t1, t2w=t2w: e.tensor_mul(t3, t1, t2w))
                    else:
                        ve(lambda e, t3=t3, t1=t1, t2=t2: e.tensor_mul(t3, t1, t2),
                           wait=[("s_sc", sg)])
                    g2 = ve(lambda e, fc=fc, stt=stt, t5_=t5_, t3=t3:
                            e.tensor_mul(act_w[:, fc, stt * STW:(stt + 1) * STW],
                                         t5_, t3))
                    ew_done[(ei, fc, stt)] = g2


                pump_q = [(0, fc, stt) for fc in range(NFC) for stt in range(NST)]
                pump_cur = [None]
                ready_st = [-1]
                st_ready_ve = {}

                def pump(n):
                    k = 0
                    while k < n:
                        if pump_cur[0] is None:
                            if not pump_q:
                                return
                            ei_, fc_, stt_ = pump_q[0]
                            if stt_ > ready_st[0]:
                                return
                            pump_q.pop(0)
                            pump_cur[0] = gu_tile_gen(ei_, fc_, stt_, weighted=False)
                        try:
                            next(pump_cur[0])
                            k += 1
                        except StopIteration:
                            pump_cur[0] = None

                def pump_all():
                    for q in range(NST):
                        if q not in st_ready_ve:
                            st_ready_ve[q] = cnt["s_sc"]
                    ready_st[0] = NST - 1
                    while pump_q or pump_cur[0] is not None:
                        pump(8)

                NTCR = ntc_run if ntc_run else NTC
                lnA = {}
                lnB = {}

                def emitLoad(tc):
                    par3 = tc % 3
                    wl = []
                    if tc >= 3:
                        pv, ps = ln_free[par3]
                        wl = [("s_ve", pv), ("s_sc", ps)]
                    lxsem = "s_lx%d" % (tc % 2)
                    lxv = dma("sy", lxsem, xln[:, par3], x_srcs[tc], wait=wl)
                    lnA[tc] = (lxsem, lxv)

                def emitA(tc):
                    par = tc % 2
                    par3 = tc % 3
                    lxsem, lxv = lnA[tc]
                    xl = xln[:, par3]
                    ssum = stat[:, par * 8 + 0: par * 8 + 1]
                    negm = stat[:, par * 8 + 1: par * 8 + 2]
                    bb = stat[:, par * 8 + 5: par * 8 + 6]
                    var = stat[:, par * 8 + 2: par * 8 + 3]
                    ve(lambda e, xl=xl, ssum=ssum: e.reduce_sum(ssum, xl, AX),
                       wait=[(lxsem, lxv)])
                    ve(lambda e, ssum=ssum, negm=negm:
                       e.tensor_scalar_mul(negm, ssum, -1.0 / DM))
                    bwl2 = [("s_sc", bb_free[par])] if tc >= 2 else []
                    vb = ve(lambda e, negm=negm, bb=bb:
                            e.scalar_tensor_tensor(bb, negm, -1.0, negm,
                                                   ALU.mult, ALU.mult),
                            wait=bwl2)
                    vb = ve(lambda e, bb=bb: e.tensor_scalar_add(bb, bb, EPS))
                    s2 = sc(lambda e, xl=xl, var=var:
                            e.activation(otm[:, 0], xl, ACT.Square, accum_out=var),
                            wait=[(lxsem, lxv)])
                    lnB[tc] = (vb, s2)

                def emitB(tc):
                    par = tc % 2
                    par3 = tc % 3
                    xl = xln[:, par3]
                    xn = xnb[:, par]
                    negm = stat[:, par * 8 + 1: par * 8 + 2]
                    var = stat[:, par * 8 + 2: par * 8 + 3]
                    sq_ = stat[:, par * 8 + 3: par * 8 + 4]
                    inv = stat[:, par * 8 + 4: par * 8 + 5]
                    bb = stat[:, par * 8 + 5: par * 8 + 6]
                    vb, s2 = lnB[tc]
                    s3 = sc(lambda e, var=var, sq_=sq_, bb=bb:
                            e.activation(sq_, var, ACT.Sqrt, bias=bb,
                                         scale=1.0 / DM),
                            wait=[("s_ve", vb)])
                    bb_free[par] = s3
                    vwl = [("s_sc", s3)]
                    if tc >= 2:
                        vwl.append(("s_sc", inv_free[par]))
                    v3 = ve(lambda e, inv=inv, sq_=sq_: e.reciprocal(inv, sq_),
                            wait=vwl)
                    xwl = [("s_ve", v3)]
                    if tc >= 2:
                        xwl.append(("s_te", xnb_free[par]))
                    v4 = ve(lambda e, xn=xn, xl=xl, negm=negm, inv=inv:
                            e.tensor_scalar(xn, xl, negm, inv, ALU.add, ALU.mult),
                            wait=xwl)
                    inv_free[par] = cnt["s_sc"]
                    ln_free[par3] = (v4, s2)
                    lnB[tc] = v4

                def emitC(tc):
                    par = tc % 2
                    xn = xnb[:, par]
                    v4 = lnB[tc]
                    bank = pbt[par]
                    tw = [("s_ve", v4)]
                    if tc < 2:
                        tw.append(("s_lc2", lc_eye))
                    else:
                        tw.append(("s_sc", tr_evac[par]))
                    tvals = []
                    for dc in range(NDC):
                        t_ = te(lambda e, bank=bank, dc=dc, xn=xn:
                                e.transpose(bank[:, dc * 128:(dc + 1) * 128],
                                            xn[:, dc * 128:(dc + 1) * 128], sI16[:]),
                                wait=(tw if dc == 0 else ()), inc=(dc in (3, 7)))
                        tvals.append(t_)
                    xnb_free[par] = tvals[7]
                    sc(lambda e, bank=bank, tc=tc:
                       e.copy(xn_fm[:, tc, :, :], bank[:]),
                       wait=[("s_te", tvals[7])])
                    tr_evac[par] = cnt["s_sc"]

                twid = [512, 512, 512, 512, TW - 2048]
                d_evac = {}
                su_h, ss_h = {}, {}

                def emit_usp(t5):
                    wt_ = twid[t5]
                    c0 = t5 * 512
                    tc0 = c0 // 128
                    ntc_ = wt_ // 128
                    bu_, bs_, bd_ = pb[0], pb[1], pb[2]
                    uwl = (([("s_sc", usp_xn[t5])] + lc_wait) if t5 == 0
                           else [("s_sc", usp_xn[t5]), ("s_ve", d_evac[t5 - 1])])
                    mu = None
                    for dc in range(NDC):
                        rhs = xn_fm[:, tc0:tc0 + ntc_, dc, :]
                        mu = te(lambda e, dc=dc, bu_=bu_, rhs=rhs, wt_=wt_:
                                e.matmul(bu_[:, 0:wt_], sWiT[:, dc, :], rhs,
                                         start=(dc == 0), stop=(dc == NDC - 1)),
                                wait=(uwl if dc == 0 else ()), inc=(dc == NDC - 1))
                    su_h[t5] = sc(lambda e, bu_=bu_, c0=c0, wt_=wt_:
                                  e.activation(u_bf[:, c0:c0 + wt_], bu_[:, 0:wt_],
                                               ACT.Identity, bias=sbi[:]),
                                  wait=[("s_te", mu)])
                    swl2 = [] if t5 < 1 else [("s_sc", ss_h[t5 - 1])]
                    ms = None
                    for dc in range(NDC):
                        rhs = xn_fm[:, tc0:tc0 + ntc_, dc, :]
                        ms = te(lambda e, dc=dc, bs_=bs_, rhs=rhs, wt_=wt_:
                                e.matmul(bs_[:, 0:wt_], sWsT[:, dc, :], rhs,
                                         start=(dc == 0), stop=(dc == NDC - 1)),
                                wait=(swl2 if dc == 0 else ()), inc=(dc == NDC - 1))
                    ss_h[t5] = sc(lambda e, bs_=bs_, c0=c0, wt_=wt_:
                                  e.activation(SPb[:, 1 + c0:1 + c0 + wt_], bs_[:, 0:wt_],
                                               ACT.Identity, bias=sbs[:]),
                                  wait=[("s_te", ms)])
                    vs = ss_h[t5]
                    dwl = [("s_sc", vs), ("s_ve", v_init)]
                    if t5 >= 1:
                        dwl.append(("s_ve", d_evac[t5 - 1]))
                    te(lambda e, bd_=bd_, c0=c0, wt_=wt_:
                       e.matmul(bd_[:, 0:wt_], sAT[:], SPb[:, c0:c0 + wt_],
                                start=True, stop=False, skip_group_check=True),
                       wait=dwl, inc=False)
                    md2 = te(lambda e, bd_=bd_, c0=c0, wt_=wt_:
                             e.matmul(bd_[:, 0:wt_], sBT[:], u_bf[:, c0:c0 + wt_],
                                      start=False, stop=True, skip_group_check=True),
                             wait=[("s_sc", su_h[t5])])
                    pump(2)
                    if t5 == 0:
                        ve(lambda e, bd_=bd_: e.tensor_mul(Dt[:, 0:W], bd_[:, 0:W], sdm[:]),
                           wait=[("s_te", md2)])
                        d_evac[t5] = ve(lambda e, bd_=bd_:
                                        e.tensor_copy(Dt[:, W:512], bd_[:, W:512]))
                    else:
                        d_evac[t5] = ve(lambda e, bd_=bd_, c0=c0, wt_=wt_:
                                        e.tensor_copy(Dt[:, c0:c0 + wt_], bd_[:, 0:wt_]),
                                        wait=[("s_te", md2)])

                inv_free = {}
                bb_free = {}
                usp_xn = {}
                xn_ready = {}
                r_mm = {}
                r_t6 = {}
                r_v7 = {}

                def emitR1(mc):
                    rpar = mc % 2
                    tc = mc + WC
                    lbank = pb[4]
                    lg = lgb[:, rpar]
                    mmv = None
                    for dc in range(NDC):
                        lwl = []
                        if dc == 0:
                            lwl.append(("s_sc", xn_ready[tc]))
                            if mc == 0:
                                lwl.extend(lc_wait)
                            else:
                                lwl.append(("s_sc", logit_evac[(mc - 1) % 2]))
                        mmv = te(lambda e, dc=dc, lbank=lbank, tc=tc:
                                 e.matmul(lbank[:, 0:8], xn_fm[:, tc, dc, :],
                                          sWr[:, dc, :],
                                          start=(dc == 0), stop=(dc == NDC - 1)),
                                 wait=lwl, inc=(dc == NDC - 1))
                    ewl = [("s_te", mmv)]
                    if mc >= 2:
                        ewl.append(("s_ve", r_v7[rpar]))
                    logit_evac[rpar] = sc(lambda e, lg=lg, lbank=lbank:
                                          e.copy(lg, lbank[:, 0:8]), wait=ewl)
                    r_mm[mc] = logit_evac[rpar]

                def emitR2(mc):
                    rpar = mc % 2
                    lg = lgb[:, rpar]
                    m1 = rt[:, rpar, 0:1]
                    m2 = rt[:, rpar, 2:3]
                    msk = rt[:, rpar, 5:13]
                    eqm = eqmb[:, mc, :]
                    eq2 = eq2b[:, mc, :]
                    d12 = rsg[:, 0, mc:mc + 1]
                    v4 = ve(lambda e, m1=m1, lg=lg: e.reduce_max(m1, lg, AX),
                            wait=[("s_sc", r_mm[mc])])
                    ve(lambda e, eqm=eqm, lg=lg, m1=m1:
                       e.tensor_scalar(eqm, lg, m1, None, ALU.is_equal))
                    ve(lambda e, msk=msk, eqm=eqm, lg=lg:
                       e.scalar_tensor_tensor(msk, eqm, -1e30, lg, ALU.mult, ALU.add))
                    vm2 = ve(lambda e, m2=m2, msk=msk: e.reduce_max(m2, msk, AX))
                    ve(lambda e, eq2=eq2, lg=lg, m2=m2:
                       e.tensor_scalar(eq2, lg, m2, None, ALU.is_ge))
                    r_v7[rpar] = ve(lambda e, d12=d12, m1=m1, m2=m2:
                                    e.tensor_sub(d12, m1, m2))

                for col in range(NTCR + 5):
                    if col < NTCR:
                        emitLoad(col)
                        emitA(col)
                    if 1 <= col <= NTCR:
                        emitB(col - 1)
                    if 2 <= col < NTCR + 2:
                        emitC(col - 2)
                        xn_ready[col - 2] = cnt["s_sc"]
                    if not ntc_run:
                        t5r = (col - 6) // 4
                        if 0 <= t5r <= 3 and (col - 6) % 4 == 0:
                            usp_xn[t5r] = cnt["s_sc"]
                            emit_usp(t5r)
                        if col == NTCR + 2:
                            usp_xn[4] = cnt["s_sc"]
                            emit_usp(4)
                        if 5 <= col < NMC + 5:
                            emitR2(col - 5)
                        if 4 <= col < NMC + 4:
                            emitR1(col - 4)
                ve_xn_all = cnt["s_ve"]

                if PH < 0: raise _Done()
                wmaxb = rsg[:, 1, :]
                nwb = rsg[:, 2, :]
                difb = rsg[:, 3, :]
                s_sig = sc(lambda e, wmaxb=wmaxb:
                           e.activation(wmaxb, rsg[:, 0, :], ACT.Sigmoid),
                           wait=[("s_ve", cnt["s_ve"])])
                ve(lambda e, nwb=nwb, wmaxb=wmaxb:
                   e.tensor_scalar(nwb, wmaxb, -1.0, 1.0, ALU.mult, ALU.add),
                   wait=[("s_sc", s_sig)])
                v_df = ve(lambda e, difb=difb, wmaxb=wmaxb:
                          e.tensor_scalar(difb, wmaxb, 2.0, -1.0, ALU.mult, ALU.add))
                wt_done = {}
                bc0_ev = {}
                for mc in range(NMC):
                    rpar = mc % 2
                    wt = wtm[:, rpar]
                    tmp8 = rt[:, rpar, 5:13]
                    vwl = [("s_ve", v_df)] if mc < 2 else [("s_te", r_t6[rpar])]
                    ve(lambda e, wt=wt, mc=mc:
                       e.tensor_scalar(wt, eq2b[:, mc, :], rsg[:, 2, mc:mc + 1],
                                       None, ALU.mult), wait=vwl)
                    ve(lambda e, tmp8=tmp8, mc=mc:
                       e.tensor_scalar(tmp8, eqmb[:, mc, :], rsg[:, 3, mc:mc + 1],
                                       None, ALU.mult))
                    wt_done[rpar] = ve(lambda e, wt=wt, tmp8=tmp8:
                                       e.tensor_add(wt, wt, tmp8))
                    tbank = pb[5]
                    t6w = [("s_ve", wt_done[rpar])]
                    if mc >= 1:
                        t6w.append(("s_sc", router_evac[(mc - 1) % 2]))
                    t6 = te(lambda e, tbank=tbank, wt=wt:
                            e.transpose(tbank[0:8, 0:128], wt, sI32[:]), wait=t6w)
                    r_t6[rpar] = t6
                    router_evac[rpar] = sc(
                        lambda e, mc=mc, tbank=tbank:
                        e.copy(w_fmb[0:8, mc * 128:(mc + 1) * 128], tbank[0:8, 0:128]),
                        wait=[("s_te", t6)])
                    if mc % 4 == 3:
                        stt = mc // 4
                        bk = pb[3]
                        bwl = [("s_sc", cnt["s_sc"])]
                        if stt == 0:
                            bwl.extend(lc_wait)
                        else:
                            bwl.append(("s_sc", bc0_ev[stt - 1]))
                        m_ = te(lambda e, bk=bk, stt=stt:
                                e.matmul(bk[:, 0:STW], ssel[:, 0:128],
                                         w_fmb[:, stt * STW:(stt + 1) * STW],
                                         start=True, stop=True),
                                wait=bwl)
                        bc0_ev[stt] = sc(
                            lambda e, bk=bk, stt=stt:
                            e.copy(wbc[:, 0, stt * STW:(stt + 1) * STW], bk[:, 0:STW]),
                            wait=[("s_te", m_)])
                for q in range(NST):
                    if q not in st_ready_ve:
                        st_ready_ve[q] = cnt["s_sc"]
                ready_st[0] = NST - 1
                bc0_done = cnt["s_sc"]
                ve_d_all = cnt["s_ve"]

                if PH < 2: raise _Done()
                ch_act = {}
                for s in range(NSTEP):
                    pump(2)
                    b = s % 2
                    if b == 0:
                        for k in range(2):
                            sk = s + k
                            if sk >= NSTEP:
                                break
                            wl = []
                            if k == 0:
                                wl = [("s_ve", ve_d_all),
                                      ("s_sc", ch_act[s - 1] if s > 0 else cnt["s_sc"])]
                            te(lambda e, sk=sk, bk=pb[4 + sk % 2]:
                               e.matmul(bk[:, 0:NCH], sI16[:],
                                        Dt[:, WOFF + sk:WOFF + sk + SSTR:LCH],
                                        start=True, stop=False, skip_group_check=True),
                               wait=wl, inc=False)
                    if s == 0:
                        rhs = gz[:]
                    elif s <= WCH:
                        rhs = gw[:, ((s - 1) % 2) * NCH:((s - 1) % 2) * NCH + NCH]
                    else:
                        t0 = s - 1 - WCH
                        rhs = G[:, t0:t0 + SSTR:LCH]
                    wl = [("s_sc", ch_act[s - 1])] if s > 0 else []
                    mv = te(lambda e, b=b, rhs=rhs:
                            e.matmul(pb[4 + b][:, 0:NCH], sAT[:], rhs,
                                     start=False, stop=True, skip_group_check=True),
                            wait=wl)
                    if s < WCH:
                        dst = gw[:, (s % 2) * NCH:(s % 2) * NCH + NCH]
                    else:
                        t0 = s - WCH
                        dst = G[:, t0:t0 + SSTR:LCH]
                    ch_act[s] = sc(lambda e, b=b, dst=dst:
                                   e.activation(dst, pb[4 + b][:, 0:NCH], ACT.Tanh),
                                   wait=[("s_te", mv)])
                pump_all()
                sc_chain = cnt["s_sc"]

                if PH < 3: raise _Done()
                vstate = {}
                for stt in range(NST):
                    c0 = stt * STW
                    vstate[stt] = ve(lambda e, c0=c0:
                                     e.tensor_add(u_bf[:, c0:c0 + STW], G[:, c0:c0 + STW],
                                                  SPb[:, 1 + W + c0:1 + W + c0 + STW]),
                                     wait=([("s_sc", sc_chain)] if stt == 0 else ()))
                ys_evac = {}
                for stt in range(NST):
                    bk = pb[4 + stt % 2]
                    ywl = [("s_ve", vstate[stt])]
                    ywl.append(("s_sc", sc_chain if stt < 2 else ys_evac[stt - 2]))
                    m_ = te(lambda e, bk=bk, stt=stt:
                            e.matmul(bk[:, 0:STW], sCT[:], u_bf[:, stt * STW:(stt + 1) * STW],
                                     start=True, stop=True),
                            wait=ywl)
                    ys_evac[stt] = sc(lambda e, bk=bk, stt=stt:
                                      e.copy(ys_bf[:, stt * STW:(stt + 1) * STW], bk[:, 0:STW]),
                                      wait=[("s_te", m_)])
                sc_ys = cnt["s_sc"]

                attn_ev = {}
                for hc in range(NDC):
                    for stt in range(NST):
                        idx = hc * NST + stt
                        bk = pb[4 + idx % 2]
                        wl = [("s_sc", sc_ys if idx < 2 else attn_ev[idx - 2])]
                        te(lambda e, bk=bk, hc=hc, stt=stt:
                           e.matmul(bk[:, 0:STW], sWoT[:, hc, :],
                                    ys_bf[:, stt * STW:(stt + 1) * STW],
                                    start=True, stop=False, skip_group_check=True),
                           wait=wl, inc=False)
                        m_ = te(lambda e, bk=bk, hc=hc, stt=stt:
                                e.matmul(bk[:, 0:STW], sdb[:, hc * 128:(hc + 1) * 128],
                                         w_fmb[:, stt * STW:(stt + 1) * STW],
                                         start=False, stop=True, skip_group_check=True))
                        attn_ev[idx] = sc(
                            lambda e, bk=bk, hc=hc, stt=stt:
                            e.activation(acc[:, hc, stt * STW:(stt + 1) * STW],
                                         bk[:, 0:STW], ACT.Identity, bias=sbo[:, hc:hc + 1]),
                            wait=[("s_te", m_)])
                sc_attn = cnt["s_sc"]
                for stt in range(NST):
                    for fc in range(NFC):
                        fwl = [("s_sc", bc0_done)] if (stt == 0 and fc == 0) else []
                        ew_done[(0, fc, stt)] = ve(
                            lambda e, fc=fc, stt=stt:
                            e.tensor_mul(act_w[:, fc, stt * STW:(stt + 1) * STW],
                                         act_w[:, fc, stt * STW:(stt + 1) * STW],
                                         wbc[:, 0, stt * STW:(stt + 1) * STW]),
                            wait=fwl)
                wbc_cons0 = cnt["s_ve"]

                fin_add = {}
                out_done = {}

                fxload = {}

                def emit_final_load(mc):
                    par4 = mc % 4
                    fxsem = "s_lf%d" % (mc % 2)
                    xwl = [("s_ve", fin_add[par4])] if mc >= 4 else []
                    fxload[mc] = (fxsem,
                                  dma("gp", fxsem, xres[:, par4],
                                      d_xm[mc * 128:(mc + 1) * 128, :], wait=xwl))

                def emit_final_group(stt):
                    if stt == 0:
                        for mc_ in range(4):
                            emit_final_load(mc_)
                    for mc in range(stt * 4, stt * 4 + 4):
                        par = mc % 2
                        par4 = mc % 4
                        fosem = "s_ou%d" % par
                        fxsem, lxv = fxload[mc]
                        bank = pbt[par]
                        tvals = []
                        for hc in range(NDC):
                            wl2 = []
                            if hc == 0:
                                wl2 = [("s_ve", acc_add[(NDC - 1, stt)])]
                                wl2.append(("s_ve", fin_add.get(par, cnt["s_ve"])
                                            if mc >= 2 else cnt["s_ve"]))
                            t_ = te(lambda e, bank=bank, hc=hc, mc=mc:
                                    e.transpose(bank[:, hc * 128:(hc + 1) * 128],
                                                acc[:, hc, mc * 128:(mc + 1) * 128], sI16[:]),
                                    wait=wl2, inc=(hc in (3, 7)))
                            tvals.append(t_)
                        vwl = [("s_te", tvals[7]), (fxsem, lxv)]
                        if mc >= 2:
                            vwl.append((fosem, out_done[par]))
                        ve(lambda e, bank=bank, par=par, par4=par4:
                           e.tensor_add(otm[:, par, :], bank[:], xres[:, par4, :]),
                           wait=vwl)
                        fin_add[par] = cnt["s_ve"]
                        fin_add[par4] = cnt["s_ve"]
                        if mc + 4 < NMC:
                            emit_final_load(mc + 4)
                        out_done[par] = dma("sy", fosem, d_y[mc * 128:(mc + 1) * 128, :],
                                            otm[:, par], wait=[("s_ve", fin_add[par])])


                if PH < 4: raise _Done()
                acc_add = {}
                wbc_cons = {0: wbc_cons0}
                dn_read = {}
                for ei in range(NEXP):
                    epar = ei % 2
                    if ei == 0:
                        bc_ev_l = bc0_ev
                    else:
                     bc_ev_l = {}
                     for stt in range(NST):
                        bk = pb[4 + stt % 2]
                        bwl = []
                        if stt < 2:
                            if ei == 0:
                                bwl.append(("s_sc", sc_attn))
                            else:
                                bwl.append(("s_ve", dn_read[stt % 2]))
                            if ei >= 2:
                                bwl.append(("s_ve", wbc_cons[epar]))
                        else:
                            bwl.append(("s_sc", bc_ev_l[stt - 2]))
                        m_ = te(lambda e, bk=bk, ei=ei, stt=stt:
                                e.matmul(bk[:, 0:STW], ssel[:, ei * 128:(ei + 1) * 128],
                                         w_fmb[:, stt * STW:(stt + 1) * STW],
                                         start=True, stop=True),
                                wait=bwl)
                        bc_ev_l[stt] = sc(
                            lambda e, bk=bk, epar=epar, stt=stt:
                            e.copy(wbc[:, epar, stt * STW:(stt + 1) * STW], bk[:, 0:STW]),
                            wait=[("s_te", m_)])

                    if ei > 0:
                        for fc in range(NFC):
                            for stt in range(NST):
                                emit_gu_tile(ei, fc, stt, weighted=True,
                                             bc_ev_l=bc_ev_l)
                        wbc_cons[epar] = cnt["s_ve"]

                    for fc in range(NFC):
                        par = (ei * NFC + fc) % 2
                        ldsem, lwd = wload[("d", ei, fc)]
                        for stt in range(NST):
                            idx = ei * NFC * NST + fc * NST + stt
                            dp = (fc * NST + stt) % 2
                            bk = pb[4 + dp]
                            wl = [(ldsem, lwd), ("s_ve", ew_done[(ei, NFC - 1, stt)])]
                            if ei == 0 and fc == 0 and stt < 2:
                                wl.append(("s_sc", sc_attn))
                            elif fc == 0 and stt < 2:
                                wl.append(("s_sc", bc_ev_l[stt + 2]))
                            elif idx >= 2:
                                wl.append(("s_ve", dn_read[dp]))
                            md = None
                            for ic in range(NDC):
                                md = te(lambda e, ic=ic, bk=bk, par=par, stt=stt:
                                        e.matmul(bk[:, 0:STW], wdb_[:, par, ic, :],
                                                 act_w[:, ic, stt * STW:(stt + 1) * STW],
                                                 start=(ic == 0), stop=(ic == NDC - 1),
                                                 skip_group_check=True),
                                        wait=(wl if ic == 0 else ()),
                                        inc=(ic == NDC - 1))
                            if stt == NST - 1:
                                wd_cons[par] = cnt["s_te"]
                                nxt = ei * NFC + fc + 2
                                if nxt < NEXP * NFC:
                                    ne, nf = divmod(nxt, NFC)
                                    sched_wdma("d", ne, nf)
                            awl = [("s_te", md)]
                            if ei == 0:
                                awl.append(("s_sc", attn_ev[fc * NST + stt]))
                            va = ve(lambda e, fc=fc, stt=stt, bk=bk:
                                    e.tensor_add(acc[:, fc, stt * STW:(stt + 1) * STW],
                                                 acc[:, fc, stt * STW:(stt + 1) * STW],
                                                 bk[:, 0:STW]),
                                    wait=awl)
                            dn_read[dp] = va
                            acc_add[(fc, stt)] = va
                            if PH >= 5 and ei == NEXP - 1 and fc == NFC - 1:
                                emit_final_group(stt)

            except _Done:
                pass

            if debug:
                vw_, sw_, tw_ = cnt["s_ve"], cnt["s_sc"], cnt["s_te"]
                def gdbg(name, src_ap):
                    dma("gp", "s_ou0", dbg[name], src_ap,
                        wait=[("s_ve", vw_), ("s_sc", sw_), ("s_te", tw_)])
                gdbg("xnfm", xn_fm[:].rearrange("p a b c -> p (a b c)"))
                gdbg("wfm", w_fmb[:])
                if PH >= 1:
                    gdbg("D", Dt[:]); gdbg("u", u_bf[:]); gdbg("sp", SPb[:, 1:])
                if PH >= 2:
                    gdbg("G", G[:])
                if PH >= 3:
                    gdbg("ys", ys_bf[:])
                    gdbg("acc", acc[:].rearrange("p a b -> p (a b)"))

            fin_o0 = cnt.get("s_ou0", 0)
            fin_o1 = cnt.get("s_ou1", 0)
            def fin(e):
                if fin_o0:
                    e.wait_ge(s_ou0, fin_o0)
                if fin_o1:
                    e.wait_ge(s_ou1, fin_o1)
            ops["sy"].append(fin)

            @block.gpsimd
            def _(g):
                for f in ops["gp"]:
                    f(g)

            @block.sync
            def _(s_):
                for f in ops["sy"]:
                    f(s_)

            @block.tensor
            def _(t_):
                for f in ops["te"]:
                    f(t_)

            @block.vector
            def _(v_):
                for f in ops["ve"]:
                    f(v_)

            @block.scalar
            def _(sc_):
                for f in ops["sc"]:
                    f(sc_)
    return nc


def prep_inputs(L0_attn_x, L0_attn_Wi, L0_attn_bi, L0_attn_Ws, L0_attn_bs,
                L0_attn_Wo, L0_attn_bo, L0_attn_A, L0_attn_B, L0_attn_C,
                L0_ffn_router_weight, L0_ffn_gate_up_proj, L0_ffn_gate_up_proj_bias,
                L0_ffn_down_proj, L0_ffn_down_proj_bias):
    f = lambda a: np.ascontiguousarray(np.asarray(a), dtype=np.float32)
    x = f(L0_attn_x).reshape(16384, DM)
    Wi, Ws, Wo = f(L0_attn_Wi), f(L0_attn_Ws), f(L0_attn_Wo)
    A, B, C = f(L0_attn_A), f(L0_attn_B), f(L0_attn_C)
    bi, bs, bo = f(L0_attn_bi), f(L0_attn_bs), f(L0_attn_bo)
    Wr = f(L0_ffn_router_weight)
    gup, gub = f(L0_ffn_gate_up_proj), f(L0_ffn_gate_up_proj_bias)
    dwn, dwb = f(L0_ffn_down_proj), f(L0_ffn_down_proj_bias)

    com = {}
    com["eye"] = np.eye(128, dtype=np.float32)
    com["AT"] = np.ascontiguousarray(A.T)
    com["BT"] = np.ascontiguousarray(B.T)
    com["CT"] = np.ascontiguousarray(C.T)
    com["WiT"] = np.ascontiguousarray(Wi.T.reshape(NDC, 128, 128).transpose(1, 0, 2))
    com["WsT"] = np.ascontiguousarray(Ws.T.reshape(NDC, 128, 128).transpose(1, 0, 2))
    com["WoT"] = np.ascontiguousarray(Wo.T.reshape(128, NDC, 128))
    com["Wr"] = np.ascontiguousarray(Wr.reshape(NDC, 128, 8).transpose(1, 0, 2))
    com["bi"] = bi.reshape(128, 1)
    com["bs"] = bs.reshape(128, 1)
    com["bo"] = np.ascontiguousarray(bo.reshape(NDC, 128).T)
    gate = np.ascontiguousarray(gup[:, :, 0::2])
    up = np.ascontiguousarray(gup[:, :, 1::2])
    com["Wg"] = np.ascontiguousarray(
        gate.reshape(NEXP, NDC, 128, NFC, 128).transpose(0, 3, 2, 1, 4))
    com["Wu"] = np.ascontiguousarray(
        up.reshape(NEXP, NDC, 128, NFC, 128).transpose(0, 3, 2, 1, 4))
    com["Wd"] = np.ascontiguousarray(
        dwn.reshape(NEXP, NDC, 128, NFC, 128).transpose(0, 3, 2, 1, 4))
    gb = gub[:, 0::2].reshape(NEXP, NFC, 128)
    ub = gub[:, 1::2].reshape(NEXP, NFC, 128)
    com["bg"] = np.ascontiguousarray(gb.reshape(NEXP * NFC, 128).T)
    com["bu"] = np.ascontiguousarray(ub.reshape(NEXP * NFC, 128).T)
    com["dbr"] = np.ascontiguousarray(dwb)
    com["on1"] = np.ones((1, 128), np.float32)
    sel = np.zeros((NEXP, NEXP * 128), np.float32)
    for e_ in range(NEXP):
        sel[e_, e_ * 128:(e_ + 1) * 128] = 1.0
    com["sel"] = sel

    in_maps = []
    for c in range(8):
        m = dict(com)
        r0 = c * NTOK
        m["xm"] = np.ascontiguousarray(x[r0:r0 + NTOK])
        if c % 4 == 0:
            m["xw"] = np.zeros((W, DM), np.float32)
            m["dmask"] = np.zeros((128, W), np.float32)
        else:
            m["xw"] = np.ascontiguousarray(x[r0 - W:r0])
            m["dmask"] = np.ones((128, W), np.float32)
        in_maps.append(m)
    return in_maps


_cache = {}


def kernel(**inputs):
    in_maps = prep_inputs(**inputs)
    if "nc" not in _cache:
        _cache["nc"] = build_nc(debug=False)
    res = run_bass_kernel_spmd(_cache["nc"], in_maps, core_ids=list(range(8)))
    out = np.concatenate([np.asarray(res.results[c]["y"]) for c in range(8)], axis=0)
    return out.reshape(2, 8192, DM)


# version 24 (final)
# speedup vs baseline: 1.0795x; 1.0081x over previous
import numpy as np
import concourse.bass as bass
import concourse.mybir as mybir
from concourse.bass_utils import run_bass_kernel_spmd

F32 = mybir.dt.float32
BF16 = mybir.dt.bfloat16
AX = mybir.AxisListType.X
ALU = mybir.AluOpType
ACT = mybir.ActivationFunctionType

DM = 1024
DS = 128
NTOK = 2048
W = 128
LCH = 32
NCH = NTOK // LCH
WCH = 96
WOFF = W - WCH
NSTEP = WCH + LCH
TW = W + NTOK
NDC = DM // 128
NTC = TW // 128
WC = W // 128
NMC = NTOK // 128
NEXP = 8
NFC = 8
NST = 4
STW = 512
EPS = 1e-5
ALPHA = 1.702
LIMIT = 7.0
SSTR = LCH * (NCH - 1) + 1


def build_nc(debug=False, upto="full", ntc_run=None, skip_big=False):
    import os as _os
    LNL = int(_os.environ.get("MK_LN_LEVEL", "9"))
    PH = {"ln0": -1, "ln": 0, "usp": 1, "chain": 2, "attn": 3, "ffn": 4, "full": 5}[upto]
    nc = bass.Bass("TRN2", target_bir_lowering=False)

    d_xm = nc.dram_tensor("xm", [NTOK, DM], F32, kind="ExternalInput").ap()
    d_xw = nc.dram_tensor("xw", [W, DM], F32, kind="ExternalInput").ap()
    d_dmask = nc.dram_tensor("dmask", [128, W], F32, kind="ExternalInput").ap()
    d_eye = nc.dram_tensor("eye", [128, 128], F32, kind="ExternalInput").ap()
    d_AT = nc.dram_tensor("AT", [128, 128], F32, kind="ExternalInput").ap()
    d_BT = nc.dram_tensor("BT", [128, 128], F32, kind="ExternalInput").ap()
    d_CT = nc.dram_tensor("CT", [128, 128], F32, kind="ExternalInput").ap()
    d_WiT = nc.dram_tensor("WiT", [128, NDC, 128], F32, kind="ExternalInput").ap()
    d_WsT = nc.dram_tensor("WsT", [128, NDC, 128], F32, kind="ExternalInput").ap()
    d_WoT = nc.dram_tensor("WoT", [128, NDC, 128], F32, kind="ExternalInput").ap()
    d_Wr = nc.dram_tensor("Wr", [128, NDC, 8], F32, kind="ExternalInput").ap()
    d_bi = nc.dram_tensor("bi", [128, 1], F32, kind="ExternalInput").ap()
    d_bs = nc.dram_tensor("bs", [128, 1], F32, kind="ExternalInput").ap()
    d_bo = nc.dram_tensor("bo", [128, 8], F32, kind="ExternalInput").ap()
    d_bg = nc.dram_tensor("bg", [128, NEXP * NFC], F32, kind="ExternalInput").ap()
    d_bu = nc.dram_tensor("bu", [128, NEXP * NFC], F32, kind="ExternalInput").ap()
    d_dbr = nc.dram_tensor("dbr", [NEXP, DM], F32, kind="ExternalInput").ap()
    d_on1 = nc.dram_tensor("on1", [1, 128], F32, kind="ExternalInput").ap()
    d_sel = nc.dram_tensor("sel", [NEXP, NEXP * 128], F32, kind="ExternalInput").ap()
    if not skip_big:
        d_Wg = nc.dram_tensor("Wg", [NEXP, NFC, 128, NDC, 128], F32, kind="ExternalInput").ap()
        d_Wu = nc.dram_tensor("Wu", [NEXP, NFC, 128, NDC, 128], F32, kind="ExternalInput").ap()
        d_Wd = nc.dram_tensor("Wd", [NEXP, NFC, 128, NDC, 128], F32, kind="ExternalInput").ap()
    d_y = nc.dram_tensor("y", [NTOK, DM], F32, kind="ExternalOutput").ap()
    dbg = {}
    if debug:
        dbg["xnfm"] = nc.dram_tensor("dbg_xnfm", [128, NTC * DM], F32, kind="ExternalOutput").ap()
        dbg["D"] = nc.dram_tensor("dbg_D", [128, TW], F32, kind="ExternalOutput").ap()
        dbg["G"] = nc.dram_tensor("dbg_G", [128, NTOK], F32, kind="ExternalOutput").ap()
        dbg["wfm"] = nc.dram_tensor("dbg_wfm", [8, NTOK], F32, kind="ExternalOutput").ap()
        dbg["ys"] = nc.dram_tensor("dbg_ys", [128, NTOK], F32, kind="ExternalOutput").ap()
        dbg["acc"] = nc.dram_tensor("dbg_acc", [128, NDC * NTOK], F32, kind="ExternalOutput").ap()
        dbg["u"] = nc.dram_tensor("dbg_u", [128, TW], F32, kind="ExternalOutput").ap()
        dbg["sp"] = nc.dram_tensor("dbg_sp", [128, TW], F32, kind="ExternalOutput").ap()

    ops = {k: [] for k in ("te", "ve", "sc", "gp", "sy")}
    cnt = {}

    def _inc(sem, n=1):
        cnt[sem] = cnt.get(sem, 0) + n
        return cnt[sem]

    from contextlib import ExitStack
    with ExitStack() as _es:
        xn_fm = _es.enter_context(nc.sbuf_tensor("xn_fm", [128, NTC, NDC, 128], BF16))
        SPb = _es.enter_context(nc.sbuf_tensor("SPb", [128, 1 + TW], BF16))
        u_bf = _es.enter_context(nc.sbuf_tensor("u_bf", [128, TW], BF16))
        Dt = _es.enter_context(nc.sbuf_tensor("Dt", [128, TW], BF16))
        G = _es.enter_context(nc.sbuf_tensor("G", [128, NTOK], BF16))
        gw = _es.enter_context(nc.sbuf_tensor("gw", [128, 2 * NCH], BF16))
        gz = _es.enter_context(nc.sbuf_tensor("gz", [128, NCH], BF16))
        ys_bf = _es.enter_context(nc.sbuf_tensor("ys_bf", [128, NTOK], BF16))
        w_fmb = _es.enter_context(nc.sbuf_tensor("w_fmb", [8, NTOK], BF16))
        acc = _es.enter_context(nc.sbuf_tensor("acc", [128, NDC, NTOK], BF16))
        act_w = _es.enter_context(nc.sbuf_tensor("act_w", [128, NFC, NTOK], BF16))
        wbc = _es.enter_context(nc.sbuf_tensor("wbc", [128, 2, NTOK], BF16))
        sAT = _es.enter_context(nc.sbuf_tensor("sAT", [128, 128], BF16))
        sBT = _es.enter_context(nc.sbuf_tensor("sBT", [128, 128], BF16))
        sCT = _es.enter_context(nc.sbuf_tensor("sCT", [128, 128], BF16))
        sI16 = _es.enter_context(nc.sbuf_tensor("sI16", [128, 128], BF16))
        sI32 = _es.enter_context(nc.sbuf_tensor("sI32", [128, 128], F32))
        sWiT = _es.enter_context(nc.sbuf_tensor("sWiT", [128, NDC, 128], BF16))
        sWsT = _es.enter_context(nc.sbuf_tensor("sWsT", [128, NDC, 128], BF16))
        sWoT = _es.enter_context(nc.sbuf_tensor("sWoT", [128, NDC, 128], BF16))
        sWr = _es.enter_context(nc.sbuf_tensor("sWr", [128, NDC, 8], BF16))
        sbi = _es.enter_context(nc.sbuf_tensor("sbi", [128, 1], F32))
        sbs = _es.enter_context(nc.sbuf_tensor("sbs", [128, 1], F32))
        sbo = _es.enter_context(nc.sbuf_tensor("sbo", [128, 8], F32))
        sbg = _es.enter_context(nc.sbuf_tensor("sbg", [128, NEXP * NFC], F32))
        sbu = _es.enter_context(nc.sbuf_tensor("sbu", [128, NEXP * NFC], F32))
        sdb = _es.enter_context(nc.sbuf_tensor("sdb", [NEXP, DM], BF16))
        son1 = _es.enter_context(nc.sbuf_tensor("son1", [1, 128], BF16))
        ssel = _es.enter_context(nc.sbuf_tensor("ssel", [NEXP, NEXP * 128], BF16))
        sdm = _es.enter_context(nc.sbuf_tensor("sdm", [128, W], F32))
        xln = _es.enter_context(nc.sbuf_tensor("xln", [128, 3, DM], F32))
        xnb = _es.enter_context(nc.sbuf_tensor("xnb", [128, 3, DM], BF16))
        stat = _es.enter_context(nc.sbuf_tensor("stat", [128, 16], F32))
        lgb = _es.enter_context(nc.sbuf_tensor("lgb", [128, 2, 8], F32))
        rt = _es.enter_context(nc.sbuf_tensor("rt", [128, 2, 16], F32))
        wtm = _es.enter_context(nc.sbuf_tensor("wtm", [128, 2, 8], F32))
        rsg = _es.enter_context(nc.sbuf_tensor("rsg", [128, 4, NMC], F32))
        eqmb = _es.enter_context(nc.sbuf_tensor("eqmb", [128, NMC, 8], BF16))
        eq2b = _es.enter_context(nc.sbuf_tensor("eq2b", [128, NMC, 8], BF16))
        wgu = _es.enter_context(nc.sbuf_tensor("wgu", [128, 4, NDC, 128], BF16))
        wdb_ = _es.enter_context(nc.sbuf_tensor("wdb_", [128, 2, NFC, 128], BF16))
        et2 = _es.enter_context(nc.sbuf_tensor("et2", [128, 2, STW], BF16))
        et2w = _es.enter_context(nc.sbuf_tensor("et2w", [128, 2, STW], BF16))
        etg = _es.enter_context(nc.sbuf_tensor("etg", [128, 2, STW], BF16))
        etu = _es.enter_context(nc.sbuf_tensor("etu", [128, 2, STW], BF16))
        xres = _es.enter_context(nc.sbuf_tensor("xres", [128, 4, DM], BF16))
        otm = _es.enter_context(nc.sbuf_tensor("otm", [128, 2, DM], F32))
        pb0 = _es.enter_context(nc.psum_tensor([128, 512], F32))
        pb1 = _es.enter_context(nc.psum_tensor([128, 512], F32))
        pb2 = _es.enter_context(nc.psum_tensor([128, 512], F32))
        pb3 = _es.enter_context(nc.psum_tensor([128, 512], F32))
        pb4 = _es.enter_context(nc.psum_tensor([128, 512], F32))
        pb5 = _es.enter_context(nc.psum_tensor([128, 512], F32))
        pbtA = _es.enter_context(nc.psum_tensor([128, 1024], BF16))
        pbtB = _es.enter_context(nc.psum_tensor([128, 1024], BF16))
        s_lc = _es.enter_context(nc.semaphore("s_lc"))
        s_lc2 = _es.enter_context(nc.semaphore("s_lc2"))
        s_lc3 = _es.enter_context(nc.semaphore("s_lc3"))
        s_lw0 = _es.enter_context(nc.semaphore("s_lw0"))
        s_lw1 = _es.enter_context(nc.semaphore("s_lw1"))
        s_ld0 = _es.enter_context(nc.semaphore("s_ld0"))
        s_ld1 = _es.enter_context(nc.semaphore("s_ld1"))
        s_lx0 = _es.enter_context(nc.semaphore("s_lx0"))
        s_lx1 = _es.enter_context(nc.semaphore("s_lx1"))
        s_lf0 = _es.enter_context(nc.semaphore("s_lf0"))
        s_lf1 = _es.enter_context(nc.semaphore("s_lf1"))
        s_te = _es.enter_context(nc.semaphore("s_te"))
        s_ve = _es.enter_context(nc.semaphore("s_ve"))
        s_sc = _es.enter_context(nc.semaphore("s_sc"))
        s_ou0 = _es.enter_context(nc.semaphore("s_ou0"))
        s_ou1 = _es.enter_context(nc.semaphore("s_ou1"))
        s_gp = _es.enter_context(nc.semaphore("s_gp"))
        block = _es.enter_context(nc.Block())
        if True:
            pb = [pb0, pb1, pb2, pb3, pb4, pb5]
            pbt = [pbtA, pbtB]
            sems = {"s_lc": s_lc, "s_lc2": s_lc2, "s_lc3": s_lc3, "s_lw0": s_lw0, "s_lw1": s_lw1,
                    "s_ld0": s_ld0, "s_ld1": s_ld1,
                    "s_lx0": s_lx0, "s_lx1": s_lx1, "s_lf0": s_lf0, "s_lf1": s_lf1, "s_te": s_te,
                    "s_ve": s_ve, "s_sc": s_sc, "s_ou0": s_ou0, "s_ou1": s_ou1,
                    "s_gp": s_gp}

            def mk(eng_key, sem_name, self_wait=False):
                def emitfn(fn, wait=(), inc=True):
                    for sn, val in wait:
                        assert val <= cnt.get(sn, 0), (sn, val, cnt.get(sn, 0), eng_key)
                    w = tuple(wait)
                    if self_wait and cnt.get(sem_name, 0) > 0:
                        w = ((sem_name, cnt[sem_name]),) + w
                    v = _inc(sem_name) if inc else None
                    def op(e, fn=fn, w=w, v=v, sem_name=sem_name):
                        for sn, val in w:
                            e.wait_ge(sems[sn], val)
                        ins = fn(e)
                        if v is not None:
                            ins.then_inc(sems[sem_name], 1)
                    ops[eng_key].append(op)
                    return cnt.get(sem_name, 0)
                return emitfn

            te = mk("te", "s_te")
            ve = mk("ve", "s_ve", self_wait=True)
            sc = mk("sc", "s_sc", self_wait=True)
            gpc = mk("gp", "s_gp", self_wait=True)

            class _Done(Exception):
                pass

            def dma(eng_key, sem_name, dst, src, wait=()):
                for sn, val in wait:
                    assert val <= cnt.get(sn, 0), (sn, val, cnt.get(sn, 0), "dma")
                w = tuple(wait)
                if cnt.get(sem_name, 0) > 0:
                    w = ((sem_name, cnt[sem_name]),) + w
                _inc(sem_name, 16)
                def op(e, dst=dst, src=src, w=w, sem_name=sem_name):
                    for sn, val in w:
                        e.wait_ge(sems[sn], val)
                    e.dma_start(dst, src).then_inc(sems[sem_name], 16)
                ops[eng_key].append(op)
                return cnt[sem_name]

            try:
                dma("gp", "s_lc2", sI16[:], d_eye)
                lc_eye = dma("gp", "s_lc2", sI32[:], d_eye)
                for j, (dst, csrc) in enumerate(
                        [(sAT[:], d_AT), (sBT[:], d_BT), (sCT[:], d_CT),
                         (sWiT[:], d_WiT), (sWsT[:], d_WsT), (sWoT[:], d_WoT),
                         (sWr[:], d_Wr), (sbi[:], d_bi), (sbs[:], d_bs),
                         (sbo[:], d_bo), (sbg[:], d_bg), (sbu[:], d_bu),
                         (sdb[:], d_dbr), (son1[:], d_on1), (ssel[:], d_sel),
                         (sdm[:], d_dmask)]):
                    dma("gp", "s_lc" if j % 2 == 0 else "s_lc3", dst, csrc)
                lc_wait = [("s_lc", cnt["s_lc"]), ("s_lc3", cnt["s_lc3"])]

                v_init = ve(lambda e: e.memset(gz[:], 0.0))
                v_init = ve(lambda e: e.memset(stat[:, 14:15], EPS))
                v_init = ve(lambda e: e.memset(SPb[:, 0:1], 0.0))

                x_srcs = [d_xw[t * 128:(t + 1) * 128, :] for t in range(WC)] + \
                         [d_xm[t * 128:(t + 1) * 128, :] for t in range(NMC)]
                ln_free = {}
                xnb_free = {}
                tr_evac = {}
                router_evac = {}
                logit_evac = {}
                wload = {}
                gu_cons = {}
                wd_cons = {}

                def sched_wdma(kind, ei, fc):
                    par = (ei * NFC + fc) % 2
                    if kind == "g":
                        dst, src, ru, sem = wgu[:, 2 * par + 0], d_Wg[ei, fc], gu_cons.get(par), "s_lw%d" % par
                    elif kind == "u":
                        dst, src, ru, sem = wgu[:, 2 * par + 1], d_Wu[ei, fc], gu_cons.get(par), "s_lw%d" % par
                    else:
                        dst, src, ru, sem = wdb_[:, par], d_Wd[ei, fc], wd_cons.get(par), "s_ld%d" % par
                    wl = [("s_te", ru)] if ru is not None else []
                    wload[(kind, ei, fc)] = (sem, dma("gp", sem, dst, src, wait=wl))

                for fc in range(2):
                    sched_wdma("g", 0, fc); sched_wdma("u", 0, fc)
                sched_wdma("d", 0, 0); sched_wdma("d", 0, 1)

                ew_done = {}
                ts_read = {}
                gtile = [0]

                def emit_gu_tile(ei, fc, stt, weighted, bc_ev_l=None):
                    for _ in gu_tile_gen(ei, fc, stt, weighted, bc_ev_l):
                        pass

                def gu_tile_gen(ei, fc, stt, weighted, bc_ev_l=None):
                    par = (ei * NFC + fc) % 2
                    lwsem, lwg = wload[("u", ei, fc)]
                    epar = ei % 2
                    tile = gtile[0]
                    gtile[0] += 1
                    tp = tile % 2
                    bg_ = pb[tp]
                    bu2 = pb[2 + tp]
                    gwl = [(lwsem, lwg), ("s_sc", st_ready_ve[stt])]
                    if tile < 2:
                        gwl.extend(lc_wait)
                    if tile >= 2:
                        gwl.append(ts_read[tp][0])
                    mg = None
                    for dc in range(NDC):
                        rhs = xn_fm[:, WC + stt * 4:WC + stt * 4 + 4, dc, :]
                        mg = te(lambda e, dc=dc, bg_=bg_, par=par, rhs=rhs:
                                e.matmul(bg_[:, 0:STW], wgu[:, 2 * par, dc, :], rhs,
                                         start=(dc == 0), stop=(dc == NDC - 1)),
                                wait=(gwl if dc == 0 else ()), inc=(dc == NDC - 1))
                        if dc % 2 == 1:
                            yield
                    uwl2 = [ts_read[tp][1]] if tile >= 2 else []
                    mu2 = None
                    for dc in range(NDC):
                        rhs = xn_fm[:, WC + stt * 4:WC + stt * 4 + 4, dc, :]
                        mu2 = te(lambda e, dc=dc, bu2=bu2, par=par, rhs=rhs:
                                 e.matmul(bu2[:, 0:STW], wgu[:, 2 * par + 1, dc, :], rhs,
                                          start=(dc == 0), stop=(dc == NDC - 1)),
                                 wait=(uwl2 if dc == 0 else ()), inc=(dc == NDC - 1))
                        if dc % 2 == 1:
                            yield
                    if stt == NST - 1:
                        gu_cons[par] = cnt["s_te"]
                        nxt = ei * NFC + fc + 2
                        if nxt < NEXP * NFC:
                            ne, nf = divmod(nxt, NFC)
                            sched_wdma("g", ne, nf)
                            sched_wdma("u", ne, nf)
                    t2 = et2[:, tp]; t2w = et2w[:, tp]
                    tg = etg[:, tp]; tu = etu[:, tp]
                    t1 = tg; t3 = t2w; t4 = tu; t5_ = tu
                    gbias = sbg[:, ei * NFC + fc: ei * NFC + fc + 1]
                    ubias = sbu[:, ei * NFC + fc: ei * NFC + fc + 1]
                    if weighted:
                        cg = sc(lambda e, tg=tg, bg_=bg_: e.copy(tg, bg_[:, 0:STW]),
                                wait=[("s_te", mg)])
                        cu = sc(lambda e, tu=tu, bu2=bu2: e.copy(tu, bu2[:, 0:STW]),
                                wait=[("s_te", mu2)])
                        ecg, ecu = ("s_sc", cg), ("s_sc", cu)
                    else:
                        cg = ve(lambda e, tg=tg, bg_=bg_: e.tensor_copy(tg, bg_[:, 0:STW]),
                                wait=[("s_te", mg)])
                        cu = ve(lambda e, tu=tu, bu2=bu2: e.tensor_copy(tu, bu2[:, 0:STW]),
                                wait=[("s_te", mu2)])
                        ecg, ecu = ("s_ve", cg), ("s_ve", cu)
                    ts_read[tp] = (ecg, ecu)
                    vg = ve(lambda e, t1=t1, tg=tg, gbias=gbias:
                            e.tensor_scalar(t1, tg, gbias, LIMIT, ALU.add, ALU.min),
                            wait=[ecg])
                    ve(lambda e, t4=t4, tu=tu, ubias=ubias:
                       e.tensor_scalar(t4, tu, ubias, LIMIT, ALU.add, ALU.min),
                       wait=[ecu])
                    ve(lambda e, t5_=t5_, t4=t4:
                       e.tensor_scalar(t5_, t4, -LIMIT, 1.0, ALU.max, ALU.add))
                    sg = sc(lambda e, t2=t2, t1=t1:
                            e.activation(t2, t1, ACT.Sigmoid, scale=ALPHA),
                            wait=[("s_ve", vg)])
                    if weighted:
                        wbs = wbc[:, epar, stt * STW:(stt + 1) * STW]
                        ve(lambda e, t2w=t2w, t2=t2, wbs=wbs:
                           e.tensor_mul(t2w, t2, wbs),
                           wait=[("s_sc", sg), ("s_sc", bc_ev_l[stt])])
                        ve(lambda e, t3=t3, t1=t1, t2w=t2w: e.tensor_mul(t3, t1, t2w))
                    else:
                        ve(lambda e, t3=t3, t1=t1, t2=t2: e.tensor_mul(t3, t1, t2),
                           wait=[("s_sc", sg)])
                    g2 = ve(lambda e, fc=fc, stt=stt, t5_=t5_, t3=t3:
                            e.tensor_mul(act_w[:, fc, stt * STW:(stt + 1) * STW],
                                         t5_, t3))
                    ew_done[(ei, fc, stt)] = g2


                pump_q = [(0, fc, stt) for fc in range(NFC) for stt in range(NST)]
                pump_cur = [None]
                ready_st = [-1]
                st_ready_ve = {}

                def pump(n):
                    k = 0
                    while k < n:
                        if pump_cur[0] is None:
                            if not pump_q:
                                return
                            ei_, fc_, stt_ = pump_q[0]
                            if stt_ > ready_st[0]:
                                return
                            pump_q.pop(0)
                            pump_cur[0] = gu_tile_gen(ei_, fc_, stt_, weighted=False)
                        try:
                            next(pump_cur[0])
                            k += 1
                        except StopIteration:
                            pump_cur[0] = None

                def pump_all():
                    for q in range(NST):
                        if q not in st_ready_ve:
                            st_ready_ve[q] = cnt["s_sc"]
                    ready_st[0] = NST - 1
                    while pump_q or pump_cur[0] is not None:
                        pump(8)

                NTCR = ntc_run if ntc_run else NTC
                lnA = {}
                lnB = {}

                def emitLoad(tc):
                    par3 = tc % 3
                    wl = []
                    if tc >= 3:
                        pv, ps = ln_free[par3]
                        wl = [("s_ve", pv), ("s_sc", ps)]
                    lxsem = "s_lx%d" % (tc % 2)
                    lxv = dma("sy", lxsem, xln[:, par3], x_srcs[tc], wait=wl)
                    lnA[tc] = (lxsem, lxv)

                def emitA(tc):
                    par = tc % 2
                    par3 = tc % 3
                    lxsem, lxv = lnA[tc]
                    xl = xln[:, par3]
                    ssum = stat[:, par * 8 + 0: par * 8 + 1]
                    negm = stat[:, par * 8 + 1: par * 8 + 2]
                    bb = stat[:, par * 8 + 5: par * 8 + 6]
                    var = stat[:, par * 8 + 2: par * 8 + 3]
                    ve(lambda e, xl=xl, ssum=ssum: e.reduce_sum(ssum, xl, AX),
                       wait=[(lxsem, lxv)])
                    ve(lambda e, ssum=ssum, negm=negm:
                       e.tensor_scalar_mul(negm, ssum, -1.0 / DM))
                    bwl2 = [("s_sc", bb_free[par])] if tc >= 2 else []
                    vb = ve(lambda e, negm=negm, bb=bb:
                            e.scalar_tensor_tensor(bb, negm, -1.0, negm,
                                                   ALU.mult, ALU.mult),
                            wait=bwl2)
                    vb = ve(lambda e, bb=bb: e.tensor_scalar_add(bb, bb, EPS))
                    s2 = sc(lambda e, xl=xl, var=var:
                            e.activation(otm[:, 0], xl, ACT.Square, accum_out=var),
                            wait=[(lxsem, lxv)])
                    lnB[tc] = (vb, s2)

                def emitB(tc):
                    par = tc % 2
                    par3 = tc % 3
                    xl = xln[:, par3]
                    xn = xnb[:, par3]
                    negm = stat[:, par * 8 + 1: par * 8 + 2]
                    var = stat[:, par * 8 + 2: par * 8 + 3]
                    sq_ = stat[:, par * 8 + 3: par * 8 + 4]
                    inv = stat[:, par * 8 + 4: par * 8 + 5]
                    bb = stat[:, par * 8 + 5: par * 8 + 6]
                    vb, s2 = lnB[tc]
                    s3 = sc(lambda e, var=var, sq_=sq_, bb=bb:
                            e.activation(sq_, var, ACT.Sqrt, bias=bb,
                                         scale=1.0 / DM),
                            wait=[("s_ve", vb)])
                    bb_free[par] = s3
                    vwl = [("s_sc", s3)]
                    if tc >= 2:
                        vwl.append(("s_sc", inv_free[par]))
                    v3 = ve(lambda e, inv=inv, sq_=sq_: e.reciprocal(inv, sq_),
                            wait=vwl)
                    xwl = [("s_ve", v3)]
                    if tc >= 3:
                        xwl.append(("s_te", xnb_free[par3]))
                    v4 = ve(lambda e, xn=xn, xl=xl, negm=negm, inv=inv:
                            e.tensor_scalar(xn, xl, negm, inv, ALU.add, ALU.mult),
                            wait=xwl)
                    inv_free[par] = cnt["s_sc"]
                    ln_free[par3] = (v4, s2)
                    lnB[tc] = v4

                def emitC(tc):
                    par = tc % 2
                    xn = xnb[:, tc % 3]
                    v4 = lnB[tc]
                    bank = pbt[par]
                    tw = [("s_ve", v4)]
                    if tc < 2:
                        tw.append(("s_lc2", lc_eye))
                    else:
                        tw.append(("s_sc", tr_evac[par]))
                    tvals = []
                    for dc in range(NDC):
                        t_ = te(lambda e, bank=bank, dc=dc, xn=xn:
                                e.transpose(bank[:, dc * 128:(dc + 1) * 128],
                                            xn[:, dc * 128:(dc + 1) * 128], sI16[:]),
                                wait=(tw if dc == 0 else ()), inc=(dc in (3, 7)))
                        tvals.append(t_)
                    xnb_free[tc % 3] = tvals[7]
                    sc(lambda e, bank=bank, tc=tc:
                       e.copy(xn_fm[:, tc, :, :], bank[:]),
                       wait=[("s_te", tvals[7])])
                    tr_evac[par] = cnt["s_sc"]

                twid = [512, 512, 512, 512, TW - 2048]
                d_evac = {}
                su_h, ss_h = {}, {}

                def emit_usp(t5):
                    wt_ = twid[t5]
                    c0 = t5 * 512
                    tc0 = c0 // 128
                    ntc_ = wt_ // 128
                    bu_, bs_, bd_ = pb[0], pb[1], pb[2]
                    uwl = (([("s_sc", usp_xn[t5])] + lc_wait) if t5 == 0
                           else [("s_sc", usp_xn[t5]), ("s_ve", d_evac[t5 - 1])])
                    mu = None
                    for dc in range(NDC):
                        rhs = xn_fm[:, tc0:tc0 + ntc_, dc, :]
                        mu = te(lambda e, dc=dc, bu_=bu_, rhs=rhs, wt_=wt_:
                                e.matmul(bu_[:, 0:wt_], sWiT[:, dc, :], rhs,
                                         start=(dc == 0), stop=(dc == NDC - 1)),
                                wait=(uwl if dc == 0 else ()), inc=(dc == NDC - 1))
                    su_h[t5] = sc(lambda e, bu_=bu_, c0=c0, wt_=wt_:
                                  e.activation(u_bf[:, c0:c0 + wt_], bu_[:, 0:wt_],
                                               ACT.Identity, bias=sbi[:]),
                                  wait=[("s_te", mu)])
                    swl2 = [] if t5 < 1 else [("s_sc", ss_h[t5 - 1])]
                    ms = None
                    for dc in range(NDC):
                        rhs = xn_fm[:, tc0:tc0 + ntc_, dc, :]
                        ms = te(lambda e, dc=dc, bs_=bs_, rhs=rhs, wt_=wt_:
                                e.matmul(bs_[:, 0:wt_], sWsT[:, dc, :], rhs,
                                         start=(dc == 0), stop=(dc == NDC - 1)),
                                wait=(swl2 if dc == 0 else ()), inc=(dc == NDC - 1))
                    ss_h[t5] = sc(lambda e, bs_=bs_, c0=c0, wt_=wt_:
                                  e.activation(SPb[:, 1 + c0:1 + c0 + wt_], bs_[:, 0:wt_],
                                               ACT.Identity, bias=sbs[:]),
                                  wait=[("s_te", ms)])
                    vs = ss_h[t5]
                    dwl = [("s_sc", vs), ("s_ve", v_init)]
                    if t5 >= 1:
                        dwl.append(("s_ve", d_evac[t5 - 1]))
                    te(lambda e, bd_=bd_, c0=c0, wt_=wt_:
                       e.matmul(bd_[:, 0:wt_], sAT[:], SPb[:, c0:c0 + wt_],
                                start=True, stop=False, skip_group_check=True),
                       wait=dwl, inc=False)
                    md2 = te(lambda e, bd_=bd_, c0=c0, wt_=wt_:
                             e.matmul(bd_[:, 0:wt_], sBT[:], u_bf[:, c0:c0 + wt_],
                                      start=False, stop=True, skip_group_check=True),
                             wait=[("s_sc", su_h[t5])])
                    pump(2)
                    if t5 == 0:
                        ve(lambda e, bd_=bd_: e.tensor_mul(Dt[:, 0:W], bd_[:, 0:W], sdm[:]),
                           wait=[("s_te", md2)])
                        d_evac[t5] = ve(lambda e, bd_=bd_:
                                        e.tensor_copy(Dt[:, W:512], bd_[:, W:512]))
                    else:
                        d_evac[t5] = ve(lambda e, bd_=bd_, c0=c0, wt_=wt_:
                                        e.tensor_copy(Dt[:, c0:c0 + wt_], bd_[:, 0:wt_]),
                                        wait=[("s_te", md2)])

                inv_free = {}
                bb_free = {}
                usp_xn = {}
                xn_ready = {}
                r_mm = {}
                r_t6 = {}
                r_v7 = {}

                def emitR1(mc):
                    rpar = mc % 2
                    tc = mc + WC
                    lbank = pb[4]
                    lg = lgb[:, rpar]
                    mmv = None
                    for dc in range(NDC):
                        lwl = []
                        if dc == 0:
                            lwl.append(("s_sc", xn_ready[tc]))
                            if mc == 0:
                                lwl.extend(lc_wait)
                            else:
                                lwl.append(("s_sc", logit_evac[(mc - 1) % 2]))
                        mmv = te(lambda e, dc=dc, lbank=lbank, tc=tc:
                                 e.matmul(lbank[:, 0:8], xn_fm[:, tc, dc, :],
                                          sWr[:, dc, :],
                                          start=(dc == 0), stop=(dc == NDC - 1)),
                                 wait=lwl, inc=(dc == NDC - 1))
                    ewl = [("s_te", mmv)]
                    if mc >= 2:
                        ewl.append(("s_ve", r_v7[rpar]))
                    logit_evac[rpar] = sc(lambda e, lg=lg, lbank=lbank:
                                          e.copy(lg, lbank[:, 0:8]), wait=ewl)
                    r_mm[mc] = logit_evac[rpar]

                def emitR2(mc):
                    rpar = mc % 2
                    lg = lgb[:, rpar]
                    m1 = rt[:, rpar, 0:1]
                    m2 = rt[:, rpar, 2:3]
                    msk = rt[:, rpar, 5:13]
                    eqm = eqmb[:, mc, :]
                    eq2 = eq2b[:, mc, :]
                    d12 = rsg[:, 0, mc:mc + 1]
                    v4 = ve(lambda e, m1=m1, lg=lg: e.reduce_max(m1, lg, AX),
                            wait=[("s_sc", r_mm[mc])])
                    ve(lambda e, eqm=eqm, lg=lg, m1=m1:
                       e.tensor_scalar(eqm, lg, m1, None, ALU.is_equal))
                    ve(lambda e, msk=msk, eqm=eqm, lg=lg:
                       e.scalar_tensor_tensor(msk, eqm, -1e30, lg, ALU.mult, ALU.add))
                    vm2 = ve(lambda e, m2=m2, msk=msk: e.reduce_max(m2, msk, AX))
                    ve(lambda e, eq2=eq2, lg=lg, m2=m2:
                       e.tensor_scalar(eq2, lg, m2, None, ALU.is_ge))
                    r_v7[rpar] = ve(lambda e, d12=d12, m1=m1, m2=m2:
                                    e.tensor_sub(d12, m1, m2))

                for col in range(NTCR + 5):
                    if col < NTCR:
                        emitLoad(col)
                        emitA(col)
                    if 1 <= col <= NTCR:
                        emitB(col - 1)
                    if 3 <= col < NTCR + 3:
                        emitC(col - 3)
                        xn_ready[col - 3] = cnt["s_sc"]
                    if not ntc_run:
                        t5r = (col - 7) // 4
                        if 0 <= t5r <= 3 and (col - 7) % 4 == 0:
                            usp_xn[t5r] = cnt["s_sc"]
                            emit_usp(t5r)
                        if col == NTCR + 3:
                            usp_xn[4] = cnt["s_sc"]
                            emit_usp(4)
                        if 6 <= col < NMC + 6:
                            emitR2(col - 6)
                        if 5 <= col < NMC + 5:
                            emitR1(col - 5)
                ve_xn_all = cnt["s_ve"]

                if PH < 0: raise _Done()
                wmaxb = rsg[:, 1, :]
                nwb = rsg[:, 2, :]
                difb = rsg[:, 3, :]
                s_sig = sc(lambda e, wmaxb=wmaxb:
                           e.activation(wmaxb, rsg[:, 0, :], ACT.Sigmoid),
                           wait=[("s_ve", cnt["s_ve"])])
                ve(lambda e, nwb=nwb, wmaxb=wmaxb:
                   e.tensor_scalar(nwb, wmaxb, -1.0, 1.0, ALU.mult, ALU.add),
                   wait=[("s_sc", s_sig)])
                v_df = ve(lambda e, difb=difb, wmaxb=wmaxb:
                          e.tensor_scalar(difb, wmaxb, 2.0, -1.0, ALU.mult, ALU.add))
                wt_done = {}
                bc0_ev = {}
                sc_tail0 = cnt["s_sc"]
                for mc in range(NMC):
                    rpar = mc % 2
                    wt = wtm[:, rpar]
                    tmp8 = rt[:, rpar, 5:13]
                    vwl = [("s_ve", v_df)] if mc < 2 else [("s_te", r_t6[rpar])]
                    ve(lambda e, wt=wt, mc=mc:
                       e.tensor_scalar(wt, eq2b[:, mc, :], rsg[:, 2, mc:mc + 1],
                                       None, ALU.mult), wait=vwl)
                    ve(lambda e, tmp8=tmp8, mc=mc:
                       e.tensor_scalar(tmp8, eqmb[:, mc, :], rsg[:, 3, mc:mc + 1],
                                       None, ALU.mult))
                    wt_done[rpar] = ve(lambda e, wt=wt, tmp8=tmp8:
                                       e.tensor_add(wt, wt, tmp8))
                    tbank = pb[5 - rpar]
                    t6w = [("s_ve", wt_done[rpar])]
                    if mc < 2:
                        t6w.append(("s_sc", sc_tail0))
                    else:
                        t6w.append(("s_sc", router_evac[rpar]))
                    t6 = te(lambda e, tbank=tbank, wt=wt:
                            e.transpose(tbank[0:8, 0:128], wt, sI32[:]), wait=t6w)
                    r_t6[rpar] = t6
                    router_evac[rpar] = sc(
                        lambda e, mc=mc, tbank=tbank:
                        e.copy(w_fmb[0:8, mc * 128:(mc + 1) * 128],
                               tbank[0:8, 0:128]),
                        wait=[("s_te", t6)])
                    if mc % 4 == 3:
                        stt = mc // 4
                        bk = pb[3]
                        bwl = [("s_sc", cnt["s_sc"])]
                        if stt == 0:
                            bwl.extend(lc_wait)
                        else:
                            bwl.append(("s_sc", bc0_ev[stt - 1]))
                        m_ = te(lambda e, bk=bk, stt=stt:
                                e.matmul(bk[:, 0:STW], ssel[:, 0:128],
                                         w_fmb[:, stt * STW:(stt + 1) * STW],
                                         start=True, stop=True),
                                wait=bwl)
                        bc0_ev[stt] = sc(
                            lambda e, bk=bk, stt=stt:
                            e.copy(wbc[:, 0, stt * STW:(stt + 1) * STW], bk[:, 0:STW]),
                            wait=[("s_te", m_)])
                for q in range(NST):
                    if q not in st_ready_ve:
                        st_ready_ve[q] = cnt["s_sc"]
                ready_st[0] = NST - 1
                bc0_done = cnt["s_sc"]
                ve_d_all = cnt["s_ve"]

                if PH < 2: raise _Done()
                ch_act = {}
                for s in range(NSTEP):
                    pump(2)
                    b = s % 2
                    if b == 0:
                        for k in range(2):
                            sk = s + k
                            if sk >= NSTEP:
                                break
                            wl = []
                            if k == 0:
                                wl = [("s_ve", ve_d_all),
                                      ("s_sc", ch_act[s - 1] if s > 0 else cnt["s_sc"])]
                            te(lambda e, sk=sk, bk=pb[4 + sk % 2]:
                               e.matmul(bk[:, 0:NCH], sI16[:],
                                        Dt[:, WOFF + sk:WOFF + sk + SSTR:LCH],
                                        start=True, stop=False, skip_group_check=True),
                               wait=wl, inc=False)
                    if s == 0:
                        rhs = gz[:]
                    elif s <= WCH:
                        rhs = gw[:, ((s - 1) % 2) * NCH:((s - 1) % 2) * NCH + NCH]
                    else:
                        t0 = s - 1 - WCH
                        rhs = G[:, t0:t0 + SSTR:LCH]
                    wl = [("s_sc", ch_act[s - 1])] if s > 0 else []
                    mv = te(lambda e, b=b, rhs=rhs:
                            e.matmul(pb[4 + b][:, 0:NCH], sAT[:], rhs,
                                     start=False, stop=True, skip_group_check=True),
                            wait=wl)
                    if s < WCH:
                        dst = gw[:, (s % 2) * NCH:(s % 2) * NCH + NCH]
                    else:
                        t0 = s - WCH
                        dst = G[:, t0:t0 + SSTR:LCH]
                    ch_act[s] = sc(lambda e, b=b, dst=dst:
                                   e.activation(dst, pb[4 + b][:, 0:NCH], ACT.Tanh),
                                   wait=[("s_te", mv)])
                pump_all()
                sc_chain = cnt["s_sc"]

                if PH < 3: raise _Done()
                vstate = {}
                for stt in range(NST):
                    c0 = stt * STW
                    vstate[stt] = ve(lambda e, c0=c0:
                                     e.tensor_add(u_bf[:, c0:c0 + STW], G[:, c0:c0 + STW],
                                                  SPb[:, 1 + W + c0:1 + W + c0 + STW]),
                                     wait=([("s_sc", sc_chain)] if stt == 0 else ()))
                ys_evac = {}
                for stt in range(NST):
                    bk = pb[4 + stt % 2]
                    ywl = [("s_ve", vstate[stt])]
                    ywl.append(("s_sc", sc_chain if stt < 2 else ys_evac[stt - 2]))
                    m_ = te(lambda e, bk=bk, stt=stt:
                            e.matmul(bk[:, 0:STW], sCT[:], u_bf[:, stt * STW:(stt + 1) * STW],
                                     start=True, stop=True),
                            wait=ywl)
                    ys_evac[stt] = sc(lambda e, bk=bk, stt=stt:
                                      e.copy(ys_bf[:, stt * STW:(stt + 1) * STW], bk[:, 0:STW]),
                                      wait=[("s_te", m_)])
                sc_ys = cnt["s_sc"]

                attn_ev = {}
                for hc in range(NDC):
                    for stt in range(NST):
                        idx = hc * NST + stt
                        bk = pb[4 + idx % 2]
                        wl = [("s_sc", sc_ys if idx < 2 else attn_ev[idx - 2])]
                        te(lambda e, bk=bk, hc=hc, stt=stt:
                           e.matmul(bk[:, 0:STW], sWoT[:, hc, :],
                                    ys_bf[:, stt * STW:(stt + 1) * STW],
                                    start=True, stop=False, skip_group_check=True),
                           wait=wl, inc=False)
                        m_ = te(lambda e, bk=bk, hc=hc, stt=stt:
                                e.matmul(bk[:, 0:STW], sdb[:, hc * 128:(hc + 1) * 128],
                                         w_fmb[:, stt * STW:(stt + 1) * STW],
                                         start=False, stop=True, skip_group_check=True))
                        attn_ev[idx] = sc(
                            lambda e, bk=bk, hc=hc, stt=stt:
                            e.activation(acc[:, hc, stt * STW:(stt + 1) * STW],
                                         bk[:, 0:STW], ACT.Identity, bias=sbo[:, hc:hc + 1]),
                            wait=[("s_te", m_)])
                sc_attn = cnt["s_sc"]
                for stt in range(NST):
                    for fc in range(NFC):
                        fwl = [("s_sc", bc0_done)] if (stt == 0 and fc == 0) else []
                        ew_done[(0, fc, stt)] = ve(
                            lambda e, fc=fc, stt=stt:
                            e.tensor_mul(act_w[:, fc, stt * STW:(stt + 1) * STW],
                                         act_w[:, fc, stt * STW:(stt + 1) * STW],
                                         wbc[:, 0, stt * STW:(stt + 1) * STW]),
                            wait=fwl)
                wbc_cons0 = cnt["s_ve"]

                fin_add = {}
                out_done = {}

                fxload = {}

                def emit_final_load(mc):
                    par4 = mc % 4
                    fxsem = "s_lf%d" % (mc % 2)
                    xwl = [("s_ve", fin_add[par4])] if mc >= 4 else []
                    fxload[mc] = (fxsem,
                                  dma("gp", fxsem, xres[:, par4],
                                      d_xm[mc * 128:(mc + 1) * 128, :], wait=xwl))

                def emit_final_group(stt):
                    if stt == 0:
                        for mc_ in range(4):
                            emit_final_load(mc_)
                    for mc in range(stt * 4, stt * 4 + 4):
                        par = mc % 2
                        par4 = mc % 4
                        fosem = "s_ou%d" % par
                        fxsem, lxv = fxload[mc]
                        bank = pbt[par]
                        tvals = []
                        for hc in range(NDC):
                            wl2 = []
                            if hc == 0:
                                wl2 = [("s_ve", acc_add[(NDC - 1, stt)])]
                                wl2.append(("s_ve", fin_add.get(par, cnt["s_ve"])
                                            if mc >= 2 else cnt["s_ve"]))
                            t_ = te(lambda e, bank=bank, hc=hc, mc=mc:
                                    e.transpose(bank[:, hc * 128:(hc + 1) * 128],
                                                acc[:, hc, mc * 128:(mc + 1) * 128], sI16[:]),
                                    wait=wl2, inc=(hc in (3, 7)))
                            tvals.append(t_)
                        vwl = [("s_te", tvals[7]), (fxsem, lxv)]
                        if mc >= 2:
                            vwl.append((fosem, out_done[par]))
                        ve(lambda e, bank=bank, par=par, par4=par4:
                           e.tensor_add(otm[:, par, :], bank[:], xres[:, par4, :]),
                           wait=vwl)
                        fin_add[par] = cnt["s_ve"]
                        fin_add[par4] = cnt["s_ve"]
                        if mc + 4 < NMC:
                            emit_final_load(mc + 4)
                        out_done[par] = dma("sy", fosem, d_y[mc * 128:(mc + 1) * 128, :],
                                            otm[:, par], wait=[("s_ve", fin_add[par])])


                if PH < 4: raise _Done()
                acc_add = {}
                wbc_cons = {0: wbc_cons0}
                dn_read = {}
                for ei in range(NEXP):
                    epar = ei % 2
                    if ei == 0:
                        bc_ev_l = bc0_ev
                    else:
                     bc_ev_l = {}
                     for stt in range(NST):
                        bk = pb[4 + stt % 2]
                        bwl = []
                        if stt < 2:
                            if ei == 0:
                                bwl.append(("s_sc", sc_attn))
                            else:
                                bwl.append(("s_ve", dn_read[stt % 2]))
                            if ei >= 2:
                                bwl.append(("s_ve", wbc_cons[epar]))
                        else:
                            bwl.append(("s_sc", bc_ev_l[stt - 2]))
                        m_ = te(lambda e, bk=bk, ei=ei, stt=stt:
                                e.matmul(bk[:, 0:STW], ssel[:, ei * 128:(ei + 1) * 128],
                                         w_fmb[:, stt * STW:(stt + 1) * STW],
                                         start=True, stop=True),
                                wait=bwl)
                        bc_ev_l[stt] = sc(
                            lambda e, bk=bk, epar=epar, stt=stt:
                            e.copy(wbc[:, epar, stt * STW:(stt + 1) * STW], bk[:, 0:STW]),
                            wait=[("s_te", m_)])

                    if ei > 0:
                        for fc in range(NFC):
                            for stt in range(NST):
                                emit_gu_tile(ei, fc, stt, weighted=True,
                                             bc_ev_l=bc_ev_l)
                        wbc_cons[epar] = cnt["s_ve"]

                    for fc in range(NFC):
                        par = (ei * NFC + fc) % 2
                        ldsem, lwd = wload[("d", ei, fc)]
                        for stt in range(NST):
                            idx = ei * NFC * NST + fc * NST + stt
                            dp = (fc * NST + stt) % 2
                            bk = pb[4 + dp]
                            wl = [(ldsem, lwd), ("s_ve", ew_done[(ei, NFC - 1, stt)])]
                            if ei == 0 and fc == 0 and stt < 2:
                                wl.append(("s_sc", sc_attn))
                            elif fc == 0 and stt < 2:
                                wl.append(("s_sc", bc_ev_l[stt + 2]))
                            elif idx >= 2:
                                wl.append(("s_ve", dn_read[dp]))
                            md = None
                            for ic in range(NDC):
                                md = te(lambda e, ic=ic, bk=bk, par=par, stt=stt:
                                        e.matmul(bk[:, 0:STW], wdb_[:, par, ic, :],
                                                 act_w[:, ic, stt * STW:(stt + 1) * STW],
                                                 start=(ic == 0), stop=(ic == NDC - 1),
                                                 skip_group_check=True),
                                        wait=(wl if ic == 0 else ()),
                                        inc=(ic == NDC - 1))
                            if stt == NST - 1:
                                wd_cons[par] = cnt["s_te"]
                                nxt = ei * NFC + fc + 2
                                if nxt < NEXP * NFC:
                                    ne, nf = divmod(nxt, NFC)
                                    sched_wdma("d", ne, nf)
                            awl = [("s_te", md)]
                            if ei == 0:
                                awl.append(("s_sc", attn_ev[fc * NST + stt]))
                            va = ve(lambda e, fc=fc, stt=stt, bk=bk:
                                    e.tensor_add(acc[:, fc, stt * STW:(stt + 1) * STW],
                                                 acc[:, fc, stt * STW:(stt + 1) * STW],
                                                 bk[:, 0:STW]),
                                    wait=awl)
                            dn_read[dp] = va
                            acc_add[(fc, stt)] = va
                            if PH >= 5 and ei == NEXP - 1 and fc == NFC - 1:
                                emit_final_group(stt)

            except _Done:
                pass

            if debug:
                vw_, sw_, tw_ = cnt["s_ve"], cnt["s_sc"], cnt["s_te"]
                def gdbg(name, src_ap):
                    dma("gp", "s_ou0", dbg[name], src_ap,
                        wait=[("s_ve", vw_), ("s_sc", sw_), ("s_te", tw_)])
                gdbg("xnfm", xn_fm[:].rearrange("p a b c -> p (a b c)"))
                gdbg("wfm", w_fmb[:])
                if PH >= 1:
                    gdbg("D", Dt[:]); gdbg("u", u_bf[:]); gdbg("sp", SPb[:, 1:])
                if PH >= 2:
                    gdbg("G", G[:])
                if PH >= 3:
                    gdbg("ys", ys_bf[:])
                    gdbg("acc", acc[:].rearrange("p a b -> p (a b)"))

            fin_o0 = cnt.get("s_ou0", 0)
            fin_o1 = cnt.get("s_ou1", 0)
            def fin(e):
                if fin_o0:
                    e.wait_ge(s_ou0, fin_o0)
                if fin_o1:
                    e.wait_ge(s_ou1, fin_o1)
            ops["sy"].append(fin)

            @block.gpsimd
            def _(g):
                for f in ops["gp"]:
                    f(g)

            @block.sync
            def _(s_):
                for f in ops["sy"]:
                    f(s_)

            @block.tensor
            def _(t_):
                for f in ops["te"]:
                    f(t_)

            @block.vector
            def _(v_):
                for f in ops["ve"]:
                    f(v_)

            @block.scalar
            def _(sc_):
                for f in ops["sc"]:
                    f(sc_)
    return nc


def prep_inputs(L0_attn_x, L0_attn_Wi, L0_attn_bi, L0_attn_Ws, L0_attn_bs,
                L0_attn_Wo, L0_attn_bo, L0_attn_A, L0_attn_B, L0_attn_C,
                L0_ffn_router_weight, L0_ffn_gate_up_proj, L0_ffn_gate_up_proj_bias,
                L0_ffn_down_proj, L0_ffn_down_proj_bias):
    f = lambda a: np.ascontiguousarray(np.asarray(a), dtype=np.float32)
    x = f(L0_attn_x).reshape(16384, DM)
    Wi, Ws, Wo = f(L0_attn_Wi), f(L0_attn_Ws), f(L0_attn_Wo)
    A, B, C = f(L0_attn_A), f(L0_attn_B), f(L0_attn_C)
    bi, bs, bo = f(L0_attn_bi), f(L0_attn_bs), f(L0_attn_bo)
    Wr = f(L0_ffn_router_weight)
    gup, gub = f(L0_ffn_gate_up_proj), f(L0_ffn_gate_up_proj_bias)
    dwn, dwb = f(L0_ffn_down_proj), f(L0_ffn_down_proj_bias)

    com = {}
    com["eye"] = np.eye(128, dtype=np.float32)
    com["AT"] = np.ascontiguousarray(A.T)
    com["BT"] = np.ascontiguousarray(B.T)
    com["CT"] = np.ascontiguousarray(C.T)
    com["WiT"] = np.ascontiguousarray(Wi.T.reshape(NDC, 128, 128).transpose(1, 0, 2))
    com["WsT"] = np.ascontiguousarray(Ws.T.reshape(NDC, 128, 128).transpose(1, 0, 2))
    com["WoT"] = np.ascontiguousarray(Wo.T.reshape(128, NDC, 128))
    com["Wr"] = np.ascontiguousarray(Wr.reshape(NDC, 128, 8).transpose(1, 0, 2))
    com["bi"] = bi.reshape(128, 1)
    com["bs"] = bs.reshape(128, 1)
    com["bo"] = np.ascontiguousarray(bo.reshape(NDC, 128).T)
    gate = np.ascontiguousarray(gup[:, :, 0::2])
    up = np.ascontiguousarray(gup[:, :, 1::2])
    com["Wg"] = np.ascontiguousarray(
        gate.reshape(NEXP, NDC, 128, NFC, 128).transpose(0, 3, 2, 1, 4))
    com["Wu"] = np.ascontiguousarray(
        up.reshape(NEXP, NDC, 128, NFC, 128).transpose(0, 3, 2, 1, 4))
    com["Wd"] = np.ascontiguousarray(
        dwn.reshape(NEXP, NDC, 128, NFC, 128).transpose(0, 3, 2, 1, 4))
    gb = gub[:, 0::2].reshape(NEXP, NFC, 128)
    ub = gub[:, 1::2].reshape(NEXP, NFC, 128)
    com["bg"] = np.ascontiguousarray(gb.reshape(NEXP * NFC, 128).T)
    com["bu"] = np.ascontiguousarray(ub.reshape(NEXP * NFC, 128).T)
    com["dbr"] = np.ascontiguousarray(dwb)
    com["on1"] = np.ones((1, 128), np.float32)
    sel = np.zeros((NEXP, NEXP * 128), np.float32)
    for e_ in range(NEXP):
        sel[e_, e_ * 128:(e_ + 1) * 128] = 1.0
    com["sel"] = sel

    in_maps = []
    for c in range(8):
        m = dict(com)
        r0 = c * NTOK
        m["xm"] = np.ascontiguousarray(x[r0:r0 + NTOK])
        if c % 4 == 0:
            m["xw"] = np.zeros((W, DM), np.float32)
            m["dmask"] = np.zeros((128, W), np.float32)
        else:
            m["xw"] = np.ascontiguousarray(x[r0 - W:r0])
            m["dmask"] = np.ones((128, W), np.float32)
        in_maps.append(m)
    return in_maps


_cache = {}


def kernel(**inputs):
    in_maps = prep_inputs(**inputs)
    if "nc" not in _cache:
        _cache["nc"] = build_nc(debug=False)
    res = run_bass_kernel_spmd(_cache["nc"], in_maps, core_ids=list(range(8)))
    out = np.concatenate([np.asarray(res.results[c]["y"]) for c in range(8)], axis=0)
    return out.reshape(2, 8192, DM)
